# version 16 (final)
# speedup vs baseline: 1.0925x; 1.0121x over previous
import numpy as np
import ml_dtypes
import concourse.bass as bass
import concourse.mybir as mybir
from concourse.bass_utils import run_bass_kernel_spmd
from contextlib import ExitStack

F32 = mybir.dt.float32
BF16 = mybir.dt.bfloat16
U32 = mybir.dt.uint32
AF = mybir.ActivationFunctionType
ALU = mybir.AluOpType
AX = mybir.AxisListType

T = 4096
D = 1024
NH = 16
HD = 64
NCORES = 8

import inspect
from collections import defaultdict

SEQS = ("tensor", "vector", "scalar", "gpsimd", "sync")
ENG_KIND = {"tensor": "te", "vector": "ve", "scalar": "se", "gpsimd": "ge", "sync": "sy"}
DMA_KIND = {"gpsimd": "gd", "sync": "sd", "scalar": "cd", "vector": "vd"}
DMA_KINDS = set(DMA_KIND.values())
ZQ = 8

_WRITE_PARAMS = {"out", "ap", "accum_out"}
_CURRENT = [None]


class _Dummy:
    def then_inc(self, *a, **k):
        return self


def _ap_interval(ap):
    lo = ap.offset
    span = 0
    for s, n in ap.ap:
        if n > 1:
            span += (n - 1) * s
    return (lo, lo + span + 1)


class _Rec:
    def __init__(self, chain, ename):
        self._ch = chain
        self._en = ename
        self.reads = []
        self.writes = []

    def _collect(self, val, into):
        if isinstance(val, bass.AP):
            into.append(val)
        elif hasattr(val, "ap") and isinstance(getattr(val, "ap", None), bass.AP):
            self.reads.append(val.ap)
        elif isinstance(val, (list, tuple)):
            for v in val:
                if isinstance(v, bass.AP):
                    into.append(v)

    def __getattr__(self, m):
        if m.startswith("_"):
            raise AttributeError(m)
        ch = self._ch
        sig = ch._sig(self._en, m)
        rec = self

        def call(*a, **kw):
            ba = sig.bind(*a, **kw)
            args = ba.arguments
            for pname, val in args.items():
                if pname in _WRITE_PARAMS:
                    rec._collect(val, rec.writes)
                else:
                    rec._collect(val, rec.reads)
            if m == "matmul" and args.get("start") is False:
                rec._collect(args.get("out"), rec.reads)
            if m == "indirect_dma_start" and args.get("compute_op") is not None \
                    and args.get("compute_op") is not Ellipsis:
                rec._collect(args.get("out"), rec.reads)
            if m == "dma_start" and args.get("accum_op") is not None \
                    and args.get("accum_op") is not Ellipsis:
                rec._collect(args.get("out"), rec.reads)
            return _Dummy()

        return call


class Chain:
    def __init__(self, nc):
        self.nc = nc
        self.seq_ops = {e: [] for e in SEQS}
        self.kcount = defaultdict(int)
        self.state = {}
        self.phys = {}
        self.initdeps = {}
        self.freed = []
        self._sigs = {}
        _CURRENT[0] = self

    def _sig(self, ename, m):
        k = (ename, m)
        s = self._sigs.get(k)
        if s is None:
            s = inspect.signature(getattr(getattr(self.nc, ename), m))
            self._sigs[k] = s
        return s

    def note_alloc(self, handle):
        try:
            m = self.nc.lookup_mloc(handle)
        except Exception:
            return
        colb = int(m.dims[1]) if len(m.dims) > 1 else int(m.dims[0])
        if m.type == "PSUM":
            lo = (1 << 40) + int(m.bank) * 2048 + int(m.addr)
        else:
            lo = int(m.addr)
        hi = lo + colb
        key = m.type + ":" + m.name
        self.phys[key] = (lo, hi)
        init = {}
        for plo, phi, deps in self.freed:
            if plo < hi and lo < phi:
                for kind, pos in deps.items():
                    if init.get(kind, -1) < pos:
                        init[kind] = pos
        if init:
            self.initdeps[key] = init

    def note_free(self, handle):
        try:
            m = self.nc.lookup_mloc(handle)
        except Exception:
            return
        key = m.type + ":" + m.name
        rng = self.phys.pop(key, None)
        if rng is None:
            return
        st = self.state.pop(key, None)
        init = self.initdeps.pop(key, None)
        deps = {}
        if init:
            deps.update(init)
        if st:
            for (kind, pos, lo, hi) in st["w"]:
                if deps.get(kind, -1) < pos:
                    deps[kind] = pos
            for kind, ent in st["r"].items():
                p = ent[0][-1] if isinstance(ent[0], list) else ent[0]
                if deps.get(kind, -1) < p:
                    deps[kind] = p
        self.freed.append((rng[0], rng[1], deps))

    @staticmethod
    def _key(ap):
        sp = str(ap.space)
        if "SBUF" in sp:
            return "SB:" + ap.name
        if "PSUM" in sp:
            return "PSUM:" + ap.name
        return "D:" + ap.name

    def _deps_for(self, reads, writes, mykind):
        need = {}

        def add(kind, pos):
            if kind == "te" and mykind == "te":
                return
            v = pos + 1
            if need.get(kind, 0) < v:
                need[kind] = v

        def add_read_entry(kind, ent):
            if isinstance(ent[0], list):
                for p in ent[0]:
                    add(kind, p)
            else:
                add(kind, ent[0])

        for (key, lo, hi) in reads:
            ini = self.initdeps.get(key)
            if ini:
                for k2, p2 in ini.items():
                    add(k2, p2)
            st = self.state.get(key)
            if st:
                for (k2, p2, l2, h2) in st["w"]:
                    if l2 < hi and lo < h2:
                        add(k2, p2)
        for (key, lo, hi) in writes:
            ini = self.initdeps.get(key)
            if ini:
                for k2, p2 in ini.items():
                    add(k2, p2)
            st = self.state.get(key)
            if st:
                for (k2, p2, l2, h2) in st["w"]:
                    if l2 < hi and lo < h2:
                        add(k2, p2)
                for k2, ent in st["r"].items():
                    if ent[1] < hi and lo < ent[2]:
                        add_read_entry(k2, ent)
        return need

    def _update_state(self, reads, writes, mykind, mypos):
        for (key, lo, hi) in reads:
            st = self.state.setdefault(key, {"w": [], "r": {}})
            ent = st["r"].get(mykind)
            if mykind in DMA_KINDS:
                if ent is None:
                    st["r"][mykind] = [[mypos], lo, hi]
                else:
                    ent[0].append(mypos)
                    ent[0] = [p for p in ent[0] if p > mypos - ZQ]
                    ent[1] = min(ent[1], lo)
                    ent[2] = max(ent[2], hi)
            else:
                if ent is None:
                    st["r"][mykind] = [mypos, lo, hi]
                else:
                    ent[0] = max(ent[0], mypos)
                    ent[1] = min(ent[1], lo)
                    ent[2] = max(ent[2], hi)
        for (key, lo, hi) in writes:
            st = self.state.setdefault(key, {"w": [], "r": {}})
            st["w"] = [w for w in st["w"] if not (lo <= w[2] and w[3] <= hi)]
            st["r"] = {k: v for k, v in st["r"].items()
                       if not (lo <= v[1] and v[2] <= hi)}
            st["w"].append((mykind, mypos, lo, hi))

    def _record(self, seq, fn, kind, r, w):
        deps = self._deps_for(r, w, kind)
        pos = self.kcount[kind]
        self.kcount[kind] = pos + 1
        self._update_state(r, w, kind, pos)
        self.seq_ops[seq].append((fn, kind, pos, deps))

    def _norm(self, aps):
        out = []
        for ap in aps:
            lo, hi = _ap_interval(ap)
            out.append((self._key(ap), lo, hi))
        return out

    def op(self, engine, fn, r=None, w=None):
        kind = ENG_KIND[engine]
        if r is None and w is None:
            rec = _Rec(self, engine)
            fn(rec)
            r, w = rec.reads, rec.writes
        else:
            r = r or []
            w = w or []
        self._record(engine, fn, kind, self._norm(r), self._norm(w))

    def dma(self, engine, out, in_, **kw):
        kind = DMA_KIND[engine]
        self._record(engine, lambda e: e.dma_start(out=out, in_=in_, **kw),
                     kind, self._norm([in_]), self._norm([out]))

    def dma_fn(self, engine, fn):
        kind = DMA_KIND[engine]
        rec = _Rec(self, engine)
        fn(rec)
        self._record(engine, fn, kind, self._norm(rec.reads), self._norm(rec.writes))

    def dma_transpose(self, out, in_):
        self._record("sync", lambda e: e.dma_start_transpose(out, in_),
                     "sd", self._norm([in_]), self._norm([out]))

    def group(self):
        class _G:
            def __enter__(self):
                return self

            def __exit__(self, *a):
                return False

        return _G()

    def make_sems(self, st):
        self.esem = {}
        for kind in ("te", "ve", "se", "ge", "sy"):
            self.esem[kind] = st.enter_context(self.nc.semaphore("q_" + kind))
        self.dsem = {}
        for kind in ("gd", "sd", "cd", "vd"):
            self.dsem[kind] = [st.enter_context(self.nc.semaphore(f"q_{kind}{i}"))
                               for i in range(ZQ)]

    def _wait_target(self, kind, val):
        if kind in DMA_KINDS:
            pos = val - 1
            return (self.dsem[kind][pos % ZQ], 16 * (pos // ZQ + 1))
        return (self.esem[kind], val)

    def emit(self, block):
        for seq in SEQS:
            ops = self.seq_ops[seq]
            if not ops:
                continue

            def make_body(ops):
                def body(eng):
                    clock = {}
                    for fn, kind, pos, deps in ops:
                        for k2 in sorted(deps):
                            sem, val = self._wait_target(k2, deps[k2])
                            cid = id(sem)
                            if clock.get(cid, 0) < val:
                                eng.wait_ge(sem, val)
                                clock[cid] = val
                        if kind in DMA_KINDS:
                            sem = self.dsem[kind][pos % ZQ]
                            if pos >= ZQ:
                                gate = 16 * (pos // ZQ)
                                cid = id(sem)
                                if clock.get(cid, 0) < gate:
                                    eng.wait_ge(sem, gate)
                                    clock[cid] = gate
                            inst = fn(eng)
                            if inst is None:
                                inst = eng.nop()
                            inst.then_inc(sem, 16)
                        else:
                            inst = fn(eng)
                            if inst is None:
                                inst = eng.nop()
                            inst.then_inc(self.esem[kind], 1)
                return body

            getattr(block, seq)(make_body(ops))

        kc = dict(self.kcount)
        dtot = {k: defaultdict(int) for k in DMA_KINDS}
        for k in DMA_KINDS:
            for p in range(kc.get(k, 0)):
                dtot[k][p % ZQ] += 16

        def fin(eng):
            for kind in ("te", "ve", "se", "ge"):
                if kc.get(kind, 0) > 0:
                    eng.wait_ge(self.esem[kind], kc[kind])
            for k in DMA_KINDS:
                for r, v in dtot[k].items():
                    eng.wait_ge(self.dsem[k][r], v)

        block.sync(fin)


def alloc_sb(nc, st, name, shape, dt=F32):
    h = st.enter_context(nc.sbuf_tensor(name, shape, dt))
    ch = _CURRENT[0]
    if ch is not None:
        ch.note_alloc(h)
        st.callback(lambda: ch.note_free(h))
    return h


def alloc_ps(nc, st, name, shape, dt=F32):
    h = st.enter_context(nc.psum_tensor(name, shape, dt))
    ch = _CURRENT[0]
    if ch is not None:
        ch.note_alloc(h)
        st.callback(lambda: ch.note_free(h))
    return h


def emit_ln_tok(nc, ch, st, pre, x_src_dram, ln_dst_dram, res_add_dram=None,
                x_out_dram=None, n_tiles=None):
    NB = 3
    xt = [alloc_sb(nc, st, pre + f"xt{b}", [128, D]) for b in range(NB)]
    bnb = [alloc_sb(nc, st, pre + f"bnb{b}", [128, 12]) for b in range(NB)]
    mv = [alloc_sb(nc, st, pre + f"mv{b}", [128, 2]) for b in range(NB)]
    t1 = [alloc_sb(nc, st, pre + f"t1{b}", [128, 1]) for b in range(NB)]
    rr = [alloc_sb(nc, st, pre + f"rr{b}", [128, 1]) for b in range(NB)]
    mb = [alloc_sb(nc, st, pre + f"mb{b}", [128, 1]) for b in range(NB)]
    lnb = [alloc_sb(nc, st, pre + f"lnb{b}", [128, D], BF16) for b in range(NB)]
    epsb = alloc_sb(nc, st, pre + "epsb", [128, 1])
    if n_tiles is None:
        n_tiles = T // 128
    ch.op("vector", lambda v: v.memset(epsb[:], 1e-5))
    if res_add_dram is not None:
        resb = [alloc_sb(nc, st, pre + f"resb{b}", [128, D]) for b in range(NB)]
        outb = [alloc_sb(nc, st, pre + f"outb{b}", [128, D]) for b in range(NB)]
    def _tile(i):
        r0 = i * 128
        b = i % NB
        xtb, bnbb, mvb, t1b, rrb, mbb = xt[b], bnb[b], mv[b], t1[b], rr[b], mb[b]
        ch.dma("scalar", xtb[:], x_src_dram[r0:r0 + 128, :])
        ch.op("vector", lambda v: v.bn_stats(out=bnbb[:, 0:6], in_=xtb[:, 0:512]))
        ch.op("vector", lambda v: v.bn_stats(out=bnbb[:, 6:12], in_=xtb[:, 512:1024]))
        ch.op("vector", lambda v: v.bn_aggr(out=mvb[:], in_=bnbb[:]))
        ch.op("scalar", lambda s: s.activation(out=t1b[:], in_=mvb[:, 1:2], func=AF.Sqrt, bias=epsb[:, 0:1]))
        ch.op("vector", lambda v: v.reciprocal(out=rrb[:], in_=t1b[:]))
        ch.op("vector", lambda v: v.tensor_scalar(
            out=mbb[:], in0=mvb[:, 0:1], scalar1=rrb[:, 0:1], scalar2=-1.0,
            op0=ALU.mult, op1=ALU.mult))
        if ln_dst_dram is not None:
            ch.op("vector", lambda v: v.tensor_scalar(
                out=lnb[b][:], in0=xtb[:], scalar1=rrb[:, 0:1], scalar2=mbb[:, 0:1],
                op0=ALU.mult, op1=ALU.add))
            ch.dma("gpsimd", ln_dst_dram[r0:r0 + 128, :], lnb[b][:])
        if res_add_dram is not None:
            ch.dma("scalar", resb[b][:], res_add_dram[r0:r0 + 128, :])
            ch.op("vector", lambda v: v.tensor_scalar(
                out=outb[b][:], in0=xtb[:], scalar1=rrb[:, 0:1], scalar2=mbb[:, 0:1],
                op0=ALU.mult, op1=ALU.add))
            ch.op("vector", lambda v: v.tensor_tensor(
                out=outb[b][:], in0=outb[b][:], in1=resb[b][:], op=ALU.add))
            ch.dma("gpsimd", x_out_dram[r0:r0 + 128, :], outb[b][:])

    for i in range(n_tiles):
        _tile(i)


def emit_transpose_to_feat(nc, ch, tok_dram_bf16, xT_sb, rows=None):
    if rows is None:
        rows = T
    for c in range(len(xT_sb)):
        ch.dma_transpose(xT_sb[c][:, 0:rows], tok_dram_bf16[0:rows, c * 128:(c + 1) * 128])


def emit_l0_attn(nc, ch, st, x_dram, w_attn_dram, w_proj_dram, x1_dram, scratch):
    ln_dram = scratch["ln"]
    qkT_dram = scratch["qkT"]
    v_dram = scratch["v"]
    oT_dram = scratch["oT"]
    y_dram = scratch["y"]
    nb_dram = scratch["nb"]

    with ExitStack() as st1:
        emit_ln_tok(nc, ch, st1, "l0a_ln1", x_dram, ln_dram)

    with ExitStack() as st2:
        xT = [alloc_sb(nc, st2, f"l0a_xT{c}", [128, T], BF16) for c in range(8)]
        emit_transpose_to_feat(nc, ch, ln_dram, xT)

        wat = [alloc_sb(nc, st2, f"l0a_w{c}", [128, 3072], BF16) for c in range(8)]
        for c in range(8):
            ch.dma("gpsimd", wat[c][:], w_attn_dram[c * 128:(c + 1) * 128, :])

        ps = [alloc_ps(nc, st2, f"l0a_ps{b}", [128, 512]) for b in range(2)]
        evb = [alloc_sb(nc, st2, f"l0a_evb{b}", [128, 512], BF16) for b in range(2)]
        sqb = [alloc_sb(nc, st2, f"l0a_sqb{b}", [128, 512], BF16) for b in range(2)]
        onesAB = alloc_sb(nc, st2, "l0a_ones", [128, 2], BF16)
        psn = [alloc_ps(nc, st2, f"l0a_psn{b}", [2, 512]) for b in range(2)]
        nrow = [alloc_sb(nc, st2, f"l0a_nrow{b}", [2, 1]) for b in range(2)]
        mxq = alloc_sb(nc, st2, "l0a_mxq", [2, 8])
        mxk = alloc_sb(nc, st2, "l0a_mxk", [2, 8])
        ch.op("vector", lambda v: v.memset(onesAB[:, :], 0.0))
        ch.op("vector", lambda v: v.memset(onesAB[0:64, 0:1], 1.0))
        ch.op("vector", lambda v: v.memset(onesAB[64:128, 1:2], 1.0))
        ch.op("vector", lambda v: v.memset(mxq[:], 0.0))
        ch.op("vector", lambda v: v.memset(mxk[:], 0.0))

        def _qk_tile(it, qk, col0, mx, j, t):
            scale = 0.125 if qk == "q" else 1.0
            psb_, evbb, sqbb, psnb, nrowb = ps[it % 2], evb[it % 2], sqb[it % 2], \
                psn[it % 2], nrow[it % 2]
            for c in range(8):
                ch.op("tensor", lambda tt, c=c: tt.matmul(
                    out=psb_[:], lhsT=wat[c][:, col0 + j * 128:col0 + (j + 1) * 128],
                    rhs=xT[c][:, t * 512:(t + 1) * 512],
                    start=(c == 0), stop=(c == 7)))
            ch.op("scalar", lambda s: s.activation(
                out=evbb[:], in_=psb_[:], func=AF.Copy, scale=scale))
            ch.dma("gpsimd", qkT_dram[col0 + j * 128:col0 + (j + 1) * 128,
                                      t * 512:(t + 1) * 512], evbb[:])
            ch.op("scalar", lambda s: s.activation(
                out=sqbb[:], in_=psb_[:], func=AF.Square, scale=scale))
            ch.op("tensor", lambda tt: tt.matmul(out=psnb[:], lhsT=onesAB[:], rhs=sqbb[:],
                                                 start=True, stop=True))
            ch.op("vector", lambda v: v.tensor_reduce(out=nrowb[:], in_=psnb[:],
                                                      axis=AX.X, op=ALU.max))
            ch.op("vector", lambda v: v.tensor_tensor(
                out=mx[:, j:j + 1], in0=mx[:, j:j + 1], in1=nrowb[:], op=ALU.max))

        it = 0
        for (qk, col0, mx) in (("k", D, mxk), ("q", 0, mxq)):
            for j in range(8):
                for t in range(T // 512):
                    _qk_tile(it, qk, col0, mx, j, t)
                    it += 1

        def _v_tile(it, g, ti):
            psb_, evbb = ps[it % 2], evb[it % 2]
            for c in range(8):
                ch.op("tensor", lambda tt, c=c: tt.matmul(
                    out=psb_[:], lhsT=xT[c][:, ti * 128:(ti + 1) * 128],
                    rhs=wat[c][:, 2 * D + g * 512: 2 * D + (g + 1) * 512],
                    start=(c == 0), stop=(c == 7)))
            ch.op("scalar", lambda s: s.copy(out=evbb[:], in_=psb_[:]))
            ch.dma("gpsimd", v_dram[ti, 8 * g:8 * (g + 1), :, :].rearrange("h p c -> p h c"),
                   evbb[:])

        for g in range(2):
            for ti in range(T // 128):
                _v_tile(it, g, ti)
                it += 1

        ch.dma("gpsimd", nb_dram[0], mxq[:])
        ch.dma("gpsimd", nb_dram[1], mxk[:])

    with ExitStack() as st3:
        nb16 = alloc_sb(nc, st3, "l0a_nb16", [1, 32])
        negc1 = alloc_sb(nc, st3, "l0a_negc1", [1, 16])
        ones_r = alloc_sb(nc, st3, "l0a_ones_r", [1, 128])
        biascol = [alloc_sb(nc, st3, f"l0a_biascol{b}", [128, 1]) for b in range(2)]
        psb = [alloc_ps(nc, st3, f"l0a_psb{b}", [128, 1]) for b in range(2)]
        qt = [alloc_sb(nc, st3, f"l0a_qt{b}", [64, T], BF16) for b in range(2)]
        kt = [alloc_sb(nc, st3, f"l0a_kt{b}", [64, T], BF16) for b in range(2)]
        vt = [alloc_sb(nc, st3, f"l0a_vt{b}", [128, (T // 128) * 65], BF16) for b in range(2)]
        rd = [alloc_sb(nc, st3, f"l0a_rd{b}", [1, 512]) for b in range(2)]
        ones64 = alloc_sb(nc, st3, "l0a_ones64", [1, 64])
        psbc = [alloc_ps(nc, st3, f"l0a_psbc{b}", [64, 512]) for b in range(2)]
        bcb = [alloc_sb(nc, st3, f"l0a_bcb{b}", [64, 512]) for b in range(2)]
        obn = [alloc_sb(nc, st3, f"l0a_obn{b}", [64, 512], BF16) for b in range(2)]
        et = [alloc_sb(nc, st3, f"l0a_et{i}", [128, 512], BF16) for i in range(T // 128)]
        masks = alloc_sb(nc, st3, "l0a_mask", [128, 4 * 512], BF16)
        colid = alloc_sb(nc, st3, "l0a_colid", [128, 512])
        rowid = alloc_sb(nc, st3, "l0a_rowid", [128, 1])
        pss = [alloc_ps(nc, st3, f"l0a_pss{b}", [128, 512]) for b in range(2)]
        pso = [alloc_ps(nc, st3, f"l0a_pso{b}", [65, 512]) for b in range(2)]
        ob = [alloc_sb(nc, st3, f"l0a_ob{b}", [65, 512]) for b in range(2)]

        ch.dma("gpsimd", nb16[:], nb_dram.rearrange("a b c -> (a b c)").rearrange("(x y) -> x y", x=1))
        ch.op("scalar", lambda s: s.activation(out=nb16[:], in_=nb16[:], func=AF.Sqrt))
        ch.op("vector", lambda v: v.tensor_tensor(
            out=negc1[:], in0=nb16[0:1, 0:16], in1=nb16[0:1, 16:32], op=ALU.mult))
        ch.op("vector", lambda v: v.tensor_scalar(
            out=negc1[:], in0=negc1[:], scalar1=-1.0, scalar2=-1.0,
            op0=ALU.mult, op1=ALU.add))
        ch.op("vector", lambda v: v.memset(ones_r[:], 1.0))
        ch.op("vector", lambda v: v.memset(ones64[:], 1.0))

        ch.op("gpsimd", lambda g: g.iota(colid[:], pattern=[[1, 512]], base=0,
                                         channel_multiplier=0,
                                         allow_small_or_imprecise_dtypes=True))
        ch.op("gpsimd", lambda g: g.iota(rowid[:], pattern=[[0, 1]], base=0,
                                         channel_multiplier=1,
                                         allow_small_or_imprecise_dtypes=True))
        for i in range(4):
            ch.op("vector", lambda v, i=i: v.tensor_scalar(
                out=masks[:, i * 512:(i + 1) * 512], in0=colid[:],
                scalar1=float(128 * i), scalar2=rowid[:, 0:1],
                op0=ALU.subtract, op1=ALU.is_ge))

        def _head(h):
            hb_ = h % 2
            hflat = (h % 2) * 8 + h // 2
            qtb, ktb, vtb, bcol = qt[hb_], kt[hb_], vt[hb_], biascol[hb_]
            ch.op("tensor", lambda tt: tt.matmul(
                out=psb[hb_][:], lhsT=ones_r[:], rhs=negc1[0:1, hflat:hflat + 1],
                start=True, stop=True))
            ch.op("vector", lambda v: v.tensor_copy(out=bcol[:], in_=psb[hb_][:]))

            ch.dma("scalar", qtb[:], qkT_dram[h * 64:(h + 1) * 64, :])
            ch.dma("scalar", ktb[:], qkT_dram[D + h * 64:D + (h + 1) * 64, :])
            ch.op("vector", lambda v: v.memset(vtb[:], 1.0))
            ch.dma("scalar", vtb[:].rearrange("p (k c) -> p k c", c=65)[:, :, 0:64],
                   v_dram[:, h, :, :].rearrange("k p c -> p k c"))
            def _jchunk(j):
                jb = j % 2
                nkt = 4 * (j + 1)
                for i in range(nkt):
                    psi = pss[i % 2]
                    ch.op("tensor", lambda tt, i=i, psi=psi: tt.matmul(
                        out=psi[:], lhsT=ktb[:, i * 128:(i + 1) * 128],
                        rhs=qtb[:, j * 512:(j + 1) * 512], start=True, stop=True))
                    ch.op("scalar", lambda s, i=i, psi=psi: s.activation(
                        out=et[i][:], in_=psi[:], func=AF.Exp, bias=bcol[:, 0:1]))
                    if i >= 4 * j:
                        mi = i - 4 * j
                        ch.op("vector", lambda v, i=i, mi=mi: v.tensor_tensor(
                            out=et[i][:], in0=et[i][:],
                            in1=masks[:, mi * 512:(mi + 1) * 512], op=ALU.mult))
                for i in range(nkt):
                    ch.op("tensor", lambda tt, i=i, nkt=nkt: tt.matmul(
                        out=pso[jb][:], lhsT=vtb[:, i * 65:(i + 1) * 65], rhs=et[i][:],
                        start=(i == 0), stop=(i == nkt - 1)))
                ch.op("vector", lambda v: v.tensor_copy(out=ob[jb][:], in_=pso[jb][:]))
                ch.op("vector", lambda v: v.tensor_copy(out=rd[jb][:], in_=ob[jb][64:65, :]))
                ch.op("tensor", lambda tt: tt.matmul(out=psbc[jb][:], lhsT=ones64[:],
                                                     rhs=rd[jb][:], start=True, stop=True))
                ch.op("vector", lambda v: v.reciprocal(out=bcb[jb][:], in_=psbc[jb][:]))
                ch.op("vector", lambda v: v.tensor_tensor(out=obn[jb][:], in0=ob[jb][0:64, :],
                                                          in1=bcb[jb][:], op=ALU.mult))
                ch.dma("gpsimd", oT_dram[h * 64:(h + 1) * 64, j * 512:(j + 1) * 512],
                       obn[jb][:])

            for j in range(T // 512):
                _jchunk(j)

        for h in range(NH):
            _head(h)

    with ExitStack() as st4:
        oT = [alloc_sb(nc, st4, f"l0a_oT{c}", [128, T], BF16) for c in range(8)]
        for c in range(8):
            ch.dma("scalar", oT[c][:], oT_dram[c * 128:(c + 1) * 128, :])
        wp = [alloc_sb(nc, st4, f"l0a_wp{c}", [128, D], BF16) for c in range(8)]
        for c in range(8):
            ch.dma("gpsimd", wp[c][:], w_proj_dram[c * 128:(c + 1) * 128, :])
        ps2 = [alloc_ps(nc, st4, f"l0a_ps2{b}", [128, 512]) for b in range(2)]
        yb = [alloc_sb(nc, st4, f"l0a_yb{b}", [128, 512]) for b in range(2)]

        def _op_tile(it, ti, jc):
            p, y = ps2[it % 2], yb[it % 2]
            for c in range(8):
                ch.op("tensor", lambda tt, c=c: tt.matmul(
                    out=p[:], lhsT=oT[c][:, ti * 128:(ti + 1) * 128],
                    rhs=wp[c][:, jc * 512:(jc + 1) * 512],
                    start=(c == 0), stop=(c == 7)))
            ch.op("vector", lambda v: v.tensor_copy(out=y[:], in_=p[:]))
            ch.dma("gpsimd", y_dram[ti * 128:(ti + 1) * 128,
                                    jc * 512:(jc + 1) * 512], y[:])

        it = 0
        for ti in range(T // 128):
            for jc in range(2):
                _op_tile(it, ti, jc)
                it += 1

    with ExitStack() as st5:
        emit_ln_tok(nc, ch, st5, "l0a_ln2", y_dram, None, res_add_dram=x_dram,
                    x_out_dram=x1_dram)


def build_launch_l0attn():
    nc = bass.Bass("TRN2", target_bir_lowering=False)
    x_in = nc.dram_tensor("x", [T, D], F32, kind="ExternalInput").ap()
    w_attn = nc.dram_tensor("w_attn", [D, 3 * D], F32, kind="ExternalInput").ap()
    w_proj = nc.dram_tensor("w_proj", [D, D], F32, kind="ExternalInput").ap()
    x1 = nc.dram_tensor("x1", [T, D], F32, kind="ExternalOutput").ap()
    scratch = {
        "ln": nc.dram_tensor("s_ln", [T, D], BF16).ap(),
        "qkT": nc.dram_tensor("s_qkT", [2 * D, T], BF16).ap(),
        "v": nc.dram_tensor("s_v", [T // 128, 16, 128, 64], BF16).ap(),
        "oT": nc.dram_tensor("s_oT", [D, T], BF16).ap(),
        "y": nc.dram_tensor("s_y", [T, D], F32).ap(),
        "nb": nc.dram_tensor("s_nb", [2, 2, 8], F32).ap(),
    }
    ch = Chain(nc)
    with ExitStack() as st:
        ch.make_sems(st)
        block = st.enter_context(nc.Block())
        emit_l0_attn(nc, ch, st, x_in, w_attn, w_proj, x1, scratch)
        ch.emit(block)
    return nc


E0 = 64
K0 = 6
CAP = 384
NSLOT = E0 * CAP
ROWW = 1040


def emit_l0ffn_routing(nc, ch, st, x1_dram, rw_dram, eb_dram, ln1_dram,
                       idx6_dram, w64_dram, totals_dram, scratch):
    with ExitStack() as st1:
        emit_ln_tok(nc, ch, st1, "l0f_ln", x1_dram, ln1_dram)

    with ExitStack() as st2:
        lnT = [alloc_sb(nc, st2, f"l0f_lnT{c}", [128, T], BF16) for c in range(8)]
        emit_transpose_to_feat(nc, ch, ln1_dram, lnT)
        rwb_dram = scratch["rwb"]
        ch.dma("gpsimd", rwb_dram, rw_dram)
        rwT = [alloc_sb(nc, st2, f"l0f_rwT{c}", [128, E0], BF16) for c in range(8)]
        for c in range(8):
            ch.dma_transpose(rwT[c][:], rwb_dram[:, c * 128:(c + 1) * 128])
        ebrow = alloc_sb(nc, st2, "l0f_ebrow", [1, E0])
        ones_r = alloc_sb(nc, st2, "l0f_ones_r", [1, 128])
        psb = alloc_ps(nc, st2, "l0f_psb", [128, E0])
        ebB = alloc_sb(nc, st2, "l0f_ebB", [128, E0])
        ch.dma("gpsimd", ebrow[:], eb_dram.rearrange("(x e) -> x e", x=1))
        ch.op("vector", lambda v: v.memset(ones_r[:], 1.0))
        ch.op("tensor", lambda t: t.matmul(out=psb[:], lhsT=ones_r[:], rhs=ebrow[:],
                                           start=True, stop=True))
        ch.op("vector", lambda v: v.tensor_copy(out=ebB[:], in_=psb[:]))

        iotaE = alloc_sb(nc, st2, "l0f_iotaE", [128, E0])
        ch.op("gpsimd", lambda g: g.iota(iotaE[:], pattern=[[1, E0]], base=0,
                                         channel_multiplier=0,
                                         allow_small_or_imprecise_dtypes=True))
        psl = [alloc_ps(nc, st2, f"l0f_psl{b}", [128, E0]) for b in range(2)]
        pst = alloc_ps(nc, st2, "l0f_pst", [E0, 1])
        scb = [alloc_sb(nc, st2, f"l0f_scb{b}", [128, E0]) for b in range(2)]
        corr = [alloc_sb(nc, st2, f"l0f_corr{b}", [128, E0]) for b in range(2)]
        mx8 = [alloc_sb(nc, st2, f"l0f_mx8{b}", [128, 8]) for b in range(2)]
        mi8 = [alloc_sb(nc, st2, f"l0f_mi8{b}", [128, 8], U32) for b in range(2)]
        mif = [alloc_sb(nc, st2, f"l0f_mif{b}", [128, 8]) for b in range(2)]
        ohr = [alloc_sb(nc, st2, f"l0f_ohr{b}", [128, E0]) for b in range(2)]
        amat = [alloc_sb(nc, st2, f"l0f_amat{b}", [128, E0]) for b in range(2)]
        amatb = [alloc_sb(nc, st2, f"l0f_amatb{b}", [128, E0], BF16) for b in range(2)]
        w64 = [alloc_sb(nc, st2, f"l0f_w64{b}", [128, E0]) for b in range(2)]
        tsum = [alloc_sb(nc, st2, f"l0f_tsum{b}", [128, 1]) for b in range(2)]
        trec = [alloc_sb(nc, st2, f"l0f_trec{b}", [128, 1]) for b in range(2)]
        onescol = alloc_sb(nc, st2, "l0f_onescol", [128, 1], BF16)
        totsb = alloc_sb(nc, st2, "l0f_totsb", [E0, 1])
        ch.op("vector", lambda v: v.memset(onescol[:], 1.0))

        def _rt_tile(ti):
            b = ti % 2
            pslb, scbb, corrb, mx8b, mi8b, mifb = psl[b], scb[b], corr[b], mx8[b], mi8[b], mif[b]
            ohrb, amb, ambb, w64b, tsumb, trecb = ohr[b], amat[b], amatb[b], w64[b], tsum[b], trec[b]
            for c in range(8):
                ch.op("tensor", lambda t, c=c: t.matmul(
                    out=pslb[:], lhsT=lnT[c][:, ti * 128:(ti + 1) * 128], rhs=rwT[c][:],
                    start=(c == 0), stop=(c == 7)))
            ch.op("scalar", lambda s: s.activation(out=scbb[:], in_=pslb[:], func=AF.Sigmoid))
            ch.op("vector", lambda v: v.tensor_tensor(out=corrb[:], in0=scbb[:], in1=ebB[:],
                                                      op=ALU.add))
            ch.op("vector", lambda v: v.max(out=mx8b[:], in_=corrb[:]))
            ch.op("vector", lambda v: v.max_index(out=mi8b[:], in_max=mx8b[:], in_values=corrb[:]))
            ch.op("vector", lambda v: v.tensor_copy(out=mifb[:], in_=mi8b[:]))
            for r in range(K0):
                if r == 0:
                    ch.op("vector", lambda v, r=r: v.tensor_scalar(
                        out=amb[:], in0=iotaE[:], scalar1=mifb[:, r:r + 1], scalar2=None,
                        op0=ALU.is_equal))
                else:
                    ch.op("vector", lambda v, r=r: v.tensor_scalar(
                        out=ohrb[:], in0=iotaE[:], scalar1=mifb[:, r:r + 1], scalar2=None,
                        op0=ALU.is_equal))
                    ch.op("vector", lambda v: v.tensor_tensor(out=amb[:], in0=amb[:],
                                                              in1=ohrb[:], op=ALU.add))
            ch.op("vector", lambda v: v.tensor_tensor(out=w64b[:], in0=amb[:], in1=scbb[:],
                                                      op=ALU.mult))
            ch.op("vector", lambda v: v.tensor_reduce(out=tsumb[:], in_=w64b[:],
                                                      axis=AX.X, op=ALU.add))
            ch.op("vector", lambda v: v.tensor_scalar(out=tsumb[:], in0=tsumb[:],
                                                      scalar1=1e-20, scalar2=None, op0=ALU.add))
            ch.op("vector", lambda v: v.reciprocal(out=trecb[:], in_=tsumb[:]))
            ch.op("vector", lambda v: v.tensor_scalar(out=w64b[:], in0=w64b[:],
                                                      scalar1=trecb[:, 0:1], scalar2=None,
                                                      op0=ALU.mult))
            ch.op("vector", lambda v: v.tensor_copy(out=ambb[:], in_=amb[:]))
            ch.op("tensor", lambda t: t.matmul(out=pst[:], lhsT=ambb[:], rhs=onescol[:],
                                               start=(ti == 0), stop=(ti == T // 128 - 1)))
            ch.dma("gpsimd", idx6_dram[ti * 128:(ti + 1) * 128, :], mi8b[:])
            ch.dma("gpsimd", w64_dram[ti * 128:(ti + 1) * 128, :], w64b[:])

        for ti in range(T // 128):
            _rt_tile(ti)
        ch.op("vector", lambda v: v.tensor_copy(out=totsb[:], in_=pst[:]))
        ch.dma("gpsimd", totals_dram, totsb[:])


def emit_l0ffn_scatter(nc, ch, st, ln1_dram, idx6_dram, w64_dram, totals_all_dram,
                       prevmask_dram, slotbuf_dram, gidx_dram, cap=None):
    if cap is None:
        cap = CAP
    with ExitStack() as st2:
        pmsk = alloc_sb(nc, st2, "l0s_pmsk", [8, 1])
        tall = alloc_sb(nc, st2, "l0s_tall", [8, E0])
        psbase = alloc_ps(nc, st2, "l0s_psbase", [1, E0])
        ch.dma("gpsimd", pmsk[:], prevmask_dram)
        ch.dma("gpsimd", tall[:], totals_all_dram)
        ch.op("tensor", lambda t: t.matmul(out=psbase[:], lhsT=pmsk[:], rhs=tall[:],
                                           start=True, stop=True))
        baser = alloc_sb(nc, st2, "l0s_baser", [1, E0])
        ch.op("vector", lambda v: v.tensor_copy(out=baser[:], in_=psbase[:]))

        iotaE = alloc_sb(nc, st2, "l0s_iotaE", [128, E0])
        colid = alloc_sb(nc, st2, "l0s_colid", [128, 128])
        rowid = alloc_sb(nc, st2, "l0s_rowid", [128, 1])
        stri = alloc_sb(nc, st2, "l0s_stri", [128, 128])
        onesv = alloc_sb(nc, st2, "l0s_onesv", [1, 128])
        ch.op("gpsimd", lambda g: g.iota(iotaE[:], pattern=[[1, E0]], base=0,
                                         channel_multiplier=0,
                                         allow_small_or_imprecise_dtypes=True))
        ch.op("gpsimd", lambda g: g.iota(colid[:], pattern=[[1, 128]], base=0,
                                         channel_multiplier=0,
                                         allow_small_or_imprecise_dtypes=True))
        ch.op("gpsimd", lambda g: g.iota(rowid[:], pattern=[[0, 1]], base=0,
                                         channel_multiplier=1,
                                         allow_small_or_imprecise_dtypes=True))
        ch.op("vector", lambda v: v.tensor_scalar(out=stri[:], in0=colid[:],
                                                  scalar1=rowid[:, 0:1], scalar2=None,
                                                  op0=ALU.is_gt))
        ch.op("vector", lambda v: v.memset(onesv[:], 1.0))

        mi8 = alloc_sb(nc, st2, "l0s_mi8", [128, 8], U32)
        mif = alloc_sb(nc, st2, "l0s_mif", [128, 8])
        w64 = alloc_sb(nc, st2, "l0s_w64", [128, E0])
        ohr = alloc_sb(nc, st2, "l0s_ohr", [128, E0])
        amat = alloc_sb(nc, st2, "l0s_amat", [128, E0])
        tmpe = alloc_sb(nc, st2, "l0s_tmpe", [128, E0])
        pspos = alloc_ps(nc, st2, "l0s_pspos", [128, E0])
        posf = alloc_sb(nc, st2, "l0s_posf", [128, E0])
        valid = alloc_sb(nc, st2, "l0s_valid", [128, E0])
        wmask = alloc_sb(nc, st2, "l0s_wmask", [128, E0])
        lnb = alloc_sb(nc, st2, "l0s_lnb", [128, D], BF16)
        comb = alloc_sb(nc, st2, "l0s_comb", [128, ROWW], BF16)
        ch.op("vector", lambda v: v.memset(comb[:], 0.0))
        posr = alloc_sb(nc, st2, "l0s_posr", [128, 1])
        wr = alloc_sb(nc, st2, "l0s_wr", [128, 1])
        vr = alloc_sb(nc, st2, "l0s_vr", [128, 1], U32)
        slotf = alloc_sb(nc, st2, "l0s_slotf", [128, 1])
        ssel = alloc_sb(nc, st2, "l0s_ssel", [128, 1])
        invf = alloc_sb(nc, st2, "l0s_invf", [128, 1])
        slotu = alloc_sb(nc, st2, "l0s_slotu", [128, 1], U32)
        gslotf = alloc_sb(nc, st2, "l0s_gslotf", [128, 1])
        gslotu = alloc_sb(nc, st2, "l0s_gsu", [128, 6], U32)
        zrow = alloc_sb(nc, st2, "l0s_zrow", [128, 1])
        ch.op("vector", lambda v: v.memset(zrow[:], float(NSLOT)))
        rbase = alloc_sb(nc, st2, "l0s_rbase", [1, E0])
        psts = alloc_ps(nc, st2, "l0s_psts", [1, E0])
        tsrow = alloc_sb(nc, st2, "l0s_tsrow", [1, E0])
        onesc128 = alloc_sb(nc, st2, "l0s_onesc128", [128, 1])
        ch.op("vector", lambda v: v.memset(onesc128[:], 1.0))
        ch.op("vector", lambda v: v.tensor_copy(out=rbase[:], in_=baser[:]))

        for ti in range(T // 128):
            r0 = ti * 128
            ch.dma("gpsimd", mi8[:], idx6_dram[r0:r0 + 128, :])
            ch.dma("gpsimd", w64[:], w64_dram[r0:r0 + 128, :])
            ch.dma("gpsimd", lnb[:], ln1_dram[r0:r0 + 128, :])
            ch.op("vector", lambda v: v.tensor_copy(out=mif[:], in_=mi8[:]))
            for r in range(K0):
                if r == 0:
                    ch.op("vector", lambda v, r=r: v.tensor_scalar(
                        out=amat[:], in0=iotaE[:], scalar1=mif[:, r:r + 1], scalar2=None,
                        op0=ALU.is_equal))
                else:
                    ch.op("vector", lambda v, r=r: v.tensor_scalar(
                        out=ohr[:], in0=iotaE[:], scalar1=mif[:, r:r + 1], scalar2=None,
                        op0=ALU.is_equal))
                    ch.op("vector", lambda v: v.tensor_tensor(out=amat[:], in0=amat[:],
                                                              in1=ohr[:], op=ALU.add))
            ch.op("tensor", lambda t: t.matmul(out=pspos[:], lhsT=stri[:], rhs=amat[:],
                                               start=True, stop=False))
            ch.op("tensor", lambda t: t.matmul(out=pspos[:], lhsT=onesv[:], rhs=rbase[:],
                                               start=False, stop=True))
            ch.op("vector", lambda v: v.tensor_copy(out=posf[:], in_=pspos[:]))
            ch.op("vector", lambda v, cap=cap: v.tensor_scalar(
                out=valid[:], in0=posf[:], scalar1=float(cap), scalar2=None, op0=ALU.is_lt))
            ch.op("vector", lambda v: v.tensor_tensor(out=wmask[:], in0=w64[:], in1=valid[:],
                                                      op=ALU.mult))
            ch.op("tensor", lambda t: t.matmul(out=psts[:], lhsT=onesc128[:], rhs=amat[:],
                                               start=True, stop=True))
            ch.op("vector", lambda v: v.tensor_copy(out=tsrow[:], in_=psts[:]))
            ch.op("vector", lambda v: v.tensor_tensor(out=rbase[:], in0=rbase[:], in1=tsrow[:],
                                                      op=ALU.add))
            for r in range(K0):
                ch.op("vector", lambda v, r=r: v.tensor_scalar(
                    out=ohr[:], in0=iotaE[:], scalar1=mif[:, r:r + 1], scalar2=None,
                    op0=ALU.is_equal))
                ch.op("vector", lambda v: v.tensor_tensor(out=tmpe[:], in0=ohr[:], in1=posf[:],
                                                          op=ALU.mult))
                ch.op("vector", lambda v: v.tensor_reduce(out=posr[:], in_=tmpe[:],
                                                          axis=AX.X, op=ALU.add))
                ch.op("vector", lambda v: v.tensor_tensor(out=tmpe[:], in0=ohr[:], in1=wmask[:],
                                                          op=ALU.mult))
                ch.op("vector", lambda v: v.tensor_reduce(out=wr[:], in_=tmpe[:],
                                                          axis=AX.X, op=ALU.add))
                ch.op("vector", lambda v, cap=cap: v.tensor_scalar(
                    out=vr[:], in0=posr[:], scalar1=float(cap), scalar2=None, op0=ALU.is_lt))
                ch.op("vector", lambda v, r=r, cap=cap: v.tensor_scalar(
                    out=slotf[:], in0=mif[:, r:r + 1], scalar1=float(cap),
                    scalar2=posr[:, 0:1], op0=ALU.mult, op1=ALU.add))
                ch.op("vector", lambda v: v.tensor_scalar(
                    out=invf[:], in0=rowid[:], scalar1=float(NSLOT), scalar2=None,
                    op0=ALU.add))
                ch.op("vector", lambda v: v.select(out=ssel[:], mask=vr[:], on_true=slotf[:],
                                                   on_false=invf[:], add_drain=True))
                ch.op("vector", lambda v: v.tensor_copy(out=slotu[:], in_=ssel[:]))
                ch.op("vector", lambda v: v.select(out=gslotf[:], mask=vr[:], on_true=slotf[:],
                                                   on_false=zrow[:], add_drain=True))
                ch.op("vector", lambda v, r=r: v.tensor_copy(out=gslotu[:, r:r + 1], in_=gslotf[:]))
                ch.op("vector", lambda v: v.tensor_copy(out=comb[:, 0:D], in_=lnb[:]))
                ch.op("vector", lambda v: v.tensor_copy(out=comb[:, D:D + 1], in_=wr[:]))
                ch.dma_fn("gpsimd", lambda g: g.indirect_dma_start(
                    out=slotbuf_dram,
                    out_offset=bass.IndirectOffsetOnAxis(ap=slotu[:], axis=0),
                    in_=comb[:], in_offset=None))
            ch.dma("gpsimd", gidx_dram[r0:r0 + 128, :], gslotu[:])


GELU_C = 0.7978845608028654
GELU_A = 0.044715


def emit_gelu_tanh(nc, ch, gb, pin, out_bf, pre):
    g2, g3, inner, th = gb
    ch.op("scalar", lambda s: s.activation(out=g2[:], in_=pin[:], func=AF.Square))
    ch.op("vector", lambda v: v.tensor_tensor(out=g3[:], in0=g2[:], in1=pin[:], op=ALU.mult))
    ch.op("vector", lambda v: v.tensor_scalar(out=inner[:], in0=g3[:], scalar1=GELU_A,
                                              scalar2=None, op0=ALU.mult))
    ch.op("vector", lambda v: v.tensor_tensor(out=inner[:], in0=inner[:], in1=pin[:], op=ALU.add))
    ch.op("scalar", lambda s: s.activation(out=th[:], in_=inner[:], func=AF.Tanh, scale=GELU_C))
    ch.op("vector", lambda v: v.tensor_scalar(out=th[:], in0=th[:], scalar1=1.0, scalar2=0.5,
                                              op0=ALU.add, op1=ALU.mult))
    ch.op("vector", lambda v: v.tensor_tensor(out=out_bf[:], in0=th[:], in1=pin[:], op=ALU.mult))


def emit_l0ffn_experts(nc, ch, st, parts_dram, w1_dram, w2_dram, ys_dram, scratch):
    NR = 8 * CAP
    NTile = NR // 128
    xs_dram = scratch["xs"]
    with ExitStack() as st2:
        acc = alloc_sb(nc, st2, "l0e_acc", [128, ROWW])
        pb = alloc_sb(nc, st2, "l0e_pb", [128, ROWW], BF16)
        accb = alloc_sb(nc, st2, "l0e_accb", [128, D], BF16)
        wcol = alloc_sb(nc, st2, "l0e_wcol", [128, NTile])
        for s in range(NTile):
            r0 = s * 128
            for p in range(8):
                ch.dma("gpsimd", pb[:], parts_dram[p, r0:r0 + 128, :])
                if p == 0:
                    ch.op("vector", lambda v: v.tensor_copy(out=acc[:], in_=pb[:]))
                else:
                    ch.op("vector", lambda v: v.tensor_tensor(out=acc[:], in0=acc[:],
                                                              in1=pb[:], op=ALU.add))
            ch.op("vector", lambda v: v.tensor_copy(out=accb[:], in_=acc[:, 0:D]))
            ch.op("vector", lambda v, s=s: v.tensor_copy(out=wcol[:, s:s + 1],
                                                         in_=acc[:, D:D + 1]))
            ch.dma("gpsimd", xs_dram[r0:r0 + 128, :], accb[:])

        xsT = [alloc_sb(nc, st2, f"l0e_xsT{c}", [128, NR], BF16) for c in range(8)]
        for c in range(8):
            ch.dma_transpose(xsT[c][:], xs_dram[:, c * 128:(c + 1) * 128])

        w1b = [alloc_sb(nc, st2, f"l0e_w1b{c}", [128, 512], BF16) for c in range(8)]
        w2b = [alloc_sb(nc, st2, f"l0e_w2b{c}", [128, D], BF16) for c in range(4)]
        psh = alloc_ps(nc, st2, "l0e_psh", [128, CAP])
        hb = [alloc_sb(nc, st2, f"l0e_hb{i}", [128, CAP], BF16) for i in range(4)]
        g2 = alloc_sb(nc, st2, "l0e_g2", [128, CAP])
        g3 = alloc_sb(nc, st2, "l0e_g3", [128, CAP])
        inner = alloc_sb(nc, st2, "l0e_inner", [128, CAP])
        th = alloc_sb(nc, st2, "l0e_th", [128, CAP])
        psd = alloc_ps(nc, st2, "l0e_psd", [128, 512])
        yb = alloc_sb(nc, st2, "l0e_yb", [128, 512], BF16)

        for e in range(8):
            for c in range(8):
                ch.dma("gpsimd", w1b[c][:], w1_dram[e, c * 128:(c + 1) * 128, :])
            for c in range(4):
                ch.dma("gpsimd", w2b[c][:], w2_dram[e, c * 128:(c + 1) * 128, :])
            s0 = e * CAP
            for ic in range(4):
                for c in range(8):
                    ch.op("tensor", lambda t, c=c, ic=ic, s0=s0: t.matmul(
                        out=psh[:], lhsT=w1b[c][:, ic * 128:(ic + 1) * 128],
                        rhs=xsT[c][:, s0:s0 + CAP],
                        start=(c == 0), stop=(c == 7)))
                emit_gelu_tanh(nc, ch, (g2, g3, inner, th), psh, hb[ic], f"l0e_g{e}_{ic}")
            for stile in range(CAP // 128):
                for jc in range(2):
                    for ic in range(4):
                        ch.op("tensor", lambda t, ic=ic, stile=stile, jc=jc: t.matmul(
                            out=psd[:], lhsT=hb[ic][:, stile * 128:(stile + 1) * 128],
                            rhs=w2b[ic][:, jc * 512:(jc + 1) * 512],
                            start=(ic == 0), stop=(ic == 3)))
                    gt = e * (CAP // 128) + stile
                    ch.op("vector", lambda v, gt=gt: v.tensor_scalar(
                        out=yb[:], in0=psd[:], scalar1=wcol[:, gt:gt + 1], scalar2=None,
                        op0=ALU.mult))
                    ch.dma("gpsimd", ys_dram[s0 + stile * 128:s0 + (stile + 1) * 128,
                                             jc * 512:(jc + 1) * 512], yb[:])


def emit_l0ffn_combine(nc, ch, st, x1_dram, ln1_dram, gidx_dram, ysall_dram,
                       sw1_dram, sw2_dram, x2_dram, scratch):
    ysbuf = scratch["ysbuf"]
    y_dram = scratch["yffn"]
    sw1b_dram = scratch["sw1b"]
    sw2b_dram = scratch["sw2b"]
    hsT_dram = scratch["hsT"]

    with ExitStack() as st2:
        zb = alloc_sb(nc, st2, "l0c_zb", [128, D], BF16)
        ch.op("vector", lambda v: v.memset(zb[:], 0.0))
        ch.dma("scalar", ysbuf[0:NSLOT, :], ysall_dram)
        ch.dma("gpsimd", ysbuf[NSLOT:NSLOT + 1, :], zb[0:1, :])

        ch.dma("gpsimd", sw1b_dram, sw1_dram)
        ch.dma("gpsimd", sw2b_dram, sw2_dram)
        sw1T = [alloc_sb(nc, st2, f"l0c_sw1T{c}", [128, 2048], BF16) for c in range(8)]
        for c in range(8):
            ch.dma_transpose(sw1T[c][:], sw1b_dram[:, c * 128:(c + 1) * 128])

        lnT = [alloc_sb(nc, st2, f"l0c_lnT{c}", [128, T], BF16) for c in range(8)]
        emit_transpose_to_feat(nc, ch, ln1_dram, lnT)

        psh = alloc_ps(nc, st2, "l0c_psh", [128, 512])
        hbf = alloc_sb(nc, st2, "l0c_hbf", [128, 512], BF16)
        g2 = alloc_sb(nc, st2, "l0c_g2", [128, 512])
        g3 = alloc_sb(nc, st2, "l0c_g3", [128, 512])
        inner = alloc_sb(nc, st2, "l0c_inner", [128, 512])
        th = alloc_sb(nc, st2, "l0c_th", [128, 512])
        for fc in range(16):
            for t in range(T // 512):
                for c in range(8):
                    ch.op("tensor", lambda tt, c=c, fc=fc, t=t: tt.matmul(
                        out=psh[:], lhsT=sw1T[c][:, fc * 128:(fc + 1) * 128],
                        rhs=lnT[c][:, t * 512:(t + 1) * 512],
                        start=(c == 0), stop=(c == 7)))
                emit_gelu_tanh(nc, ch, (g2, g3, inner, th), psh, hbf, f"l0c_g{fc}_{t}")
                ch.dma("gpsimd", hsT_dram[fc * 128:(fc + 1) * 128, t * 512:(t + 1) * 512],
                       hbf[:])

    with ExitStack() as st2b:
        sw2T = [alloc_sb(nc, st2b, f"l0c_sw2T{c}", [128, D], BF16) for c in range(16)]
        for c in range(16):
            ch.dma_transpose(sw2T[c][:], sw2b_dram[:, c * 128:(c + 1) * 128])
        hstile = alloc_sb(nc, st2b, "l0c_hstile", [128, 16, 128], BF16)
        gix = alloc_sb(nc, st2b, "l0c_gix", [128, 6], U32)
        racc = alloc_sb(nc, st2b, "l0c_racc", [128, D])
        psd = alloc_ps(nc, st2b, "l0c_psd", [128, 512])
        ycmb = alloc_sb(nc, st2b, "l0c_ycmb", [128, D])
        for ti in range(T // 128):
            r0 = ti * 128
            ch.dma("gpsimd", hstile[:],
                   hsT_dram[:, r0:r0 + 128].rearrange("(c p) t -> p c t", p=128))
            ch.dma("gpsimd", gix[:], gidx_dram[r0:r0 + 128, :])
            ch.op("vector", lambda v: v.memset(racc[:], 0.0))
            for r in range(K0):
                ch.dma_fn("gpsimd", lambda g, r=r: g.indirect_dma_start(
                    out=racc[:], out_offset=None, in_=ysbuf,
                    in_offset=bass.IndirectOffsetOnAxis(ap=gix[:, r:r + 1], axis=0),
                    compute_op=ALU.add))
            for jc in range(2):
                for c in range(16):
                    ch.op("tensor", lambda tt, c=c, jc=jc: tt.matmul(
                        out=psd[:], lhsT=hstile[:, c, :],
                        rhs=sw2T[c][:, jc * 512:(jc + 1) * 512],
                        start=(c == 0), stop=(c == 15)))
                ch.op("vector", lambda v, jc=jc: v.tensor_tensor(
                    out=ycmb[:, jc * 512:(jc + 1) * 512], in0=psd[:],
                    in1=racc[:, jc * 512:(jc + 1) * 512], op=ALU.add))
            ch.dma("gpsimd", y_dram[r0:r0 + 128, :], ycmb[:])

    with ExitStack() as st3:
        emit_ln_tok(nc, ch, st3, "l0c_ln2", y_dram, None, res_add_dram=x1_dram,
                    x_out_dram=x2_dram)



def _mk(nc, name, shape, dt, kind=None):
    if kind:
        return nc.dram_tensor(name, shape, dt, kind=kind).ap()
    return nc.dram_tensor(name, shape, dt).ap()


def build_launch_l0ffn_route():
    nc = bass.Bass("TRN2", target_bir_lowering=False)
    x1 = _mk(nc, "x1", [T, D], F32, "ExternalInput")
    rw = _mk(nc, "rw", [E0, D], F32, "ExternalInput")
    eb = _mk(nc, "eb", [E0], F32, "ExternalInput")
    ln1 = _mk(nc, "ln1", [T, D], BF16, "ExternalOutput")
    idx6 = _mk(nc, "idx6", [T, 8], U32, "ExternalOutput")
    w64 = _mk(nc, "w64", [T, E0], F32, "ExternalOutput")
    totals = _mk(nc, "totals", [E0, 1], F32, "ExternalOutput")
    scratch = {"rwb": _mk(nc, "s_rwb", [E0, D], BF16)}
    ch = Chain(nc)
    with ExitStack() as st:
        ch.make_sems(st)
        block = st.enter_context(nc.Block())
        emit_l0ffn_routing(nc, ch, st, x1, rw, eb, ln1, idx6, w64, totals, scratch)
        ch.emit(block)
    return nc


def build_launch_l0ffn_scatter():
    nc = bass.Bass("TRN2", target_bir_lowering=False)
    ln1 = _mk(nc, "ln1", [T, D], BF16, "ExternalInput")
    idx6 = _mk(nc, "idx6", [T, 8], U32, "ExternalInput")
    w64 = _mk(nc, "w64", [T, E0], F32, "ExternalInput")
    totals_all = _mk(nc, "totals_all", [NCORES, E0], F32, "ExternalInput")
    prevmask = _mk(nc, "prevmask", [NCORES, 1], F32, "ExternalInput")
    slotbuf = _mk(nc, "slotbuf", [NSLOT + 128, ROWW], BF16, "ExternalOutput")
    gidx = _mk(nc, "gidx", [T, 6], U32, "ExternalOutput")
    ch = Chain(nc)
    with ExitStack() as st:
        ch.make_sems(st)
        block = st.enter_context(nc.Block())
        zt = alloc_sb(nc, st, "l0s_zt", [128, ROWW], BF16)
        ch.op("vector", lambda v: v.memset(zt[:], 0.0))
        nrep = NSLOT // 128 + 1
        half = (nrep + 1) // 2
        for hh in range(2):
            lo, hi = hh * half, min((hh + 1) * half, nrep)
            ch.dma("gpsimd",
                   slotbuf[lo * 128:hi * 128, :]
                   .rearrange("(n p) w -> p n w", p=128),
                   zt[:].unsqueeze(1).broadcast_to((128, hi - lo, ROWW)))
        emit_l0ffn_scatter(nc, ch, st, ln1, idx6, w64, totals_all, prevmask, slotbuf, gidx)
        ch.emit(block)
    return nc


def build_launch_l0ffn_experts():
    nc = bass.Bass("TRN2", target_bir_lowering=False)
    parts = _mk(nc, "parts", [NCORES, 8 * CAP, ROWW], BF16, "ExternalInput")
    w1 = _mk(nc, "w1", [8, D, 512], F32, "ExternalInput")
    w2 = _mk(nc, "w2", [8, 512, D], F32, "ExternalInput")
    ys = _mk(nc, "ys", [8 * CAP, D], BF16, "ExternalOutput")
    scratch = {"xs": _mk(nc, "s_xs", [8 * CAP, D], BF16)}
    ch = Chain(nc)
    with ExitStack() as st:
        ch.make_sems(st)
        block = st.enter_context(nc.Block())
        emit_l0ffn_experts(nc, ch, st, parts, w1, w2, ys, scratch)
        ch.emit(block)
    return nc


def build_launch_l0ffn_combine():
    nc = bass.Bass("TRN2", target_bir_lowering=False)
    x1 = _mk(nc, "x1", [T, D], F32, "ExternalInput")
    ln1 = _mk(nc, "ln1", [T, D], BF16, "ExternalInput")
    gidx = _mk(nc, "gidx", [T, 6], U32, "ExternalInput")
    ysall = _mk(nc, "ysall", [NSLOT, D], BF16, "ExternalInput")
    sw1 = _mk(nc, "sw1", [2048, D], F32, "ExternalInput")
    sw2 = _mk(nc, "sw2", [D, 2048], F32, "ExternalInput")
    x2 = _mk(nc, "x2", [T, D], F32, "ExternalOutput")
    scratch = {
        "ysbuf": _mk(nc, "s_ysbuf", [NSLOT + 1, D], BF16),
        "yffn": _mk(nc, "s_yffn", [T, D], F32),
        "sw1b": _mk(nc, "s_sw1b", [2048, D], BF16),
        "sw2b": _mk(nc, "s_sw2b", [D, 2048], BF16),
        "hsT": _mk(nc, "s_hsT", [2048, T], BF16),
    }
    ch = Chain(nc)
    with ExitStack() as st:
        ch.make_sems(st)
        block = st.enter_context(nc.Block())
        emit_l0ffn_combine(nc, ch, st, x1, ln1, gidx, ysall, sw1, sw2, x2, scratch)
        ch.emit(block)
    return nc


LCH = 256


def emit_l1_attn(nc, ch, st, x2_dram, wqkv_dram, bqkv_dram, wout_dram, bout_dram,
                 x3_dram, scratch):
    ln_dram = scratch["l1ln"]
    pqT_dram = scratch["pqT"]
    pkT_dram = scratch["pkT"]
    ktok_dram = scratch["ktok"]
    v_dram = scratch["l1v"]
    oT_dram = scratch["l1oT"]
    y_dram = scratch["l1y"]

    with ExitStack() as st1:
        emit_ln_tok(nc, ch, st1, "l1a_ln1", x2_dram, ln_dram)

    with ExitStack() as st2:
        xT = [alloc_sb(nc, st2, f"l1a_xT{c}", [128, T], BF16) for c in range(8)]
        emit_transpose_to_feat(nc, ch, ln_dram, xT)
        wat = [alloc_sb(nc, st2, f"l1a_w{c}", [128, 3072], BF16) for c in range(8)]
        for c in range(8):
            ch.dma("gpsimd", wat[c][:], wqkv_dram[c * 128:(c + 1) * 128, :])
        bcol = [alloc_sb(nc, st2, f"l1a_bcol{b}", [128, 1]) for b in range(2)]
        bcol8 = [alloc_sb(nc, st2, f"l1a_bcol8{b}", [128, 1]) for b in range(2)]
        ps = [alloc_ps(nc, st2, f"l1a_ps{b}", [128, 512]) for b in range(2)]
        qf = [alloc_sb(nc, st2, f"l1a_qf{b}", [128, 512]) for b in range(2)]
        zneg = [alloc_sb(nc, st2, f"l1a_zneg{b}", [128, 512]) for b in range(2)]
        zexp = [alloc_sb(nc, st2, f"l1a_zexp{b}", [128, 512]) for b in range(2)]
        zpos = [alloc_sb(nc, st2, f"l1a_zpos{b}", [128, 512]) for b in range(2)]
        phib = [alloc_sb(nc, st2, f"l1a_phib{b}", [128, 512], BF16) for b in range(2)]
        evb = [alloc_sb(nc, st2, f"l1a_evb{b}", [128, 512], BF16) for b in range(2)]

        def _phi_col(jt, which, col0, dstT, j):
            scale = 0.125 if which == "q" else 1.0
            jb = jt % 2
            ch.dma("gpsimd", bcol[jb][:], bqkv_dram[col0 + j * 128:col0 + (j + 1) * 128]
                   .rearrange("(p o) -> p o", o=1))
            ch.op("vector", lambda v: v.tensor_scalar(
                out=bcol8[jb][:], in0=bcol[jb][:], scalar1=scale, scalar2=None, op0=ALU.mult))

            def _phi_tile(it, t):
                b = it % 2
                psb_, qfb, znegb, zexpb, zposb, phibb = \
                    ps[b], qf[b], zneg[b], zexp[b], zpos[b], phib[b]
                for c in range(8):
                    ch.op("tensor", lambda tt, c=c: tt.matmul(
                        out=psb_[:], lhsT=wat[c][:, col0 + j * 128:col0 + (j + 1) * 128],
                        rhs=xT[c][:, t * 512:(t + 1) * 512],
                        start=(c == 0), stop=(c == 7)))
                ch.op("vector", lambda v: v.tensor_scalar(
                    out=qfb[:], in0=psb_[:], scalar1=scale, scalar2=bcol8[jb][:, 0:1],
                    op0=ALU.mult, op1=ALU.add))
                ch.op("vector", lambda v: v.tensor_scalar(
                    out=znegb[:], in0=qfb[:], scalar1=0.0, scalar2=None, op0=ALU.min))
                ch.op("scalar", lambda s: s.activation(out=zexpb[:], in_=znegb[:], func=AF.Exp))
                ch.op("vector", lambda v: v.tensor_scalar(
                    out=zposb[:], in0=qfb[:], scalar1=0.0, scalar2=None, op0=ALU.max))
                ch.op("vector", lambda v: v.scalar_tensor_tensor(
                    out=phibb[:], in0=zposb[:], scalar=1e-4, in1=zexpb[:],
                    op0=ALU.add, op1=ALU.add))
                ch.dma("gpsimd", dstT[j * 128:(j + 1) * 128, t * 512:(t + 1) * 512],
                       phibb[:])

            for t in range(T // 512):
                _phi_tile(jt * (T // 512) + t, t)

        jt = 0
        for (which, col0, dstT) in (("q", 0, pqT_dram), ("k", D, pkT_dram)):
            for j in range(8):
                _phi_col(jt, which, col0, dstT, j)
                jt += 1

        vbrow = alloc_sb(nc, st2, "l1a_vbrow", [1, D])
        onesr = alloc_sb(nc, st2, "l1a_onesr", [1, 128])
        psvb = alloc_ps(nc, st2, "l1a_psvb", [128, 512])
        vbias = alloc_sb(nc, st2, "l1a_vbias", [128, D])
        ch.dma("gpsimd", vbrow[:], bqkv_dram[2 * D:3 * D].rearrange("(o d) -> o d", o=1))
        ch.op("vector", lambda v: v.memset(onesr[:], 1.0))
        for g in range(2):
            ch.op("tensor", lambda t, g=g: t.matmul(
                out=psvb[:], lhsT=onesr[:], rhs=vbrow[:, g * 512:(g + 1) * 512],
                start=True, stop=True))
            ch.op("vector", lambda v, g=g: v.tensor_copy(
                out=vbias[:, g * 512:(g + 1) * 512], in_=psvb[:]))
        def _v_tile(it, g, ti):
            b = it % 2
            for c in range(8):
                ch.op("tensor", lambda tt, c=c: tt.matmul(
                    out=ps[b][:], lhsT=xT[c][:, ti * 128:(ti + 1) * 128],
                    rhs=wat[c][:, 2 * D + g * 512: 2 * D + (g + 1) * 512],
                    start=(c == 0), stop=(c == 7)))
            ch.op("vector", lambda v: v.tensor_tensor(
                out=qf[b][:], in0=ps[b][:], in1=vbias[:, g * 512:(g + 1) * 512], op=ALU.add))
            ch.op("vector", lambda v: v.tensor_copy(out=evb[b][:], in_=qf[b][:]))
            ch.dma("gpsimd", v_dram[ti, 8 * g:8 * (g + 1), :, :].rearrange("h p c -> p h c"),
                   evb[b][:])

        it = 0
        for g in range(2):
            for ti in range(T // 128):
                _v_tile(it, g, ti)
                it += 1

        ktb = [alloc_sb(nc, st2, f"l1a_ktb{b}", [128, D], BF16) for b in range(2)]
        for ti in range(T // 128):
            ch.dma_transpose(ktb[ti % 2][:], pkT_dram[:, ti * 128:(ti + 1) * 128])
            ch.dma("gpsimd", ktok_dram[ti * 128:(ti + 1) * 128, :], ktb[ti % 2][:])

    with ExitStack() as st3:
        NCH = T // LCH
        SPC = LCH // 128
        pq = [alloc_sb(nc, st3, f"l1a_pq{b}", [64, T], BF16) for b in range(4)]
        pk = [alloc_sb(nc, st3, f"l1a_pk{b}", [64, T], BF16) for b in range(4)]
        vt = [alloc_sb(nc, st3, f"l1a_vt{b}", [128, (T // 128) * 65], BF16) for b in range(4)]
        kt = [alloc_sb(nc, st3, f"l1a_kt{b}", [128, (T // 128) * 64], BF16) for b in range(4)]
        kv = [alloc_sb(nc, st3, f"l1a_kv{j}", [64, 65]) for j in range(2)]
        kvb = [alloc_sb(nc, st3, f"l1a_kvb{j}", [64, 65], BF16) for j in range(2)]
        masks = alloc_sb(nc, st3, "l1a_masks", [128, SPC * LCH], BF16)
        colid = alloc_sb(nc, st3, "l1a_colid", [128, LCH])
        rowid = alloc_sb(nc, st3, "l1a_rowid", [128, 1])
        psa = [alloc_ps(nc, st3, f"l1a_psa{b}", [128, LCH]) for b in range(2)]
        ea = [[alloc_sb(nc, st3, f"l1a_ea{j}_{i}", [128, LCH], BF16) for i in range(SPC)]
              for j in range(2)]
        pso = [alloc_ps(nc, st3, f"l1a_pso{b}", [65, LCH]) for b in range(2)]
        pskv = [alloc_ps(nc, st3, f"l1a_pskv{b}", [64, 65]) for b in range(2)]
        rd = [alloc_sb(nc, st3, f"l1a_rd{b}", [1, LCH]) for b in range(2)]
        ones64 = alloc_sb(nc, st3, "l1a_ones64", [1, 64])
        psbc = [alloc_ps(nc, st3, f"l1a_psbc{b}", [64, LCH]) for b in range(2)]
        bcb = [alloc_sb(nc, st3, f"l1a_bcb{b}", [64, LCH]) for b in range(2)]
        obn = [alloc_sb(nc, st3, f"l1a_obn{b}", [64, LCH], BF16) for b in range(2)]

        ch.op("gpsimd", lambda g: g.iota(colid[:], pattern=[[1, LCH]], base=0,
                                         channel_multiplier=0,
                                         allow_small_or_imprecise_dtypes=True))
        ch.op("gpsimd", lambda g: g.iota(rowid[:], pattern=[[0, 1]], base=0,
                                         channel_multiplier=1,
                                         allow_small_or_imprecise_dtypes=True))
        for i in range(SPC):
            ch.op("vector", lambda v, i=i: v.tensor_scalar(
                out=masks[:, i * LCH:(i + 1) * LCH], in0=colid[:],
                scalar1=float(128 * i), scalar2=rowid[:, 0:1],
                op0=ALU.subtract, op1=ALU.is_ge))
        ch.op("vector", lambda v: v.memset(ones64[:], 1.0))

        def _load_head(h):
            hb_ = h % 4
            pqb, pkb, vtb, ktb2 = pq[hb_], pk[hb_], vt[hb_], kt[hb_]
            ch.dma("scalar", pqb[:], pqT_dram[h * 64:(h + 1) * 64, :])
            ch.dma("scalar", pkb[:], pkT_dram[h * 64:(h + 1) * 64, :])
            ch.op("vector", lambda v: v.memset(vtb[:], 1.0))
            ch.dma("scalar", vtb[:].rearrange("p (k c) -> p k c", c=65)[:, :, 0:64],
                   v_dram[:, h, :, :].rearrange("k p c -> p k c"))
            ch.dma("scalar", ktb2[:].rearrange("p (k c) -> p k c", c=64),
                   ktok_dram.rearrange("(k p) (hh c) -> p k hh c", p=128, c=64)[:, :, h, :])

        def _chunk(h, hh, cc):
            hb_ = h % 4
            pqb, pkb, vtb, ktb2 = pq[hb_], pk[hb_], vt[hb_], kt[hb_]
            kvh, kvbh, eah = kv[hh], kvb[hh], ea[hh]
            q0 = cc * LCH
            for i in range(SPC):
                s0 = (cc * SPC + i) * 128
                pa = psa[i % 2]
                ch.op("tensor", lambda tt, s0=s0, pa=pa, pkb=pkb, pqb=pqb, q0=q0: tt.matmul(
                    out=pa[:], lhsT=pkb[:, s0:s0 + 128], rhs=pqb[:, q0:q0 + LCH],
                    start=True, stop=True))
                ch.op("vector", lambda v, i=i, pa=pa, eah=eah: v.tensor_tensor(
                    out=eah[i][:], in0=pa[:], in1=masks[:, i * LCH:(i + 1) * LCH],
                    op=ALU.mult))
            for i in range(SPC):
                kti = cc * SPC + i
                ch.op("tensor", lambda tt, i=i, kti=kti, vtb=vtb, eah=eah: tt.matmul(
                    out=pso[hh][:], lhsT=vtb[:, kti * 65:(kti + 1) * 65], rhs=eah[i][:],
                    start=(i == 0), stop=False))
            ch.op("tensor", lambda tt, kvbh=kvbh, pqb=pqb, q0=q0: tt.matmul(
                out=pso[hh][:], lhsT=kvbh[:], rhs=pqb[:, q0:q0 + LCH],
                start=False, stop=True))
            ch.op("vector", lambda v: v.tensor_scalar(
                out=rd[hh][:], in0=pso[hh][64:65, :], scalar1=1e-6, scalar2=None,
                op0=ALU.add))
            ch.op("tensor", lambda tt: tt.matmul(out=psbc[hh][:], lhsT=ones64[:],
                                                 rhs=rd[hh][:], start=True, stop=True))
            ch.op("vector", lambda v: v.reciprocal(out=bcb[hh][:], in_=psbc[hh][:]))
            ch.op("vector", lambda v: v.tensor_tensor(out=obn[hh][:], in0=pso[hh][0:64, :],
                                                      in1=bcb[hh][:], op=ALU.mult))
            ch.dma("gpsimd", oT_dram[h * 64:(h + 1) * 64, q0:q0 + LCH], obn[hh][:])
            for i in range(SPC):
                kti = cc * SPC + i
                ch.op("tensor", lambda tt, i=i, kti=kti, ktb2=ktb2, vtb=vtb: tt.matmul(
                    out=pskv[hh][:], lhsT=ktb2[:, kti * 64:(kti + 1) * 64],
                    rhs=vtb[:, kti * 65:(kti + 1) * 65],
                    start=(i == 0), stop=(i == SPC - 1)))
            ch.op("vector", lambda v, kvh=kvh: v.tensor_tensor(out=kvh[:], in0=kvh[:],
                                                               in1=pskv[hh][:], op=ALU.add))
            ch.op("gpsimd", lambda g, kvh=kvh, kvbh=kvbh: g.tensor_copy(out=kvbh[:],
                                                                        in_=kvh[:]))

        for hp in range(NH // 2):
            h0 = hp * 2
            _load_head(h0)
            _load_head(h0 + 1)
            for j in range(2):
                ch.op("vector", lambda v, j=j: v.memset(kv[j][:], 0.0))
                ch.op("vector", lambda v, j=j: v.memset(kvb[j][:], 0.0))
            for cc in range(NCH):
                _chunk(h0, 0, cc)
                _chunk(h0 + 1, 1, cc)

    with ExitStack() as st4:
        oT = [alloc_sb(nc, st4, f"l1a_oT{c}", [128, T], BF16) for c in range(8)]
        for c in range(8):
            ch.dma("scalar", oT[c][:], oT_dram[c * 128:(c + 1) * 128, :])
        wp = [alloc_sb(nc, st4, f"l1a_wp{c}", [128, D], BF16) for c in range(8)]
        for c in range(8):
            ch.dma("gpsimd", wp[c][:], wout_dram[c * 128:(c + 1) * 128, :])
        borow = alloc_sb(nc, st4, "l1a_borow", [1, D])
        ones_r = alloc_sb(nc, st4, "l1a_onesr2", [1, 128])
        psbb = alloc_ps(nc, st4, "l1a_psbb", [128, 512])
        bob = alloc_sb(nc, st4, "l1a_bob", [128, D])
        ch.dma("gpsimd", borow[:], bout_dram.rearrange("(o d) -> o d", o=1))
        ch.op("vector", lambda v: v.memset(ones_r[:], 1.0))
        for jc in range(2):
            ch.op("tensor", lambda t, jc=jc: t.matmul(
                out=psbb[:], lhsT=ones_r[:], rhs=borow[:, jc * 512:(jc + 1) * 512],
                start=True, stop=True))
            ch.op("vector", lambda v, jc=jc: v.tensor_copy(
                out=bob[:, jc * 512:(jc + 1) * 512], in_=psbb[:]))
        ps2 = [alloc_ps(nc, st4, f"l1a_ps2{b}", [128, 512]) for b in range(2)]
        yb = [alloc_sb(nc, st4, f"l1a_yb{b}", [128, 512]) for b in range(2)]

        def _oproj(it, ti, jc):
            p, y = ps2[it % 2], yb[it % 2]
            for c in range(8):
                ch.op("tensor", lambda tt, c=c: tt.matmul(
                    out=p[:], lhsT=oT[c][:, ti * 128:(ti + 1) * 128],
                    rhs=wp[c][:, jc * 512:(jc + 1) * 512],
                    start=(c == 0), stop=(c == 7)))
            ch.op("vector", lambda v: v.tensor_tensor(
                out=y[:], in0=p[:], in1=bob[:, jc * 512:(jc + 1) * 512], op=ALU.add))
            ch.dma("gpsimd", y_dram[ti * 128:(ti + 1) * 128,
                                    jc * 512:(jc + 1) * 512], y[:])

        it = 0
        for ti in range(T // 128):
            for jc in range(2):
                _oproj(it, ti, jc)
                it += 1

    with ExitStack() as st5:
        emit_ln_tok(nc, ch, st5, "l1a_ln2", y_dram, None, res_add_dram=x2_dram,
                    x_out_dram=x3_dram)


def build_launch_l1attn():
    nc = bass.Bass("TRN2", target_bir_lowering=False)
    x2 = _mk(nc, "x2", [T, D], F32, "ExternalInput")
    wqkv = _mk(nc, "wqkv", [D, 3 * D], F32, "ExternalInput")
    bqkv = _mk(nc, "bqkv", [3 * D], F32, "ExternalInput")
    wout = _mk(nc, "wout", [D, D], F32, "ExternalInput")
    bout = _mk(nc, "bout", [D], F32, "ExternalInput")
    x3 = _mk(nc, "x3", [T, D], F32, "ExternalOutput")
    scratch = {
        "l1ln": _mk(nc, "s_l1ln", [T, D], BF16),
        "pqT": _mk(nc, "s_pqT", [D, T], BF16),
        "pkT": _mk(nc, "s_pkT", [D, T], BF16),
        "ktok": _mk(nc, "s_ktok", [T, D], BF16),
        "l1v": _mk(nc, "s_l1v", [T // 128, 16, 128, 64], BF16),
        "l1oT": _mk(nc, "s_l1oT", [D, T], BF16),
        "l1y": _mk(nc, "s_l1y", [T, D], F32),
    }
    ch = Chain(nc)
    with ExitStack() as st:
        ch.make_sems(st)
        block = st.enter_context(nc.Block())
        emit_l1_attn(nc, ch, st, x2, wqkv, bqkv, wout, bout, x3, scratch)
        ch.emit(block)
    return nc


E1 = 8
K1 = 4


def emit_l1_ffn(nc, ch, st, x3_dram, rw_dram, rb_dram, gw_dram, uw_dram, dw_dram,
                sgw_dram, suw_dram, sdw_dram, xout_dram, scratch):
    ln_dram = scratch["l2ln"]
    actT_dram = scratch["actT"]
    hsT_dram = scratch["hsT2"]
    y_dram = scratch["l2y"]

    w8all = alloc_sb(nc, st, "l2_w8all", [128, (T // 128) * E1])

    with ExitStack() as st1:
        emit_ln_tok(nc, ch, st1, "l2_ln1", x3_dram, ln_dram)

    with ExitStack() as st2:
        lnT = [alloc_sb(nc, st2, f"l2_lnT{c}", [128, T], BF16) for c in range(8)]
        emit_transpose_to_feat(nc, ch, ln_dram, lnT)

        rwb = [alloc_sb(nc, st2, f"l2_rwb{c}", [128, E1], BF16) for c in range(8)]
        for c in range(8):
            ch.dma("gpsimd", rwb[c][:], rw_dram[c * 128:(c + 1) * 128, :])
        psr = [alloc_ps(nc, st2, f"l2_psr{b}", [128, E1]) for b in range(2)]
        nmx = [alloc_sb(nc, st2, f"l2_nmx{b}", [128, 1]) for b in range(2)]
        exr = [alloc_sb(nc, st2, f"l2_exr{b}", [128, E1]) for b in range(2)]
        ssum = [alloc_sb(nc, st2, f"l2_ssum{b}", [128, 1]) for b in range(2)]
        srec = [alloc_sb(nc, st2, f"l2_srec{b}", [128, 1]) for b in range(2)]
        scr = [alloc_sb(nc, st2, f"l2_scr{b}", [128, E1]) for b in range(2)]
        mx8 = [alloc_sb(nc, st2, f"l2_mx8{b}", [128, 8]) for b in range(2)]
        mi8 = [alloc_sb(nc, st2, f"l2_mi8{b}", [128, 8], U32) for b in range(2)]
        mif = [alloc_sb(nc, st2, f"l2_mif{b}", [128, 8]) for b in range(2)]
        w4s = [alloc_sb(nc, st2, f"l2_w4s{b}", [128, 1]) for b in range(2)]
        w4r = [alloc_sb(nc, st2, f"l2_w4r{b}", [128, 1]) for b in range(2)]
        w4n = [alloc_sb(nc, st2, f"l2_w4n{b}", [128, K1]) for b in range(2)]
        iota8 = alloc_sb(nc, st2, "l2_iota8", [128, E1])
        rb1 = alloc_sb(nc, st2, "l2_rb1", [1, 1])
        onesc = alloc_sb(nc, st2, "l2_onesc", [1, 128])
        psrb = alloc_ps(nc, st2, "l2_psrb", [128, 1])
        rbB = alloc_sb(nc, st2, "l2_rbB", [128, 1])
        corr8 = [alloc_sb(nc, st2, f"l2_corr8{b}", [128, E1]) for b in range(2)]
        ch.dma("gpsimd", rb1[:], rb_dram)
        ch.op("vector", lambda v: v.memset(onesc[:], 1.0))
        ch.op("tensor", lambda t: t.matmul(out=psrb[:], lhsT=onesc[:], rhs=rb1[:],
                                           start=True, stop=True))
        ch.op("vector", lambda v: v.tensor_copy(out=rbB[:], in_=psrb[:]))
        ohw = [alloc_sb(nc, st2, f"l2_ohw{b}", [128, E1]) for b in range(2)]
        w8 = [alloc_sb(nc, st2, f"l2_w8{b}", [128, E1]) for b in range(2)]
        ch.op("gpsimd", lambda g: g.iota(iota8[:], pattern=[[1, E1]], base=0,
                                         channel_multiplier=0,
                                         allow_small_or_imprecise_dtypes=True))

        def _router_tile(ti, psr, nmx, exr, ssum, srec, scr, mx8, mi8, mif,
                         w4s, w4r, w4n, corr8, ohw, w8):
            for c in range(8):
                ch.op("tensor", lambda t, c=c: t.matmul(
                    out=psr[:], lhsT=lnT[c][:, ti * 128:(ti + 1) * 128], rhs=rwb[c][:],
                    start=(c == 0), stop=(c == 7)))
            ch.op("vector", lambda v: v.tensor_reduce(out=nmx[:], in_=psr[:],
                                                      axis=AX.X, op=ALU.max, negate=True))
            ch.op("scalar", lambda s: s.activation(out=exr[:], in_=psr[:], func=AF.Exp,
                                                   bias=nmx[:, 0:1]))
            ch.op("vector", lambda v: v.tensor_reduce(out=ssum[:], in_=exr[:],
                                                      axis=AX.X, op=ALU.add))
            ch.op("vector", lambda v: v.reciprocal(out=srec[:], in_=ssum[:]))
            ch.op("vector", lambda v: v.tensor_scalar(out=scr[:], in0=exr[:],
                                                      scalar1=srec[:, 0:1], scalar2=None,
                                                      op0=ALU.mult))
            ch.op("vector", lambda v: v.tensor_scalar(out=corr8[:], in0=scr[:],
                                                      scalar1=rbB[:, 0:1], scalar2=None,
                                                      op0=ALU.add))
            ch.op("vector", lambda v: v.max(out=mx8[:], in_=corr8[:]))
            ch.op("vector", lambda v: v.max_index(out=mi8[:], in_max=mx8[:], in_values=corr8[:]))
            ch.op("vector", lambda v: v.tensor_copy(out=mif[:], in_=mi8[:]))
            ch.op("vector", lambda v: v.tensor_scalar(out=mx8[:], in0=mx8[:],
                                                      scalar1=rbB[:, 0:1], scalar2=None,
                                                      op0=ALU.subtract))
            ch.op("vector", lambda v: v.tensor_reduce(out=w4s[:], in_=mx8[:, 0:K1],
                                                      axis=AX.X, op=ALU.add))
            ch.op("vector", lambda v: v.tensor_scalar(out=w4s[:], in0=w4s[:],
                                                      scalar1=1e-20, scalar2=None, op0=ALU.add))
            ch.op("vector", lambda v: v.reciprocal(out=w4r[:], in_=w4s[:]))
            ch.op("vector", lambda v: v.tensor_scalar(out=w4n[:], in0=mx8[:, 0:K1],
                                                      scalar1=w4r[:, 0:1], scalar2=None,
                                                      op0=ALU.mult))
            for r in range(K1):
                if r == 0:
                    ch.op("vector", lambda v, r=r: v.tensor_scalar(
                        out=w8[:], in0=iota8[:], scalar1=mif[:, r:r + 1],
                        scalar2=w4n[:, r:r + 1], op0=ALU.is_equal, op1=ALU.mult))
                else:
                    ch.op("vector", lambda v, r=r: v.tensor_scalar(
                        out=ohw[:], in0=iota8[:], scalar1=mif[:, r:r + 1],
                        scalar2=w4n[:, r:r + 1], op0=ALU.is_equal, op1=ALU.mult))
                    ch.op("vector", lambda v: v.tensor_tensor(out=w8[:], in0=w8[:],
                                                              in1=ohw[:], op=ALU.add))
            ch.op("vector", lambda v, ti=ti: v.tensor_copy(
                out=w8all[:, ti * E1:(ti + 1) * E1], in_=w8[:]))

        for ti in range(T // 128):
            b = ti % 2
            _router_tile(ti, psr[b], nmx[b], exr[b], ssum[b], srec[b], scr[b],
                         mx8[b], mi8[b], mif[b], w4s[b], w4r[b], w4n[b],
                         corr8[b], ohw[b], w8[b])

        gwb = [[alloc_sb(nc, st2, f"l2_gwb{b}_{c}", [128, 512], BF16) for c in range(8)]
               for b in range(2)]
        uwb = [[alloc_sb(nc, st2, f"l2_uwb{b}_{c}", [128, 512], BF16) for c in range(8)]
               for b in range(2)]
        psg = [alloc_ps(nc, st2, f"l2_psg{b}", [128, 512]) for b in range(2)]
        psu = [alloc_ps(nc, st2, f"l2_psu{b}", [128, 512]) for b in range(2)]
        sg = [alloc_sb(nc, st2, f"l2_sg{b}", [128, 512]) for b in range(2)]
        actb = [alloc_sb(nc, st2, f"l2_actb{b}", [128, 512], BF16) for b in range(2)]

        def _gu_tile(it, gw8, uw8, co, t, dst_dram, dst_row):
            b = it % 2
            for c in range(8):
                ch.op("tensor", lambda tt, c=c: tt.matmul(
                    out=psg[b][:], lhsT=gw8[c][:, co:co + 128],
                    rhs=lnT[c][:, t * 512:(t + 1) * 512],
                    start=(c == 0), stop=(c == 7)))
            for c in range(8):
                ch.op("tensor", lambda tt, c=c: tt.matmul(
                    out=psu[b][:], lhsT=uw8[c][:, co:co + 128],
                    rhs=lnT[c][:, t * 512:(t + 1) * 512],
                    start=(c == 0), stop=(c == 7)))
            ch.op("scalar", lambda s: s.activation(out=sg[b][:], in_=psg[b][:], func=AF.Silu))
            ch.op("vector", lambda v: v.tensor_tensor(out=actb[b][:], in0=sg[b][:],
                                                      in1=psu[b][:], op=ALU.mult))
            ch.dma("gpsimd", dst_dram[dst_row * 128:(dst_row + 1) * 128,
                                      t * 512:(t + 1) * 512], actb[b][:])

        it = 0
        for e in range(E1):
            eb_ = e % 2
            for c in range(8):
                ch.dma("gpsimd", gwb[eb_][c][:], gw_dram[e, c * 128:(c + 1) * 128, :])
                ch.dma("gpsimd", uwb[eb_][c][:], uw_dram[e, c * 128:(c + 1) * 128, :])
            for ic in range(4):
                for t in range(T // 512):
                    _gu_tile(it, gwb[eb_], uwb[eb_], ic * 128, t, actT_dram[e], ic)
                    it += 1

        sgw = [alloc_sb(nc, st2, f"l2_sgw{c}", [128, 2048], BF16) for c in range(8)]
        suw = [alloc_sb(nc, st2, f"l2_suw{c}", [128, 2048], BF16) for c in range(8)]
        for c in range(8):
            ch.dma("gpsimd", sgw[c][:], sgw_dram[c * 128:(c + 1) * 128, :])
            ch.dma("gpsimd", suw[c][:], suw_dram[c * 128:(c + 1) * 128, :])
        for mc in range(16):
            for t in range(T // 512):
                _gu_tile(it, sgw, suw, mc * 128, t, hsT_dram, mc)
                it += 1

    with ExitStack() as st3:
        dwb = [[alloc_sb(nc, st3, f"l2_dwb{e}_{c}", [128, D], BF16) for c in range(4)]
               for e in range(E1)]
        for e in range(E1):
            for c in range(4):
                ch.dma("gpsimd", dwb[e][c][:], dw_dram[e, c * 128:(c + 1) * 128, :])
        sdw = [alloc_sb(nc, st3, f"l2_sdw{c}", [128, D], BF16) for c in range(16)]
        for c in range(16):
            ch.dma("gpsimd", sdw[c][:], sdw_dram[c * 128:(c + 1) * 128, :])
        atile = [alloc_sb(nc, st3, f"l2_atile{b}", [128, E1, 4, 128], BF16) for b in range(2)]
        hstile = [alloc_sb(nc, st3, f"l2_hstile{b}", [128, 16, 128], BF16) for b in range(2)]
        psd = [alloc_ps(nc, st3, f"l2_psd{b}", [128, 512]) for b in range(2)]
        pss = [alloc_ps(nc, st3, f"l2_pss{b}", [128, 512]) for b in range(2)]
        rsum = [alloc_sb(nc, st3, f"l2_rsum{b}", [128, 512]) for b in range(2)]
        yb = [alloc_sb(nc, st3, f"l2_yb{b}", [128, 512]) for b in range(2)]

        def _down_tile(ti):
            b = ti % 2
            r0 = ti * 128
            ch.dma("scalar", atile[b][:],
                   actT_dram[:, :, r0:r0 + 128].rearrange("e (c p) t -> p e c t", p=128))
            ch.dma("scalar", hstile[b][:],
                   hsT_dram[:, r0:r0 + 128].rearrange("(c p) t -> p c t", p=128))
            for jc in range(2):
                jb = (ti * 2 + jc) % 2
                for e in range(E1):
                    pe_ = psd[e % 2]
                    for ic in range(4):
                        ch.op("tensor", lambda tt, e=e, ic=ic, jc=jc, pe_=pe_: tt.matmul(
                            out=pe_[:], lhsT=atile[b][:, e, ic, :],
                            rhs=dwb[e][ic][:, jc * 512:(jc + 1) * 512],
                            start=(ic == 0), stop=(ic == 3)))
                    if e == 0:
                        ch.op("vector", lambda v, e=e, jb=jb, pe_=pe_: v.tensor_scalar(
                            out=rsum[jb][:], in0=pe_[:],
                            scalar1=w8all[:, ti * E1 + e:ti * E1 + e + 1], scalar2=None,
                            op0=ALU.mult))
                    else:
                        ch.op("vector", lambda v, e=e, jb=jb, pe_=pe_: v.scalar_tensor_tensor(
                            out=rsum[jb][:], in0=pe_[:],
                            scalar=w8all[:, ti * E1 + e:ti * E1 + e + 1],
                            in1=rsum[jb][:], op0=ALU.mult, op1=ALU.add))
                for c in range(16):
                    ch.op("tensor", lambda tt, c=c, jc=jc, jb=jb: tt.matmul(
                        out=pss[jb][:], lhsT=hstile[b][:, c, :],
                        rhs=sdw[c][:, jc * 512:(jc + 1) * 512],
                        start=(c == 0), stop=(c == 15)))
                ch.op("vector", lambda v, jb=jb: v.tensor_tensor(out=yb[jb][:], in0=pss[jb][:],
                                                                 in1=rsum[jb][:], op=ALU.add))
                ch.dma("gpsimd", y_dram[r0:r0 + 128, jc * 512:(jc + 1) * 512], yb[jb][:])

        for ti in range(T // 128):
            _down_tile(ti)

    with ExitStack() as st4:
        emit_ln_tok(nc, ch, st4, "l2_ln2", y_dram, None, res_add_dram=x3_dram,
                    x_out_dram=xout_dram)


def build_launch_l1ffn():
    nc = bass.Bass("TRN2", target_bir_lowering=False)
    x3 = _mk(nc, "x3", [T, D], F32, "ExternalInput")
    rw = _mk(nc, "rw", [D, E1], F32, "ExternalInput")
    rb = _mk(nc, "rb", [1, 1], F32, "ExternalInput")
    gw = _mk(nc, "gw", [E1, D, 512], F32, "ExternalInput")
    uw = _mk(nc, "uw", [E1, D, 512], F32, "ExternalInput")
    dw = _mk(nc, "dw", [E1, 512, D], F32, "ExternalInput")
    sgw = _mk(nc, "sgw", [D, 2048], F32, "ExternalInput")
    suw = _mk(nc, "suw", [D, 2048], F32, "ExternalInput")
    sdw = _mk(nc, "sdw", [2048, D], F32, "ExternalInput")
    xout = _mk(nc, "xout", [T, D], F32, "ExternalOutput")
    scratch = {
        "l2ln": _mk(nc, "s_l2ln", [T, D], BF16),
        "actT": _mk(nc, "s_actT", [E1, 512, T], BF16),
        "hsT2": _mk(nc, "s_hsT2", [2048, T], BF16),
        "l2y": _mk(nc, "s_l2y", [T, D], F32),
    }
    ch = Chain(nc)
    with ExitStack() as st:
        ch.make_sems(st)
        block = st.enter_context(nc.Block())
        emit_l1_ffn(nc, ch, st, x3, rw, rb, gw, uw, dw, sgw, suw, sdw, xout, scratch)
        ch.emit(block)
    return nc


_BUILD_CACHE = {}


def _get(name, fn):
    if name not in _BUILD_CACHE:
        _BUILD_CACHE[name] = fn()
    return _BUILD_CACHE[name]


def kernel_7launch(**inputs):
    cores = list(range(NCORES))
    x = np.ascontiguousarray(inputs["L0_attn_x"])

    nc1 = _get("l0attn", build_launch_l0attn)
    r1 = run_bass_kernel_spmd(nc1, [
        {"x": x[i], "w_attn": inputs["L0_attn_w_attn"],
         "w_proj": inputs["L0_attn_w_proj"]} for i in cores], core_ids=cores).results
    x1 = [r["x1"] for r in r1]

    nc2 = _get("l0route", build_launch_l0ffn_route)
    r2 = run_bass_kernel_spmd(nc2, [
        {"x1": x1[i], "rw": inputs["L0_ffn_router_weight"],
         "eb": inputs["L0_ffn_expert_bias"]} for i in cores], core_ids=cores).results
    totals_all = np.stack([r["totals"].reshape(E0) for r in r2], 0)

    nc3 = _get("l0scat", build_launch_l0ffn_scatter)
    ins3 = []
    for i in cores:
        pm = np.zeros((NCORES, 1), np.float32)
        pm[:i] = 1.0
        ins3.append({"ln1": r2[i]["ln1"], "idx6": r2[i]["idx6"], "w64": r2[i]["w64"],
                     "totals_all": totals_all, "prevmask": pm})
    r3 = run_bass_kernel_spmd(nc3, ins3, core_ids=cores).results

    nc4 = _get("l0exp", build_launch_l0ffn_experts)
    ins4 = []
    for c in cores:
        parts = np.stack([r3[p]["slotbuf"][c * 8 * CAP:(c + 1) * 8 * CAP]
                          for p in cores], 0)
        ins4.append({"parts": parts, "w1": inputs["L0_ffn_w1"][c * 8:(c + 1) * 8],
                     "w2": inputs["L0_ffn_w2"][c * 8:(c + 1) * 8]})
    r4 = run_bass_kernel_spmd(nc4, ins4, core_ids=cores).results
    ysall = np.concatenate([r["ys"] for r in r4], 0)

    nc5 = _get("l0comb", build_launch_l0ffn_combine)
    ins5 = [{"x1": x1[i], "ln1": r2[i]["ln1"], "gidx": r3[i]["gidx"],
             "ysall": ysall, "sw1": inputs["L0_ffn_sw1"],
             "sw2": inputs["L0_ffn_sw2"]} for i in cores]
    r5 = run_bass_kernel_spmd(nc5, ins5, core_ids=cores).results
    x2 = [r["x2"] for r in r5]

    nc6 = _get("l1attn", build_launch_l1attn)
    r6 = run_bass_kernel_spmd(nc6, [
        {"x2": x2[i], "wqkv": inputs["L1_attn_W_qkv"], "bqkv": inputs["L1_attn_b_qkv"],
         "wout": inputs["L1_attn_W_out"], "bout": inputs["L1_attn_b_out"]}
        for i in cores], core_ids=cores).results
    x3 = [r["x3"] for r in r6]

    nc7 = _get("l1ffn", build_launch_l1ffn)
    r7 = run_bass_kernel_spmd(nc7, [
        {"x3": x3[i], "rw": inputs["L1_ffn_router_w"],
         "rb": np.asarray(inputs["L1_ffn_router_bias"]).reshape(1, 1), "gw": inputs["L1_ffn_gate_w"],
         "uw": inputs["L1_ffn_up_w"], "dw": inputs["L1_ffn_down_w"],
         "sgw": inputs["L1_ffn_s_gate_w"], "suw": inputs["L1_ffn_s_up_w"],
         "sdw": inputs["L1_ffn_s_down_w"]} for i in cores], core_ids=cores).results
    return np.stack([r["xout"] for r in r7], 0)



def emit_l0ffn_scatter_local(nc, ch, st, ln1_dram, idx6_dram, w64_dram,
                             slotbuf_dram, counts_dram, cx_i32=None):
    with ExitStack() as st2:
        iotaE = alloc_sb(nc, st2, "l0s_iotaE", [128, E0])
        colid = alloc_sb(nc, st2, "l0s_colid", [128, 128])
        rowid = alloc_sb(nc, st2, "l0s_rowid", [128, 1])
        stri = alloc_sb(nc, st2, "l0s_stri", [128, 128])
        onesv = alloc_sb(nc, st2, "l0s_onesv", [1, 128])
        ch.op("gpsimd", lambda g: g.iota(iotaE[:], pattern=[[1, E0]], base=0,
                                         channel_multiplier=0,
                                         allow_small_or_imprecise_dtypes=True))
        ch.op("gpsimd", lambda g: g.iota(colid[:], pattern=[[1, 128]], base=0,
                                         channel_multiplier=0,
                                         allow_small_or_imprecise_dtypes=True))
        ch.op("gpsimd", lambda g: g.iota(rowid[:], pattern=[[0, 1]], base=0,
                                         channel_multiplier=1,
                                         allow_small_or_imprecise_dtypes=True))
        ch.op("vector", lambda v: v.tensor_scalar(out=stri[:], in0=colid[:],
                                                  scalar1=rowid[:, 0:1], scalar2=None,
                                                  op0=ALU.is_gt))
        ch.op("vector", lambda v: v.memset(onesv[:], 1.0))

        mi8 = [alloc_sb(nc, st2, f"l0s_mi8{b}", [128, 8], U32) for b in range(2)]
        mif = [alloc_sb(nc, st2, f"l0s_mif{b}", [128, 8]) for b in range(2)]
        w64 = [alloc_sb(nc, st2, f"l0s_w64{b}", [128, E0]) for b in range(2)]
        ohr = [alloc_sb(nc, st2, f"l0s_ohr{b}", [128, E0]) for b in range(2)]
        amat = [alloc_sb(nc, st2, f"l0s_amat{b}", [128, E0]) for b in range(2)]
        tmpe = [alloc_sb(nc, st2, f"l0s_tmpe{b}", [128, E0]) for b in range(2)]
        pspos = [alloc_ps(nc, st2, f"l0s_pspos{b}", [128, E0]) for b in range(2)]
        posf = [alloc_sb(nc, st2, f"l0s_posf{b}", [128, E0]) for b in range(2)]
        lnb = [alloc_sb(nc, st2, f"l0s_lnb{b}", [128, D], BF16) for b in range(2)]
        comb = [alloc_sb(nc, st2, f"l0s_comb{b}", [128, ROWW], BF16) for b in range(2)]
        posr = [alloc_sb(nc, st2, f"l0s_posr{b}", [128, 1]) for b in range(2)]
        wr = [alloc_sb(nc, st2, f"l0s_wr{b}", [128, 1]) for b in range(2)]
        vr = [alloc_sb(nc, st2, f"l0s_vr{b}", [128, 1], U32) for b in range(2)]
        slotf = [alloc_sb(nc, st2, f"l0s_slotf{b}", [128, 1]) for b in range(2)]
        ssel = [alloc_sb(nc, st2, f"l0s_ssel{b}", [128, 1]) for b in range(2)]
        invf = alloc_sb(nc, st2, "l0s_invf", [128, 1])
        slotu = [alloc_sb(nc, st2, f"l0s_slotu{b}", [128, 1], U32) for b in range(2)]
        rbase = alloc_sb(nc, st2, "l0s_rbase", [1, E0])
        psts = alloc_ps(nc, st2, "l0s_psts", [1, E0])
        tsrow = alloc_sb(nc, st2, "l0s_tsrow", [1, E0])
        onesc128 = alloc_sb(nc, st2, "l0s_onesc128", [128, 1])
        ch.op("vector", lambda v: v.memset(onesc128[:], 1.0))
        ch.op("vector", lambda v: v.memset(rbase[:], 0.0))
        ch.op("vector", lambda v: v.memset(comb[0][:], 0.0))
        ch.op("vector", lambda v: v.memset(comb[1][:], 0.0))
        ch.op("vector", lambda v: v.tensor_scalar(
            out=invf[:], in0=rowid[:], scalar1=float(NSLOT), scalar2=None,
            op0=ALU.add))

        def _sc_tile(ti):
            b = ti % 2
            mi8b, mifb, w64b, lnbb = mi8[b], mif[b], w64[b], lnb[b]
            amb, pspb, posfb = amat[b], pspos[b], posf[b]
            r0 = ti * 128
            ch.dma("gpsimd", mi8b[:], idx6_dram[r0:r0 + 128, :])
            ch.dma("gpsimd", w64b[:], w64_dram[r0:r0 + 128, :])
            ch.dma("scalar", lnbb[:], ln1_dram[r0:r0 + 128, :])
            ch.op("vector", lambda v: v.tensor_copy(out=mifb[:], in_=mi8b[:]))
            for r in range(K0):
                if r == 0:
                    ch.op("vector", lambda v, r=r: v.tensor_scalar(
                        out=amb[:], in0=iotaE[:], scalar1=mifb[:, r:r + 1], scalar2=None,
                        op0=ALU.is_equal))
                else:
                    ch.op("vector", lambda v, r=r: v.tensor_scalar(
                        out=ohr[b][:], in0=iotaE[:], scalar1=mifb[:, r:r + 1], scalar2=None,
                        op0=ALU.is_equal))
                    ch.op("vector", lambda v: v.tensor_tensor(out=amb[:], in0=amb[:],
                                                              in1=ohr[b][:], op=ALU.add))
            ch.op("tensor", lambda t: t.matmul(out=pspb[:], lhsT=stri[:], rhs=amb[:],
                                               start=True, stop=False))
            ch.op("tensor", lambda t: t.matmul(out=pspb[:], lhsT=onesv[:], rhs=rbase[:],
                                               start=False, stop=True))
            ch.op("vector", lambda v: v.tensor_copy(out=posfb[:], in_=pspb[:]))
            ch.op("tensor", lambda t: t.matmul(out=psts[:], lhsT=onesc128[:], rhs=amb[:],
                                               start=True, stop=True))
            ch.op("vector", lambda v: v.tensor_copy(out=tsrow[:], in_=psts[:]))
            ch.op("vector", lambda v: v.tensor_tensor(out=rbase[:], in0=rbase[:], in1=tsrow[:],
                                                      op=ALU.add))
            for r in range(K0):
                rb_ = r % 2
                ohrb, tmpb, posrb, wrb = ohr[rb_], tmpe[rb_], posr[rb_], wr[rb_]
                vrb, slotfb, sselb, slotub, cmb = vr[rb_], slotf[rb_], ssel[rb_], slotu[rb_], comb[rb_]
                ch.op("vector", lambda v, r=r: v.tensor_scalar(
                    out=ohrb[:], in0=iotaE[:], scalar1=mifb[:, r:r + 1], scalar2=None,
                    op0=ALU.is_equal))
                ch.op("vector", lambda v: v.tensor_tensor(out=tmpb[:], in0=ohrb[:], in1=posfb[:],
                                                          op=ALU.mult))
                ch.op("vector", lambda v: v.tensor_reduce(out=posrb[:], in_=tmpb[:],
                                                          axis=AX.X, op=ALU.add))
                ch.op("vector", lambda v: v.tensor_tensor(out=tmpb[:], in0=ohrb[:], in1=w64b[:],
                                                          op=ALU.mult))
                ch.op("vector", lambda v: v.tensor_reduce(out=wrb[:], in_=tmpb[:],
                                                          axis=AX.X, op=ALU.add))
                ch.op("vector", lambda v: v.tensor_scalar(
                    out=vrb[:], in0=posrb[:], scalar1=float(CAP), scalar2=None, op0=ALU.is_lt))
                ch.op("vector", lambda v, r=r: v.tensor_scalar(
                    out=slotfb[:], in0=mifb[:, r:r + 1], scalar1=float(CAP),
                    scalar2=posrb[:, 0:1], op0=ALU.mult, op1=ALU.add))
                ch.op("vector", lambda v: v.select(out=sselb[:], mask=vrb[:], on_true=slotfb[:],
                                                   on_false=invf[:], add_drain=True))
                ch.op("vector", lambda v: v.tensor_copy(out=slotub[:], in_=sselb[:]))
                ch.op("vector", lambda v: v.tensor_copy(out=cmb[:, 0:D], in_=lnbb[:]))
                ch.op("vector", lambda v: v.tensor_copy(out=cmb[:, D:D + 1], in_=wrb[:]))
                ch.dma_fn("gpsimd", lambda g: g.indirect_dma_start(
                    out=slotbuf_dram,
                    out_offset=bass.IndirectOffsetOnAxis(ap=slotub[:], axis=0),
                    in_=cmb[:], in_offset=None))

        for ti in range(T // 128):
            _sc_tile(ti)
        cnt64 = alloc_sb(nc, st2, "l0s_cnt64", [1, E0])
        ch.op("vector", lambda v: v.tensor_copy(out=cnt64[:], in_=rbase[:]))
        ch.dma("gpsimd", counts_dram, cnt64[:])
        if cx_i32 is not None:
            ch.op("vector", lambda v: v.tensor_copy(out=cx_i32[:], in_=rbase[:]))


def emit_l0ffn_experts_local(nc, ch, st, slotbuf_dram, w1_dram, w2_dram, ys_dram,
                             scratch):
    NR = NSLOT
    NTile = NR // 128
    xs_dram = scratch["xs"]
    with ExitStack() as st2:
        pb = [alloc_sb(nc, st2, f"l0e_pb{b}", [128, ROWW], BF16) for b in range(2)]
        accb = [alloc_sb(nc, st2, f"l0e_accb{b}", [128, D], BF16) for b in range(2)]
        wcol = alloc_sb(nc, st2, "l0e_wcol", [128, NTile])

        def _stage(s):
            b = s % 2
            r0 = s * 128
            ch.dma("scalar", pb[b][:], slotbuf_dram[r0:r0 + 128, :])
            ch.op("vector", lambda v: v.tensor_copy(out=accb[b][:], in_=pb[b][:, 0:D]))
            ch.op("vector", lambda v, s=s: v.tensor_copy(out=wcol[:, s:s + 1],
                                                         in_=pb[b][:, D:D + 1]))
            ch.dma("gpsimd", xs_dram[r0:r0 + 128, :], accb[b][:])

        for s in range(NTile):
            _stage(s)

        GRP = 8 * CAP
        xsT = [alloc_sb(nc, st2, f"l0e_xsT{c}", [128, GRP], BF16) for c in range(8)]

        w1b = [[alloc_sb(nc, st2, f"l0e_w1b{b}_{c}", [128, 512], BF16) for c in range(8)]
               for b in range(2)]
        w2b = [[alloc_sb(nc, st2, f"l0e_w2b{b}_{c}", [128, D], BF16) for c in range(4)]
               for b in range(2)]
        psh = [alloc_ps(nc, st2, f"l0e_psh{b}", [128, CAP]) for b in range(2)]
        hb = [alloc_sb(nc, st2, f"l0e_hb{i}", [128, CAP], BF16) for i in range(4)]
        gsc = [tuple(alloc_sb(nc, st2, f"l0e_g{n}{b}", [128, CAP])
                     for n in ("a", "b", "c", "d")) for b in range(2)]
        psd = [alloc_ps(nc, st2, f"l0e_psd{b}", [128, 512]) for b in range(2)]
        yb = [alloc_sb(nc, st2, f"l0e_yb{b}", [128, 512], BF16) for b in range(2)]

        def _expert(grp, e):
            eb_ = e % 2
            w1e, w2e = w1b[eb_], w2b[eb_]
            for c in range(8):
                ch.dma("gpsimd", w1e[c][:], w1_dram[e, c * 128:(c + 1) * 128, :])
            for c in range(4):
                ch.dma("gpsimd", w2e[c][:], w2_dram[e, c * 128:(c + 1) * 128, :])
            s0 = e * CAP
            sg0 = e * CAP - grp * GRP
            for ic in range(4):
                ib_ = ic % 2
                ch_psh = psh[ib_]
                for c in range(8):
                    ch.op("tensor", lambda t, c=c, ic=ic, ch_psh=ch_psh: t.matmul(
                        out=ch_psh[:], lhsT=w1e[c][:, ic * 128:(ic + 1) * 128],
                        rhs=xsT[c][:, sg0:sg0 + CAP],
                        start=(c == 0), stop=(c == 7)))
                emit_gelu_tanh(nc, ch, gsc[ib_], ch_psh, hb[ic], "")
            for stile in range(CAP // 128):
                for jc in range(2):
                    db_ = (stile * 2 + jc) % 2
                    for ic in range(4):
                        ch.op("tensor", lambda t, ic=ic, stile=stile, jc=jc, db_=db_: t.matmul(
                            out=psd[db_][:], lhsT=hb[ic][:, stile * 128:(stile + 1) * 128],
                            rhs=w2e[ic][:, jc * 512:(jc + 1) * 512],
                            start=(ic == 0), stop=(ic == 3)))
                    gt = e * (CAP // 128) + stile
                    ch.op("vector", lambda v, gt=gt, db_=db_: v.tensor_scalar(
                        out=yb[db_][:], in0=psd[db_][:], scalar1=wcol[:, gt:gt + 1],
                        scalar2=None, op0=ALU.mult))
                    ch.dma("gpsimd", ys_dram[s0 + stile * 128:s0 + (stile + 1) * 128,
                                             jc * 512:(jc + 1) * 512], yb[db_][:])

        for grp in range(E0 * CAP // GRP):
            for c in range(8):
                ch.dma_transpose(xsT[c][:], xs_dram[grp * GRP:(grp + 1) * GRP,
                                                    c * 128:(c + 1) * 128])
            for e in range(grp * (GRP // CAP), (grp + 1) * (GRP // CAP)):
                _expert(grp, e)


def emit_l0ffn_combine_local(nc, ch, st, x1_dram, idx6_dram, counts_all_dram,
                             prevmask_dram, ysall_dram, yshared_dram, x2_dram,
                             scratch):
    ysbuf = scratch["ysbuf"]
    y_dram = scratch["yffn"]

    with ExitStack() as st2:
        zb = alloc_sb(nc, st2, "l0c_zb", [128, D], BF16)
        ch.op("vector", lambda v: v.memset(zb[:], 0.0))
        ch.dma("scalar", ysbuf[0:NSLOT, :], ysall_dram)
        ch.dma("gpsimd", ysbuf[NSLOT:NSLOT + 1, :], zb[0:1, :])

        pmsk = alloc_sb(nc, st2, "l0c_pmsk", [8, 1])
        tall = alloc_sb(nc, st2, "l0c_tall", [8, E0])
        psbase = alloc_ps(nc, st2, "l0c_psbase", [1, E0])
        baser = alloc_sb(nc, st2, "l0c_baser", [1, E0])
        onesr = alloc_sb(nc, st2, "l0c_onesr", [1, 128])
        psbb = alloc_ps(nc, st2, "l0c_psbb", [128, E0])
        baseB = alloc_sb(nc, st2, "l0c_baseB", [128, E0])
        ch.dma("gpsimd", pmsk[:], prevmask_dram)
        if counts_all_dram.dtype == F32:
            ch.dma("gpsimd", tall[:], counts_all_dram)
        else:
            talli = alloc_sb(nc, st2, "l0c_talli", [8, E0], I32)
            ch.dma("gpsimd", talli[:], counts_all_dram)
            ch.op("vector", lambda v: v.tensor_copy(out=tall[:], in_=talli[:]))
        ch.op("tensor", lambda t: t.matmul(out=psbase[:], lhsT=pmsk[:], rhs=tall[:],
                                           start=True, stop=True))
        ch.op("vector", lambda v: v.tensor_copy(out=baser[:], in_=psbase[:]))
        ch.op("vector", lambda v: v.memset(onesr[:], 1.0))
        ch.op("tensor", lambda t: t.matmul(out=psbb[:], lhsT=onesr[:], rhs=baser[:],
                                           start=True, stop=True))
        ch.op("vector", lambda v: v.tensor_copy(out=baseB[:], in_=psbb[:]))

        iotaE = alloc_sb(nc, st2, "l0c_iotaE", [128, E0])
        colid = alloc_sb(nc, st2, "l0c_colid", [128, 128])
        rowid = alloc_sb(nc, st2, "l0c_rowid", [128, 1])
        stri = alloc_sb(nc, st2, "l0c_stri", [128, 128])
        onesv = alloc_sb(nc, st2, "l0c_onesv", [1, 128])
        ch.op("gpsimd", lambda g: g.iota(iotaE[:], pattern=[[1, E0]], base=0,
                                         channel_multiplier=0,
                                         allow_small_or_imprecise_dtypes=True))
        ch.op("gpsimd", lambda g: g.iota(colid[:], pattern=[[1, 128]], base=0,
                                         channel_multiplier=0,
                                         allow_small_or_imprecise_dtypes=True))
        ch.op("gpsimd", lambda g: g.iota(rowid[:], pattern=[[0, 1]], base=0,
                                         channel_multiplier=1,
                                         allow_small_or_imprecise_dtypes=True))
        ch.op("vector", lambda v: v.tensor_scalar(out=stri[:], in0=colid[:],
                                                  scalar1=rowid[:, 0:1], scalar2=None,
                                                  op0=ALU.is_gt))
        ch.op("vector", lambda v: v.memset(onesv[:], 1.0))

        mi8 = [alloc_sb(nc, st2, f"l0c_mi8{b}", [128, 8], U32) for b in range(2)]
        mif = [alloc_sb(nc, st2, f"l0c_mif{b}", [128, 8]) for b in range(2)]
        ohr = [alloc_sb(nc, st2, f"l0c_ohr{b}", [128, E0]) for b in range(2)]
        amat = [alloc_sb(nc, st2, f"l0c_amat{b}", [128, E0]) for b in range(2)]
        tmpe = [alloc_sb(nc, st2, f"l0c_tmpe{b}", [128, E0]) for b in range(2)]
        pspos = [alloc_ps(nc, st2, f"l0c_pspos{b}", [128, E0]) for b in range(2)]
        posf = [alloc_sb(nc, st2, f"l0c_posf{b}", [128, E0]) for b in range(2)]
        vmat = [alloc_sb(nc, st2, f"l0c_vmat{b}", [128, E0]) for b in range(2)]
        posr = alloc_sb(nc, st2, "l0c_posr", [128, 1])
        vr = alloc_sb(nc, st2, "l0c_vr", [128, 1], U32)
        slotf = alloc_sb(nc, st2, "l0c_slotf", [128, 1])
        gsel = alloc_sb(nc, st2, "l0c_gsel", [128, 1])
        zrow = alloc_sb(nc, st2, "l0c_zrow", [128, 1])
        gslotu = [alloc_sb(nc, st2, f"l0c_gsu{b}", [128, 6], U32) for b in range(2)]
        rbase = alloc_sb(nc, st2, "l0c_rbase", [1, E0])
        psts = alloc_ps(nc, st2, "l0c_psts", [1, E0])
        tsrow = alloc_sb(nc, st2, "l0c_tsrow", [1, E0])
        onesc128 = alloc_sb(nc, st2, "l0c_onesc128", [128, 1])
        racc = [alloc_sb(nc, st2, f"l0c_racc{b}", [128, D]) for b in range(2)]
        ysh = [alloc_sb(nc, st2, f"l0c_ysh{b}", [128, D], BF16) for b in range(2)]
        ycmb = [alloc_sb(nc, st2, f"l0c_ycmb{b}", [128, D]) for b in range(2)]
        ch.op("vector", lambda v: v.memset(onesc128[:], 1.0))
        ch.op("vector", lambda v: v.memset(rbase[:], 0.0))
        ch.op("vector", lambda v: v.memset(zrow[:], float(NSLOT)))

        def _cb_tile(ti, mi8, mif, ohr, amat, tmpe, pspos, posf, vmat, gslotu,
                     racc, ysh, ycmb):
            r0 = ti * 128
            ch.dma("gpsimd", mi8[:], idx6_dram[r0:r0 + 128, :])
            ch.op("vector", lambda v: v.tensor_copy(out=mif[:], in_=mi8[:]))
            for r in range(K0):
                if r == 0:
                    ch.op("vector", lambda v, r=r: v.tensor_scalar(
                        out=amat[:], in0=iotaE[:], scalar1=mif[:, r:r + 1], scalar2=None,
                        op0=ALU.is_equal))
                else:
                    ch.op("vector", lambda v, r=r: v.tensor_scalar(
                        out=ohr[:], in0=iotaE[:], scalar1=mif[:, r:r + 1], scalar2=None,
                        op0=ALU.is_equal))
                    ch.op("vector", lambda v: v.tensor_tensor(out=amat[:], in0=amat[:],
                                                              in1=ohr[:], op=ALU.add))
            ch.op("tensor", lambda t: t.matmul(out=pspos[:], lhsT=stri[:], rhs=amat[:],
                                               start=True, stop=False))
            ch.op("tensor", lambda t: t.matmul(out=pspos[:], lhsT=onesv[:], rhs=rbase[:],
                                               start=False, stop=True))
            ch.op("vector", lambda v: v.tensor_copy(out=posf[:], in_=pspos[:]))
            ch.op("tensor", lambda t: t.matmul(out=psts[:], lhsT=onesc128[:], rhs=amat[:],
                                               start=True, stop=True))
            ch.op("vector", lambda v: v.tensor_copy(out=tsrow[:], in_=psts[:]))
            ch.op("vector", lambda v: v.tensor_tensor(out=rbase[:], in0=rbase[:], in1=tsrow[:],
                                                      op=ALU.add))
            ch.op("vector", lambda v: v.tensor_tensor(out=vmat[:], in0=posf[:], in1=baseB[:],
                                                      op=ALU.add))
            ch.op("vector", lambda v: v.tensor_scalar(out=vmat[:], in0=vmat[:],
                                                      scalar1=float(CAP), scalar2=None,
                                                      op0=ALU.is_lt))
            ch.op("vector", lambda v: v.memset(racc[:], 0.0))
            for r in range(K0):
                ch.op("vector", lambda v, r=r: v.tensor_scalar(
                    out=ohr[:], in0=iotaE[:], scalar1=mif[:, r:r + 1], scalar2=None,
                    op0=ALU.is_equal))
                ch.op("vector", lambda v: v.tensor_tensor(out=tmpe[:], in0=ohr[:], in1=posf[:],
                                                          op=ALU.mult))
                ch.op("vector", lambda v: v.tensor_reduce(out=posr[:], in_=tmpe[:],
                                                          axis=AX.X, op=ALU.add))
                ch.op("vector", lambda v: v.tensor_tensor(out=tmpe[:], in0=ohr[:], in1=vmat[:],
                                                          op=ALU.mult))
                ch.op("vector", lambda v: v.tensor_reduce(out=slotf[:], in_=tmpe[:],
                                                          axis=AX.X, op=ALU.add))
                ch.op("vector", lambda v: v.tensor_copy(out=vr[:], in_=slotf[:]))
                ch.op("vector", lambda v, r=r: v.tensor_scalar(
                    out=slotf[:], in0=mif[:, r:r + 1], scalar1=float(CAP),
                    scalar2=posr[:, 0:1], op0=ALU.mult, op1=ALU.add))
                ch.op("vector", lambda v: v.select(out=gsel[:], mask=vr[:], on_true=slotf[:],
                                                   on_false=zrow[:], add_drain=True))
                ch.op("vector", lambda v, r=r: v.tensor_copy(out=gslotu[:, r:r + 1],
                                                             in_=gsel[:]))
            for r in range(K0):
                ch.dma_fn("gpsimd", lambda g, r=r: g.indirect_dma_start(
                    out=racc[:], out_offset=None, in_=ysbuf,
                    in_offset=bass.IndirectOffsetOnAxis(ap=gslotu[:, r:r + 1], axis=0),
                    compute_op=ALU.add))
            ch.dma("scalar", ysh[:], yshared_dram[r0:r0 + 128, :])
            ch.op("vector", lambda v: v.tensor_tensor(out=ycmb[:], in0=racc[:], in1=ysh[:],
                                                      op=ALU.add))
            ch.dma("gpsimd", y_dram[r0:r0 + 128, :], ycmb[:])

        for ti in range(T // 128):
            b = ti % 2
            _cb_tile(ti, mi8[b], mif[b], ohr[b], amat[b], tmpe[b], pspos[b],
                     posf[b], vmat[b], gslotu[b], racc[b], ysh[b], ycmb[b])

    with ExitStack() as st3:
        emit_ln_tok(nc, ch, st3, "l0c_ln2", y_dram, None, res_add_dram=x1_dram,
                    x_out_dram=x2_dram)


def emit_l0ffn_shared(nc, ch, st, ln1_dram, sw1_dram, sw2_dram, yshared_dram,
                      scratch):
    sw1b_dram = scratch["sw1b"]
    sw2b_dram = scratch["sw2b"]
    hsT_dram = scratch["hsT"]
    with ExitStack() as st2:
        ch.dma("gpsimd", sw1b_dram, sw1_dram)
        ch.dma("gpsimd", sw2b_dram, sw2_dram)
        sw1T = [alloc_sb(nc, st2, f"l0h_sw1T{c}", [128, 2048], BF16) for c in range(8)]
        for c in range(8):
            ch.dma_transpose(sw1T[c][:], sw1b_dram[:, c * 128:(c + 1) * 128])
        lnT = [alloc_sb(nc, st2, f"l0h_lnT{c}", [128, T], BF16) for c in range(8)]
        emit_transpose_to_feat(nc, ch, ln1_dram, lnT)
        psh = [alloc_ps(nc, st2, f"l0h_psh{b}", [128, 512]) for b in range(2)]
        hbf = [alloc_sb(nc, st2, f"l0h_hbf{b}", [128, 512], BF16) for b in range(2)]
        gsc = [tuple(alloc_sb(nc, st2, f"l0h_g{n}{b}", [128, 512])
                     for n in ("a", "b", "c", "d")) for b in range(2)]

        def _mid(it, fc, t):
            b = it % 2
            for c in range(8):
                ch.op("tensor", lambda tt, c=c: tt.matmul(
                    out=psh[b][:], lhsT=sw1T[c][:, fc * 128:(fc + 1) * 128],
                    rhs=lnT[c][:, t * 512:(t + 1) * 512],
                    start=(c == 0), stop=(c == 7)))
            emit_gelu_tanh(nc, ch, gsc[b], psh[b], hbf[b], "")
            ch.dma("gpsimd", hsT_dram[fc * 128:(fc + 1) * 128, t * 512:(t + 1) * 512],
                   hbf[b][:])

        it = 0
        for fc in range(16):
            for t in range(T // 512):
                _mid(it, fc, t)
                it += 1
    with ExitStack() as st2b:
        sw2T = [alloc_sb(nc, st2b, f"l0h_sw2T{c}", [128, D], BF16) for c in range(16)]
        for c in range(16):
            ch.dma_transpose(sw2T[c][:], sw2b_dram[:, c * 128:(c + 1) * 128])
        hstile = [alloc_sb(nc, st2b, f"l0h_hstile{b}", [128, 16, 128], BF16) for b in range(2)]
        psd = [alloc_ps(nc, st2b, f"l0h_psd{b}", [128, 512]) for b in range(2)]
        yw = [alloc_sb(nc, st2b, f"l0h_yw{b}", [128, 512], BF16) for b in range(2)]

        def _down(ti):
            b = ti % 2
            r0 = ti * 128
            ch.dma("scalar", hstile[b][:],
                   hsT_dram[:, r0:r0 + 128].rearrange("(c p) t -> p c t", p=128))
            for jc in range(2):
                jb = (ti * 2 + jc) % 2
                for c in range(16):
                    ch.op("tensor", lambda tt, c=c, jc=jc, jb=jb: tt.matmul(
                        out=psd[jb][:], lhsT=hstile[b][:, c, :],
                        rhs=sw2T[c][:, jc * 512:(jc + 1) * 512],
                        start=(c == 0), stop=(c == 15)))
                ch.op("vector", lambda v, jb=jb: v.tensor_copy(out=yw[jb][:], in_=psd[jb][:]))
                ch.dma("gpsimd", yshared_dram[r0:r0 + 128, jc * 512:(jc + 1) * 512],
                       yw[jb][:])

        for ti in range(T // 128):
            _down(ti)


def emit_l0ffn_scatter_v2(nc, ch, st, ln1_dram, idx6_dram, slotbuf_dram,
                          counts_dram, cx_i32=None, wlab=None):
    with ExitStack() as st2:
        iotaE = alloc_sb(nc, st2, "l2s_iotaE", [128, E0])
        colid = alloc_sb(nc, st2, "l2s_colid", [128, 128])
        rowid = alloc_sb(nc, st2, "l2s_rowid", [128, 1])
        stri = alloc_sb(nc, st2, "l2s_stri", [128, 128])
        onesv = alloc_sb(nc, st2, "l2s_onesv", [1, 128])
        ch.op("gpsimd", lambda g: g.iota(iotaE[:], pattern=[[1, E0]], base=0,
                                         channel_multiplier=0,
                                         allow_small_or_imprecise_dtypes=True))
        ch.op("gpsimd", lambda g: g.iota(colid[:], pattern=[[1, 128]], base=0,
                                         channel_multiplier=0,
                                         allow_small_or_imprecise_dtypes=True))
        ch.op("gpsimd", lambda g: g.iota(rowid[:], pattern=[[0, 1]], base=0,
                                         channel_multiplier=1,
                                         allow_small_or_imprecise_dtypes=True))
        ch.op("vector", lambda v: v.tensor_scalar(out=stri[:], in0=colid[:],
                                                  scalar1=rowid[:, 0:1], scalar2=None,
                                                  op0=ALU.is_gt))
        ch.op("vector", lambda v: v.memset(onesv[:], 1.0))

        mi8 = [alloc_sb(nc, st2, f"l2s_mi8{b}", [128, 8], U32) for b in range(2)]
        mif = [alloc_sb(nc, st2, f"l2s_mif{b}", [128, 8]) for b in range(2)]
        ohr = [alloc_sb(nc, st2, f"l2s_ohr{b}", [128, E0]) for b in range(2)]
        amat = [alloc_sb(nc, st2, f"l2s_amat{b}", [128, E0]) for b in range(2)]
        tmpe = [alloc_sb(nc, st2, f"l2s_tmpe{b}", [128, E0]) for b in range(2)]
        pspos = [alloc_ps(nc, st2, f"l2s_pspos{b}", [128, E0]) for b in range(2)]
        posf = [alloc_sb(nc, st2, f"l2s_posf{b}", [128, E0]) for b in range(2)]
        lnb = [alloc_sb(nc, st2, f"l2s_lnb{b}", [128, D], BF16) for b in range(2)]
        posr = [alloc_sb(nc, st2, f"l2s_posr{b}", [128, 1]) for b in range(2)]
        vr = [alloc_sb(nc, st2, f"l2s_vr{b}", [128, 1], U32) for b in range(2)]
        slotf = [alloc_sb(nc, st2, f"l2s_slotf{b}", [128, 1]) for b in range(2)]
        ssel = [alloc_sb(nc, st2, f"l2s_ssel{b}", [128, 1]) for b in range(2)]
        invf = alloc_sb(nc, st2, "l2s_invf", [128, 1])
        slotu6 = [alloc_sb(nc, st2, f"l2s_slotu6{b}", [128, K0], U32) for b in range(2)]
        rbase = alloc_sb(nc, st2, "l2s_rbase", [1, E0])
        psts = alloc_ps(nc, st2, "l2s_psts", [1, E0])
        tsrow = alloc_sb(nc, st2, "l2s_tsrow", [1, E0])
        onesc128 = alloc_sb(nc, st2, "l2s_onesc128", [128, 1])
        ch.op("vector", lambda v: v.memset(onesc128[:], 1.0))
        ch.op("vector", lambda v: v.memset(rbase[:], 0.0))
        ch.op("vector", lambda v: v.tensor_scalar(
            out=invf[:], in0=rowid[:], scalar1=float(NSLOT), scalar2=None,
            op0=ALU.add))

        def _sc_tile(ti):
            b = ti % 2
            mi8b, mifb, lnbb = mi8[b], mif[b], lnb[b]
            amb, pspb, posfb = amat[b], pspos[b], posf[b]
            r0 = ti * 128
            ch.dma("gpsimd", mi8b[:], idx6_dram[r0:r0 + 128, :])
            ch.dma("scalar", lnbb[:], ln1_dram[r0:r0 + 128, :])
            ch.op("vector", lambda v: v.tensor_copy(out=mifb[:], in_=mi8b[:]))
            for r in range(K0):
                if r == 0:
                    ch.op("vector", lambda v, r=r: v.tensor_scalar(
                        out=amb[:], in0=iotaE[:], scalar1=mifb[:, r:r + 1], scalar2=None,
                        op0=ALU.is_equal))
                else:
                    ch.op("vector", lambda v, r=r: v.tensor_scalar(
                        out=ohr[b][:], in0=iotaE[:], scalar1=mifb[:, r:r + 1], scalar2=None,
                        op0=ALU.is_equal))
                    ch.op("vector", lambda v: v.tensor_tensor(out=amb[:], in0=amb[:],
                                                              in1=ohr[b][:], op=ALU.add))
            ch.op("tensor", lambda t: t.matmul(out=pspb[:], lhsT=stri[:], rhs=amb[:],
                                               start=True, stop=False))
            ch.op("tensor", lambda t: t.matmul(out=pspb[:], lhsT=onesv[:], rhs=rbase[:],
                                               start=False, stop=True))
            ch.op("vector", lambda v: v.tensor_copy(out=posfb[:], in_=pspb[:]))
            ch.op("tensor", lambda t: t.matmul(out=psts[:], lhsT=onesc128[:], rhs=amb[:],
                                               start=True, stop=True))
            ch.op("vector", lambda v: v.tensor_copy(out=tsrow[:], in_=psts[:]))
            ch.op("vector", lambda v: v.tensor_tensor(out=rbase[:], in0=rbase[:], in1=tsrow[:],
                                                      op=ALU.add))
            for r in range(K0):
                rb_ = r % 2
                ohrb, tmpb, posrb = ohr[rb_], tmpe[rb_], posr[rb_]
                vrb, slotfb, sselb = vr[rb_], slotf[rb_], ssel[rb_]
                ch.op("vector", lambda v, r=r: v.tensor_scalar(
                    out=ohrb[:], in0=iotaE[:], scalar1=mifb[:, r:r + 1], scalar2=None,
                    op0=ALU.is_equal))
                ch.op("vector", lambda v: v.tensor_tensor(out=tmpb[:], in0=ohrb[:], in1=posfb[:],
                                                          op=ALU.mult))
                ch.op("vector", lambda v: v.tensor_reduce(out=posrb[:], in_=tmpb[:],
                                                          axis=AX.X, op=ALU.add))
                ch.op("vector", lambda v: v.tensor_scalar(
                    out=vrb[:], in0=posrb[:], scalar1=float(CAP), scalar2=None, op0=ALU.is_lt))
                ch.op("vector", lambda v, r=r: v.tensor_scalar(
                    out=slotfb[:], in0=mifb[:, r:r + 1], scalar1=float(CAP),
                    scalar2=posrb[:, 0:1], op0=ALU.mult, op1=ALU.add))
                ch.op("vector", lambda v: v.select(out=sselb[:], mask=vrb[:], on_true=slotfb[:],
                                                   on_false=invf[:], add_drain=True))
                ch.op("vector", lambda v, r=r: v.tensor_copy(out=slotu6[b][:, r:r + 1],
                                                             in_=sselb[:]))
                kl = wlab[0]
                wlab[0] += 1
                ch._record(
                    "gpsimd",
                    lambda g, sl=slotu6[b], r=r, src=lnbb: g.indirect_dma_start(
                        out=slotbuf_dram,
                        out_offset=bass.IndirectOffsetOnAxis(ap=sl[:, r:r + 1], axis=0),
                        in_=src[:], in_offset=None),
                    "gd",
                    ch._norm([slotu6[b][:, r:r + 1], lnbb[:]]),
                    ch._norm([slotbuf_dram[kl:kl + 1, :]]))

        for ti in range(T // 128):
            _sc_tile(ti)
        cnt64 = alloc_sb(nc, st2, "l2s_cnt64", [1, E0])
        ch.op("vector", lambda v: v.tensor_copy(out=cnt64[:], in_=rbase[:]))
        ch.dma("gpsimd", counts_dram, cnt64[:])
        if cx_i32 is not None:
            ch.op("vector", lambda v: v.tensor_copy(out=cx_i32[:], in_=rbase[:]))


def emit_l0ffn_experts_v2(nc, ch, st, slotbuf_dram, w1_dram, w2_dram, ysbuf_dram):
    with ExitStack() as st2:
        GRP = 8 * CAP
        xsT = [alloc_sb(nc, st2, f"l2e_xsT{c}", [128, GRP], BF16) for c in range(8)]
        w1b = [alloc_sb(nc, st2, f"l2e_w1b{b}", [128, 8 * 512], BF16) for b in range(2)]
        w2b = [alloc_sb(nc, st2, f"l2e_w2b{b}", [128, 4 * D], BF16) for b in range(2)]
        w1s = [alloc_sb(nc, st2, f"l2e_w1s{b}", [128, 8 * 512]) for b in range(2)]
        w2s = [alloc_sb(nc, st2, f"l2e_w2s{b}", [128, 4 * D]) for b in range(2)]
        psh = [alloc_ps(nc, st2, f"l2e_psh{b}", [128, CAP]) for b in range(4)]
        hb = [alloc_sb(nc, st2, f"l2e_hb{i}", [128, CAP], BF16) for i in range(4)]
        gsc = [tuple(alloc_sb(nc, st2, f"l2e_g{n}{b}", [128, CAP])
                     for n in ("a", "b", "c", "d")) for b in range(2)]
        psd = [alloc_ps(nc, st2, f"l2e_psd{b}", [128, 512]) for b in range(4)]
        yb = [alloc_sb(nc, st2, f"l2e_yb{b}", [128, 512], BF16) for b in range(4)]

        def _expert(grp, e):
            eb_ = e % 2
            w1e, w2e = w1b[eb_], w2b[eb_]
            w1f, w2f = w1s[eb_], w2s[eb_]
            ch.dma("sync", w1f[:].rearrange("p (c f) -> p c f", c=8),
                   w1_dram[e].rearrange("(c p) f -> p c f", p=128))
            ch.dma("scalar", w2f[:].rearrange("p (c f) -> p c f", c=4),
                   w2_dram[e].rearrange("(c p) f -> p c f", p=128))
            ch.op("vector", lambda v, w1e=w1e, w1f=w1f: v.tensor_copy(out=w1e[:], in_=w1f[:]))
            ch.op("scalar", lambda s, w2e=w2e, w2f=w2f: s.copy(out=w2e[:], in_=w2f[:]))
            sg0 = e * CAP - grp * GRP
            for ic in range(4):
                ib_ = ic
                ch_psh = psh[ib_]
                for c in range(8):
                    ch.op("tensor", lambda t, c=c, ic=ic, ch_psh=ch_psh: t.matmul(
                        out=ch_psh[:],
                        lhsT=w1e[:, c * 512 + ic * 128:c * 512 + (ic + 1) * 128],
                        rhs=xsT[c][:, sg0:sg0 + CAP],
                        start=(c == 0), stop=(c == 7)))
                emit_gelu_tanh(nc, ch, gsc[ic % 2], ch_psh, hb[ic], "")
            s0 = e * CAP
            for stile in range(CAP // 128):
                for jc in range(2):
                    db_ = (stile * 2 + jc) % 4
                    for ic in range(4):
                        ch.op("tensor", lambda t, ic=ic, stile=stile, jc=jc, db_=db_: t.matmul(
                            out=psd[db_][:], lhsT=hb[ic][:, stile * 128:(stile + 1) * 128],
                            rhs=w2e[:, ic * D + jc * 512:ic * D + (jc + 1) * 512],
                            start=(ic == 0), stop=(ic == 3)))
                    ch.op("scalar", lambda s, db_=db_: s.copy(out=yb[db_][:], in_=psd[db_][:]))
                    ch.dma("gpsimd", ysbuf_dram[s0 + stile * 128:s0 + (stile + 1) * 128,
                                                jc * 512:(jc + 1) * 512], yb[db_][:])

        for grp in range(E0 * CAP // GRP):
            for c in range(8):
                ch._record(
                    "sync",
                    lambda e, c=c, grp=grp: e.dma_start_transpose(
                        xsT[c][:],
                        slotbuf_dram[grp * GRP:(grp + 1) * GRP, c * 128:(c + 1) * 128]),
                    "sd",
                    ch._norm([slotbuf_dram]),
                    ch._norm([xsT[c][:]]))
            for e in range(grp * (GRP // CAP), (grp + 1) * (GRP // CAP)):
                _expert(grp, e)


def emit_l0ffn_combine_v2(nc, ch, st, x1_dram, idx6_dram, w64_dram, counts_all_dram,
                          prevmask_dram, ysbuf_dram, yshared_dram, x2_dram, scratch):
    y_dram = scratch["yffn"]

    with ExitStack() as st2:
        zb = alloc_sb(nc, st2, "l2c_zb", [128, D], BF16)
        ch.op("vector", lambda v: v.memset(zb[:], 0.0))
        ch.dma("gpsimd", ysbuf_dram[NSLOT:NSLOT + 1, :], zb[0:1, :])

        pmsk = alloc_sb(nc, st2, "l2c_pmsk", [8, 1])
        tall = alloc_sb(nc, st2, "l2c_tall", [8, E0])
        psbase = alloc_ps(nc, st2, "l2c_psbase", [1, E0])
        baser = alloc_sb(nc, st2, "l2c_baser", [1, E0])
        onesr = alloc_sb(nc, st2, "l2c_onesr", [1, 128])
        psbb = alloc_ps(nc, st2, "l2c_psbb", [128, E0])
        baseB = alloc_sb(nc, st2, "l2c_baseB", [128, E0])
        ch.dma("gpsimd", pmsk[:], prevmask_dram)
        talli = alloc_sb(nc, st2, "l2c_talli", [8, E0], I32)
        ch.dma("gpsimd", talli[:], counts_all_dram)
        ch.op("vector", lambda v: v.tensor_copy(out=tall[:], in_=talli[:]))
        ch.op("tensor", lambda t: t.matmul(out=psbase[:], lhsT=pmsk[:], rhs=tall[:],
                                           start=True, stop=True))
        ch.op("vector", lambda v: v.tensor_copy(out=baser[:], in_=psbase[:]))
        ch.op("vector", lambda v: v.memset(onesr[:], 1.0))
        ch.op("tensor", lambda t: t.matmul(out=psbb[:], lhsT=onesr[:], rhs=baser[:],
                                           start=True, stop=True))
        ch.op("vector", lambda v: v.tensor_copy(out=baseB[:], in_=psbb[:]))

        iotaE = alloc_sb(nc, st2, "l2c_iotaE", [128, E0])
        colid = alloc_sb(nc, st2, "l2c_colid", [128, 128])
        rowid = alloc_sb(nc, st2, "l2c_rowid", [128, 1])
        stri = alloc_sb(nc, st2, "l2c_stri", [128, 128])
        onesv = alloc_sb(nc, st2, "l2c_onesv", [1, 128])
        ch.op("gpsimd", lambda g: g.iota(iotaE[:], pattern=[[1, E0]], base=0,
                                         channel_multiplier=0,
                                         allow_small_or_imprecise_dtypes=True))
        ch.op("gpsimd", lambda g: g.iota(colid[:], pattern=[[1, 128]], base=0,
                                         channel_multiplier=0,
                                         allow_small_or_imprecise_dtypes=True))
        ch.op("gpsimd", lambda g: g.iota(rowid[:], pattern=[[0, 1]], base=0,
                                         channel_multiplier=1,
                                         allow_small_or_imprecise_dtypes=True))
        ch.op("vector", lambda v: v.tensor_scalar(out=stri[:], in0=colid[:],
                                                  scalar1=rowid[:, 0:1], scalar2=None,
                                                  op0=ALU.is_gt))
        ch.op("vector", lambda v: v.memset(onesv[:], 1.0))

        mi8 = [alloc_sb(nc, st2, f"l2c_mi8{b}", [128, 8], U32) for b in range(2)]
        mif = [alloc_sb(nc, st2, f"l2c_mif{b}", [128, 8]) for b in range(2)]
        ohr = [alloc_sb(nc, st2, f"l2c_ohr{b}", [128, E0]) for b in range(2)]
        amat = [alloc_sb(nc, st2, f"l2c_amat{b}", [128, E0]) for b in range(2)]
        tmpe = [alloc_sb(nc, st2, f"l2c_tmpe{b}", [128, E0]) for b in range(2)]
        pspos = [alloc_ps(nc, st2, f"l2c_pspos{b}", [128, E0]) for b in range(2)]
        posf = [alloc_sb(nc, st2, f"l2c_posf{b}", [128, E0]) for b in range(2)]
        vmat = [alloc_sb(nc, st2, f"l2c_vmat{b}", [128, E0]) for b in range(2)]
        w64b = [alloc_sb(nc, st2, f"l2c_w64b{b}", [128, E0]) for b in range(2)]
        wrb = [alloc_sb(nc, st2, f"l2c_wrb{b}", [128, K0]) for b in range(2)]
        posr = alloc_sb(nc, st2, "l2c_posr", [128, 1])
        vr = alloc_sb(nc, st2, "l2c_vr", [128, 1], U32)
        slotf = alloc_sb(nc, st2, "l2c_slotf", [128, 1])
        gsel = alloc_sb(nc, st2, "l2c_gsel", [128, 1])
        zrow = alloc_sb(nc, st2, "l2c_zrow", [128, 1])
        gslotu = [alloc_sb(nc, st2, f"l2c_gsu{b}", [128, K0], U32) for b in range(2)]
        gbuf = [[alloc_sb(nc, st2, f"l2c_gb{b}_{r}", [128, D], BF16) for r in range(K0)]
                for b in range(2)]
        ohrg = [alloc_sb(nc, st2, f"l2c_ohrg{b}", [128, E0]) for b in range(2)]
        tmpg = [alloc_sb(nc, st2, f"l2c_tmpg{b}", [128, E0]) for b in range(2)]
        racc2 = [alloc_sb(nc, st2, f"l2c_racc2{b}", [128, D]) for b in range(2)]
        rbase = alloc_sb(nc, st2, "l2c_rbase", [1, E0])
        psts = alloc_ps(nc, st2, "l2c_psts", [1, E0])
        tsrow = alloc_sb(nc, st2, "l2c_tsrow", [1, E0])
        onesc128 = alloc_sb(nc, st2, "l2c_onesc128", [128, 1])
        racc = [alloc_sb(nc, st2, f"l2c_racc{b}", [128, D]) for b in range(2)]
        ysh = [alloc_sb(nc, st2, f"l2c_ysh{b}", [128, D], BF16) for b in range(2)]
        ycmb = [alloc_sb(nc, st2, f"l2c_ycmb{b}", [128, D]) for b in range(2)]
        ch.op("vector", lambda v: v.memset(onesc128[:], 1.0))
        ch.op("vector", lambda v: v.memset(rbase[:], 0.0))
        ch.op("vector", lambda v: v.memset(zrow[:], float(NSLOT)))

        GQ = ("gpsimd",) * K0

        def _cb_tile(ti, mi8, mif, ohr, amat, tmpe, pspos, posf, vmat, w64t,
                     wrt, gslotu, gbufs, racc, ysh, ycmb):
            bb = ti % 2
            r0 = ti * 128
            ch.dma("gpsimd", mi8[:], idx6_dram[r0:r0 + 128, :])
            ch.dma("gpsimd", w64t[:], w64_dram[r0:r0 + 128, :])
            ch.op("vector", lambda v: v.tensor_copy(out=mif[:], in_=mi8[:]))
            for r in range(K0):
                if r == 0:
                    ch.op("vector", lambda v, r=r: v.tensor_scalar(
                        out=amat[:], in0=iotaE[:], scalar1=mif[:, r:r + 1], scalar2=None,
                        op0=ALU.is_equal))
                else:
                    ch.op("vector", lambda v, r=r: v.tensor_scalar(
                        out=ohr[:], in0=iotaE[:], scalar1=mif[:, r:r + 1], scalar2=None,
                        op0=ALU.is_equal))
                    ch.op("vector", lambda v: v.tensor_tensor(out=amat[:], in0=amat[:],
                                                              in1=ohr[:], op=ALU.add))
            ch.op("tensor", lambda t: t.matmul(out=pspos[:], lhsT=stri[:], rhs=amat[:],
                                               start=True, stop=False))
            ch.op("tensor", lambda t: t.matmul(out=pspos[:], lhsT=onesv[:], rhs=rbase[:],
                                               start=False, stop=True))
            ch.op("vector", lambda v: v.tensor_copy(out=posf[:], in_=pspos[:]))
            ch.op("tensor", lambda t: t.matmul(out=psts[:], lhsT=onesc128[:], rhs=amat[:],
                                               start=True, stop=True))
            ch.op("vector", lambda v: v.tensor_copy(out=tsrow[:], in_=psts[:]))
            ch.op("vector", lambda v: v.tensor_tensor(out=rbase[:], in0=rbase[:], in1=tsrow[:],
                                                      op=ALU.add))
            ch.op("vector", lambda v: v.tensor_tensor(out=vmat[:], in0=posf[:], in1=baseB[:],
                                                      op=ALU.add))
            ch.op("vector", lambda v: v.tensor_scalar(out=vmat[:], in0=vmat[:],
                                                      scalar1=float(CAP), scalar2=None,
                                                      op0=ALU.is_lt))
            for r in range(K0):
                ch.op("vector", lambda v, r=r: v.tensor_scalar(
                    out=ohr[:], in0=iotaE[:], scalar1=mif[:, r:r + 1], scalar2=None,
                    op0=ALU.is_equal))
                ch.op("vector", lambda v: v.tensor_tensor(out=tmpe[:], in0=ohr[:], in1=posf[:],
                                                          op=ALU.mult))
                ch.op("vector", lambda v: v.tensor_reduce(out=posr[:], in_=tmpe[:],
                                                          axis=AX.X, op=ALU.add))
                ch.op("vector", lambda v: v.tensor_tensor(out=tmpe[:], in0=ohr[:], in1=vmat[:],
                                                          op=ALU.mult))
                ch.op("vector", lambda v: v.tensor_reduce(out=slotf[:], in_=tmpe[:],
                                                          axis=AX.X, op=ALU.add))
                ch.op("vector", lambda v: v.tensor_copy(out=vr[:], in_=slotf[:]))
                ch.op("vector", lambda v, r=r: v.tensor_scalar(
                    out=slotf[:], in0=mif[:, r:r + 1], scalar1=float(CAP),
                    scalar2=posr[:, 0:1], op0=ALU.mult, op1=ALU.add))
                ch.op("vector", lambda v: v.select(out=gsel[:], mask=vr[:], on_true=slotf[:],
                                                   on_false=zrow[:], add_drain=True))
                ch.op("vector", lambda v, r=r: v.tensor_copy(out=gslotu[:, r:r + 1],
                                                             in_=gsel[:]))
                ch.op("vector", lambda v: v.tensor_tensor(out=tmpe[:], in0=ohr[:], in1=w64t[:],
                                                          op=ALU.mult))
                ch.op("vector", lambda v, r=r: v.tensor_reduce(out=wrt[:, r:r + 1], in_=tmpe[:],
                                                               axis=AX.X, op=ALU.add))
            for r in range(K0):
                ch.dma_fn(GQ[r], lambda g, r=r: g.indirect_dma_start(
                    out=gbufs[r][:], out_offset=None, in_=ysbuf_dram,
                    in_offset=bass.IndirectOffsetOnAxis(ap=gslotu[:, r:r + 1], axis=0)))
            ch.dma("scalar", ysh[:], yshared_dram[r0:r0 + 128, :])
            ch.op("vector", lambda v: v.tensor_scalar(
                out=racc[:], in0=gbufs[0][:], scalar1=wrt[:, 0:1], scalar2=None,
                op0=ALU.mult))
            for r in range(1, K0):
                ch.op("vector", lambda v, r=r: v.scalar_tensor_tensor(
                    out=racc[:], in0=gbufs[r][:], scalar=wrt[:, r:r + 1],
                    in1=racc[:], op0=ALU.mult, op1=ALU.add))
            ch.op("vector", lambda v: v.tensor_tensor(out=ycmb[:], in0=racc[:], in1=ysh[:],
                                                      op=ALU.add))
            ch.dma("sync", y_dram[r0:r0 + 128, :], ycmb[:])

        for ti in range(T // 128):
            b = ti % 2
            _cb_tile(ti, mi8[b], mif[b], ohr[b], amat[b], tmpe[b], pspos[b],
                     posf[b], vmat[b], w64b[b], wrb[b], gslotu[b], gbuf[b],
                     racc[b], ysh[b], ycmb[b])

    with ExitStack() as st3:
        emit_ln_tok(nc, ch, st3, "l2c_ln2", y_dram, None, res_add_dram=x1_dram,
                    x_out_dram=x2_dram)


def build_launch_A():
    nc = bass.Bass("TRN2", target_bir_lowering=False)
    x_in = _mk(nc, "x", [T, D], F32, "ExternalInput")
    w_attn = _mk(nc, "w_attn", [D, 3 * D], F32, "ExternalInput")
    w_proj = _mk(nc, "w_proj", [D, D], F32, "ExternalInput")
    rw = _mk(nc, "rw", [E0, D], F32, "ExternalInput")
    eb = _mk(nc, "eb", [E0], F32, "ExternalInput")
    w1 = _mk(nc, "w1", [E0, D, 512], F32, "ExternalInput")
    w2 = _mk(nc, "w2", [E0, 512, D], F32, "ExternalInput")
    sw1 = _mk(nc, "sw1", [2048, D], F32, "ExternalInput")
    sw2 = _mk(nc, "sw2", [D, 2048], F32, "ExternalInput")
    x1 = _mk(nc, "x1", [T, D], F32, "ExternalOutput")
    idx6 = _mk(nc, "idx6", [T, 8], U32, "ExternalOutput")
    counts = _mk(nc, "counts", [1, E0], F32, "ExternalOutput")
    ys = _mk(nc, "ys", [NSLOT, D], BF16, "ExternalOutput")
    yshared = _mk(nc, "yshared", [T, D], BF16, "ExternalOutput")

    scratch = {
        "ln": _mk(nc, "s_ln", [T, D], BF16),
        "qkT": _mk(nc, "s_qkT", [2 * D, T], BF16),
        "v": _mk(nc, "s_v", [T // 128, 16, 128, 64], BF16),
        "oT": _mk(nc, "s_oT", [D, T], BF16),
        "y": _mk(nc, "s_y", [T, D], F32),
        "nb": _mk(nc, "s_nb", [2, 2, 8], F32),
        "rwb": _mk(nc, "s_rwb", [E0, D], BF16),
        "xs": _mk(nc, "s_xs", [NSLOT, D], BF16),
        "sw1b": _mk(nc, "s_sw1b", [2048, D], BF16),
        "sw2b": _mk(nc, "s_sw2b", [D, 2048], BF16),
        "hsT": _mk(nc, "s_hsT", [2048, T], BF16),
    }
    ln1 = _mk(nc, "s_ln1", [T, D], BF16)
    w64 = _mk(nc, "s_w64", [T, E0], F32)
    totals_junk = _mk(nc, "s_totjunk", [E0, 1], F32)
    slotbuf = _mk(nc, "s_slotbuf", [NSLOT + 128, ROWW], BF16)

    ch = Chain(nc)
    with ExitStack() as st:
        ch.make_sems(st)
        block = st.enter_context(nc.Block())
        emit_l0_attn(nc, ch, st, x_in, w_attn, w_proj, x1, scratch)
        emit_l0ffn_routing(nc, ch, st, x1, rw, eb, ln1, idx6, w64, totals_junk, scratch)
        zt = alloc_sb(nc, st, "lA_zt", [128, ROWW], BF16)
        ch.op("vector", lambda v: v.memset(zt[:], 0.0))
        nrep = NSLOT // 128 + 1
        half = (nrep + 1) // 2
        for hh in range(2):
            lo, hi = hh * half, min((hh + 1) * half, nrep)
            ch.dma("gpsimd", slotbuf[lo * 128:hi * 128, :].rearrange("(n p) w -> p n w", p=128),
                   zt[:].unsqueeze(1).broadcast_to((128, hi - lo, ROWW)))
        emit_l0ffn_scatter_local(nc, ch, st, ln1, idx6, w64, slotbuf, counts)
        emit_l0ffn_experts_local(nc, ch, st, slotbuf, w1, w2, ys, scratch)
        emit_l0ffn_shared(nc, ch, st, ln1, sw1, sw2, yshared, scratch)
        ch.emit(block)
    return nc


def build_launch_B():
    nc = bass.Bass("TRN2", target_bir_lowering=False)
    x1 = _mk(nc, "x1", [T, D], F32, "ExternalInput")
    idx6 = _mk(nc, "idx6", [T, 8], U32, "ExternalInput")
    counts_all = _mk(nc, "counts_all", [NCORES, E0], F32, "ExternalInput")
    prevmask = _mk(nc, "prevmask", [NCORES, 1], F32, "ExternalInput")
    ysall = _mk(nc, "ysall", [NSLOT, D], BF16, "ExternalInput")
    yshared = _mk(nc, "yshared", [T, D], BF16, "ExternalInput")
    wqkv = _mk(nc, "wqkv", [D, 3 * D], F32, "ExternalInput")
    bqkv = _mk(nc, "bqkv", [3 * D], F32, "ExternalInput")
    wout = _mk(nc, "wout", [D, D], F32, "ExternalInput")
    bout = _mk(nc, "bout", [D], F32, "ExternalInput")
    rw1 = _mk(nc, "rw1", [D, E1], F32, "ExternalInput")
    rb1 = _mk(nc, "rb1", [1, 1], F32, "ExternalInput")
    gw = _mk(nc, "gw", [E1, D, 512], F32, "ExternalInput")
    uw = _mk(nc, "uw", [E1, D, 512], F32, "ExternalInput")
    dw = _mk(nc, "dw", [E1, 512, D], F32, "ExternalInput")
    sgw = _mk(nc, "sgw", [D, 2048], F32, "ExternalInput")
    suw = _mk(nc, "suw", [D, 2048], F32, "ExternalInput")
    sdw = _mk(nc, "sdw", [2048, D], F32, "ExternalInput")
    xout = _mk(nc, "xout", [T, D], F32, "ExternalOutput")

    x2 = _mk(nc, "s_x2", [T, D], F32)
    x3 = _mk(nc, "s_x3", [T, D], F32)
    scratch = {
        "ysbuf": _mk(nc, "s_ysbuf", [NSLOT + 1, D], BF16),
        "yffn": _mk(nc, "s_yffn", [T, D], F32),
        "l1ln": _mk(nc, "s_l1ln", [T, D], BF16),
        "pqT": _mk(nc, "s_pqT", [D, T], BF16),
        "pkT": _mk(nc, "s_pkT", [D, T], BF16),
        "ktok": _mk(nc, "s_ktok", [T, D], BF16),
        "l1v": _mk(nc, "s_l1v", [T // 128, 16, 128, 64], BF16),
        "l1oT": _mk(nc, "s_l1oT", [D, T], BF16),
        "l1y": _mk(nc, "s_l1y", [T, D], F32),
        "l2ln": _mk(nc, "s_l2ln", [T, D], BF16),
        "actT": _mk(nc, "s_actT", [E1, 512, T], BF16),
        "hsT2": _mk(nc, "s_hsT2", [2048, T], BF16),
        "l2y": _mk(nc, "s_l2y", [T, D], F32),
    }
    ch = Chain(nc)
    with ExitStack() as st:
        ch.make_sems(st)
        block = st.enter_context(nc.Block())
        emit_l0ffn_combine_local(nc, ch, st, x1, idx6, counts_all, prevmask,
                                 ysall, yshared, x2, scratch)
        emit_l1_attn(nc, ch, st, x2, wqkv, bqkv, wout, bout, x3, scratch)
        emit_l1_ffn(nc, ch, st, x3, rw1, rb1, gw, uw, dw, sgw, suw, sdw, xout, scratch)
        ch.emit(block)
    return nc


def kernel_2launch(**inputs):
    cores = list(range(NCORES))
    x = np.ascontiguousarray(inputs["L0_attn_x"])
    ncA = _get("launchA", build_launch_A)
    rA = run_bass_kernel_spmd(ncA, [
        {"x": x[i], "w_attn": inputs["L0_attn_w_attn"], "w_proj": inputs["L0_attn_w_proj"],
         "rw": inputs["L0_ffn_router_weight"], "eb": inputs["L0_ffn_expert_bias"],
         "w1": inputs["L0_ffn_w1"], "w2": inputs["L0_ffn_w2"],
         "sw1": inputs["L0_ffn_sw1"], "sw2": inputs["L0_ffn_sw2"]} for i in cores],
        core_ids=cores).results
    counts_all = np.concatenate([r["counts"] for r in rA], 0)

    ncB = _get("launchB", build_launch_B)
    insB = []
    for i in cores:
        pm = np.zeros((NCORES, 1), np.float32)
        pm[:i] = 1.0
        insB.append({
            "x1": rA[i]["x1"], "idx6": rA[i]["idx6"], "counts_all": counts_all,
            "prevmask": pm, "ysall": rA[i]["ys"], "yshared": rA[i]["yshared"],
            "wqkv": inputs["L1_attn_W_qkv"], "bqkv": inputs["L1_attn_b_qkv"],
            "wout": inputs["L1_attn_W_out"], "bout": inputs["L1_attn_b_out"],
            "rw1": inputs["L1_ffn_router_w"],
            "rb1": np.asarray(inputs["L1_ffn_router_bias"]).reshape(1, 1),
            "gw": inputs["L1_ffn_gate_w"], "uw": inputs["L1_ffn_up_w"],
            "dw": inputs["L1_ffn_down_w"], "sgw": inputs["L1_ffn_s_gate_w"],
            "suw": inputs["L1_ffn_s_up_w"], "sdw": inputs["L1_ffn_s_down_w"]})
    rB = run_bass_kernel_spmd(ncB, insB, core_ids=cores).results
    return np.stack([r["xout"] for r in rB], 0)


TPB_BASES = [0x00002000000000, 0x00003000000000, 0x00006000000000, 0x00007000000000,
             0x00802000000000, 0x00803000000000, 0x00806000000000, 0x00807000000000]
I32 = mybir.dt.int32


def emit_exchange_bcast(nc, g, cx, crecv, flags, seedt, met):
    crecv_off = nc.lookup_mloc(crecv).addr
    flags_off = nc.lookup_mloc(flags).addr
    me_v = g.value_load(met[0:1, 0:1])
    seed_r = g.alloc_register(name="xseed")
    g.reg_load(seed_r, seedt[0:1, 0:1])
    val_r = g.alloc_register(name="xval")
    addr = g.alloc_register64(name="xaddr")
    for slot in g.Switch(me_v, 8):
        for e in range(E0):
            g.reg_load(val_r, cx[0:1, e:e + 1])
            for dest in range(8):
                g.reg_mov64(addr, TPB_BASES[dest] + crecv_off + (slot * E0 + e) * 4)
                g.store(addr, val_r)
        for dest in range(8):
            g.reg_mov64(addr, TPB_BASES[dest] + flags_off + slot * 4)
            g.store(addr, seed_r)


def emit_exchange_poll(nc, g, flags, seedt, met):
    seed_r = g.alloc_register(name="pseed")
    g.reg_load(seed_r, seedt[0:1, 0:1])
    me_r = g.alloc_register(name="pme")
    g.reg_load(me_r, met[0:1, 0:1])
    gates = []
    for j in range(8):
        r = g.alloc_register(name=f"pg{j}")
        g.reg_alu(r, me_r, j, ALU.is_gt)
        gates.append(r)
    facc = g.alloc_register(name="xfacc")
    ftmp = g.alloc_register(name="xftmp")
    cond = g.alloc_register(name="xcond")
    g.reg_mov(cond, 1)
    wh = g.While(cond)
    with wh:
        g.reg_mov(facc, 0)
        for p in range(8):
            g.reg_load(ftmp, flags[0:1, p:p + 1])
            g.reg_alu(ftmp, ftmp, seed_r, ALU.is_equal)
            g.reg_alu(ftmp, ftmp, gates[p], ALU.mult)
            g.reg_alu(facc, facc, ftmp, ALU.add)
        g.reg_alu(cond, facc, me_r, ALU.is_lt)


def build_launch_FUSED():
    nc = bass.Bass("TRN2", target_bir_lowering=False)
    x_in = _mk(nc, "x", [T, D], F32, "ExternalInput")
    w_attn = _mk(nc, "w_attn", [D, 3 * D], F32, "ExternalInput")
    w_proj = _mk(nc, "w_proj", [D, D], F32, "ExternalInput")
    rw = _mk(nc, "rw", [E0, D], F32, "ExternalInput")
    eb = _mk(nc, "eb", [E0], F32, "ExternalInput")
    w1 = _mk(nc, "w1", [E0, D, 512], F32, "ExternalInput")
    w2 = _mk(nc, "w2", [E0, 512, D], F32, "ExternalInput")
    sw1 = _mk(nc, "sw1", [2048, D], F32, "ExternalInput")
    sw2 = _mk(nc, "sw2", [D, 2048], F32, "ExternalInput")
    wqkv = _mk(nc, "wqkv", [D, 3 * D], F32, "ExternalInput")
    bqkv = _mk(nc, "bqkv", [3 * D], F32, "ExternalInput")
    wout = _mk(nc, "wout", [D, D], F32, "ExternalInput")
    bout = _mk(nc, "bout", [D], F32, "ExternalInput")
    rw1 = _mk(nc, "rw1", [D, E1], F32, "ExternalInput")
    rb1 = _mk(nc, "rb1", [1, 1], F32, "ExternalInput")
    gw = _mk(nc, "gw", [E1, D, 512], F32, "ExternalInput")
    uw = _mk(nc, "uw", [E1, D, 512], F32, "ExternalInput")
    dw = _mk(nc, "dw", [E1, 512, D], F32, "ExternalInput")
    sgw = _mk(nc, "sgw", [D, 2048], F32, "ExternalInput")
    suw = _mk(nc, "suw", [D, 2048], F32, "ExternalInput")
    sdw = _mk(nc, "sdw", [2048, D], F32, "ExternalInput")
    me_in = _mk(nc, "me", [1, 1], I32, "ExternalInput")
    seed_in = _mk(nc, "seed", [1, 1], I32, "ExternalInput")
    prevmask = _mk(nc, "prevmask", [NCORES, 1], F32, "ExternalInput")
    xout = _mk(nc, "xout", [T, D], F32, "ExternalOutput")

    x1 = _mk(nc, "s_x1", [T, D], F32)
    x2 = _mk(nc, "s_x2", [T, D], F32)
    x3 = _mk(nc, "s_x3", [T, D], F32)
    ln1 = _mk(nc, "s_ln1", [T, D], BF16)
    idx6 = _mk(nc, "s_idx6", [T, 8], U32)
    w64 = _mk(nc, "s_w64", [T, E0], F32)
    totals_junk = _mk(nc, "s_totjunk", [E0, 1], F32)
    slotbuf = _mk(nc, "s_slotbuf", [NSLOT + 128, D], BF16)
    yshared = _mk(nc, "s_yshared", [T, D], BF16)
    counts_junk = _mk(nc, "s_cntjunk", [1, E0], F32)
    counts_dram = _mk(nc, "s_cnt8", [NCORES, E0], I32)
    scratch = {
        "ln": _mk(nc, "s_ln", [T, D], BF16),
        "qkT": _mk(nc, "s_qkT", [2 * D, T], BF16),
        "v": _mk(nc, "s_v", [T // 128, 16, 128, 64], BF16),
        "oT": _mk(nc, "s_oT", [D, T], BF16),
        "y": _mk(nc, "s_y", [T, D], F32),
        "nb": _mk(nc, "s_nb", [2, 2, 8], F32),
        "rwb": _mk(nc, "s_rwb", [E0, D], BF16),
        "xs": _mk(nc, "s_xs", [NSLOT, D], BF16),
        "sw1b": _mk(nc, "s_sw1b", [2048, D], BF16),
        "sw2b": _mk(nc, "s_sw2b", [D, 2048], BF16),
        "hsT": _mk(nc, "s_hsT", [2048, T], BF16),
        "ysbuf": _mk(nc, "s_ysbuf", [NSLOT + 1, D], BF16),
        "yffn": _mk(nc, "s_yffn", [T, D], F32),
        "l1ln": _mk(nc, "s_l1ln", [T, D], BF16),
        "pqT": _mk(nc, "s_pqT", [D, T], BF16),
        "pkT": _mk(nc, "s_pkT", [D, T], BF16),
        "ktok": _mk(nc, "s_ktok", [T, D], BF16),
        "l1v": _mk(nc, "s_l1v", [T // 128, 16, 128, 64], BF16),
        "l1oT": _mk(nc, "s_l1oT", [D, T], BF16),
        "l1y": _mk(nc, "s_l1y", [T, D], F32),
        "l2ln": _mk(nc, "s_l2ln", [T, D], BF16),
        "actT": _mk(nc, "s_actT", [E1, 512, T], BF16),
        "hsT2": _mk(nc, "s_hsT2", [2048, T], BF16),
        "l2y": _mk(nc, "s_l2y", [T, D], F32),
    }

    ch = Chain(nc)
    with ExitStack() as st:
        crecv = alloc_sb(nc, st, "xc_crecv", [1, 8 * E0], I32)
        flags = alloc_sb(nc, st, "xc_flags", [1, 8], I32)
        met = alloc_sb(nc, st, "xc_met", [1, 1], I32)
        seedt = alloc_sb(nc, st, "xc_seedt", [1, 1], I32)
        cx = alloc_sb(nc, st, "xc_cx", [1, E0], I32)
        ch.make_sems(st)
        block = st.enter_context(nc.Block())

        with ch.group():
            ch.dma("gpsimd", met[:], me_in)
            ch.dma("gpsimd", seedt[:], seed_in)

        emit_l0_attn(nc, ch, st, x_in, w_attn, w_proj, x1, scratch)
        emit_l0ffn_routing(nc, ch, st, x1, rw, eb, ln1, idx6, w64, totals_junk, scratch)
        wlab = [0]
        emit_l0ffn_scatter_v2(nc, ch, st, ln1, idx6, slotbuf, counts_junk,
                              cx_i32=cx, wlab=wlab)
        ch.op("gpsimd", lambda g: emit_exchange_bcast(nc, g, cx, crecv, flags, seedt, met),
              r=[cx[:], met[:], seedt[:]],
              w=[flags[:]])

        emit_l0ffn_experts_v2(nc, ch, st, slotbuf, w1, w2, scratch["ysbuf"])
        emit_l0ffn_shared(nc, ch, st, ln1, sw1, sw2, yshared, scratch)

        ch.op("gpsimd", lambda g: emit_exchange_poll(nc, g, flags, seedt, met),
              r=[flags[:], seedt[:], met[:]],
              w=[crecv[:]])
        ch.dma("gpsimd", counts_dram.rearrange("a b -> (a b)").rearrange("(o x) -> o x", o=1),
               crecv[:])
        emit_l0ffn_combine_v2(nc, ch, st, x1, idx6, w64, counts_dram, prevmask,
                              scratch["ysbuf"], yshared, x2, scratch)
        emit_l1_attn(nc, ch, st, x2, wqkv, bqkv, wout, bout, x3, scratch)
        emit_l1_ffn(nc, ch, st, x3, rw1, rb1, gw, uw, dw, sgw, suw, sdw, xout, scratch)
        ch.emit(block)
    return nc


_SEED_COUNTER = [0]


def kernel(**inputs):
    cores = list(range(NCORES))
    x = np.ascontiguousarray(inputs["L0_attn_x"])
    nc = _get("fused", build_launch_FUSED)
    import time
    _SEED_COUNTER[0] += 1
    seed = np.array([[(int(time.time_ns() // 1000) + _SEED_COUNTER[0] * 7919) % 2000000000 + 1]],
                    np.int32)
    ins = []
    for i in cores:
        pm = np.zeros((NCORES, 1), np.float32)
        pm[:i] = 1.0
        ins.append({
            "x": x[i], "w_attn": inputs["L0_attn_w_attn"], "w_proj": inputs["L0_attn_w_proj"],
            "rw": inputs["L0_ffn_router_weight"], "eb": inputs["L0_ffn_expert_bias"],
            "w1": inputs["L0_ffn_w1"], "w2": inputs["L0_ffn_w2"],
            "sw1": inputs["L0_ffn_sw1"], "sw2": inputs["L0_ffn_sw2"],
            "wqkv": inputs["L1_attn_W_qkv"], "bqkv": inputs["L1_attn_b_qkv"],
            "wout": inputs["L1_attn_W_out"], "bout": inputs["L1_attn_b_out"],
            "rw1": inputs["L1_ffn_router_w"],
            "rb1": np.asarray(inputs["L1_ffn_router_bias"]).reshape(1, 1),
            "gw": inputs["L1_ffn_gate_w"], "uw": inputs["L1_ffn_up_w"],
            "dw": inputs["L1_ffn_down_w"], "sgw": inputs["L1_ffn_s_gate_w"],
            "suw": inputs["L1_ffn_s_up_w"], "sdw": inputs["L1_ffn_s_down_w"],
            "me": np.array([[i]], np.int32), "seed": seed, "prevmask": pm})
    res = run_bass_kernel_spmd(nc, ins, core_ids=cores)
    return np.stack([r["xout"] for r in res.results], 0)


# version 17
# speedup vs baseline: 1.0936x; 1.0010x over previous
import numpy as np
import ml_dtypes
import concourse.bass as bass
import concourse.mybir as mybir
from concourse.bass_utils import run_bass_kernel_spmd
from contextlib import ExitStack

F32 = mybir.dt.float32
BF16 = mybir.dt.bfloat16
U32 = mybir.dt.uint32
AF = mybir.ActivationFunctionType
ALU = mybir.AluOpType
AX = mybir.AxisListType

T = 4096
D = 1024
NH = 16
HD = 64
NCORES = 8

import inspect
from collections import defaultdict

SEQS = ("tensor", "vector", "scalar", "gpsimd", "sync")
ENG_KIND = {"tensor": "te", "vector": "ve", "scalar": "se", "gpsimd": "ge", "sync": "sy"}
DMA_KIND = {"gpsimd": "gd", "sync": "sd", "scalar": "cd", "vector": "vd"}
DMA_KINDS = set(DMA_KIND.values())
ZQ = 12

_WRITE_PARAMS = {"out", "ap", "accum_out"}
_CURRENT = [None]


class _Dummy:
    def then_inc(self, *a, **k):
        return self


def _ap_interval(ap):
    lo = ap.offset
    span = 0
    for s, n in ap.ap:
        if n > 1:
            span += (n - 1) * s
    return (lo, lo + span + 1)


class _Rec:
    def __init__(self, chain, ename):
        self._ch = chain
        self._en = ename
        self.reads = []
        self.writes = []

    def _collect(self, val, into):
        if isinstance(val, bass.AP):
            into.append(val)
        elif hasattr(val, "ap") and isinstance(getattr(val, "ap", None), bass.AP):
            self.reads.append(val.ap)
        elif isinstance(val, (list, tuple)):
            for v in val:
                if isinstance(v, bass.AP):
                    into.append(v)

    def __getattr__(self, m):
        if m.startswith("_"):
            raise AttributeError(m)
        ch = self._ch
        sig = ch._sig(self._en, m)
        rec = self

        def call(*a, **kw):
            ba = sig.bind(*a, **kw)
            args = ba.arguments
            for pname, val in args.items():
                if pname in _WRITE_PARAMS:
                    rec._collect(val, rec.writes)
                else:
                    rec._collect(val, rec.reads)
            if m == "matmul" and args.get("start") is False:
                rec._collect(args.get("out"), rec.reads)
            if m == "indirect_dma_start" and args.get("compute_op") is not None \
                    and args.get("compute_op") is not Ellipsis:
                rec._collect(args.get("out"), rec.reads)
            if m == "dma_start" and args.get("accum_op") is not None \
                    and args.get("accum_op") is not Ellipsis:
                rec._collect(args.get("out"), rec.reads)
            return _Dummy()

        return call


class Chain:
    def __init__(self, nc):
        self.nc = nc
        self.seq_ops = {e: [] for e in SEQS}
        self.kcount = defaultdict(int)
        self.state = {}
        self.phys = {}
        self.initdeps = {}
        self.freed = []
        self._sigs = {}
        _CURRENT[0] = self

    def _sig(self, ename, m):
        k = (ename, m)
        s = self._sigs.get(k)
        if s is None:
            s = inspect.signature(getattr(getattr(self.nc, ename), m))
            self._sigs[k] = s
        return s

    def note_alloc(self, handle):
        try:
            m = self.nc.lookup_mloc(handle)
        except Exception:
            return
        colb = int(m.dims[1]) if len(m.dims) > 1 else int(m.dims[0])
        if m.type == "PSUM":
            lo = (1 << 40) + int(m.bank) * 2048 + int(m.addr)
        else:
            lo = int(m.addr)
        hi = lo + colb
        key = m.type + ":" + m.name
        self.phys[key] = (lo, hi)
        init = {}
        for plo, phi, deps in self.freed:
            if plo < hi and lo < phi:
                for kind, pos in deps.items():
                    if init.get(kind, -1) < pos:
                        init[kind] = pos
        if init:
            self.initdeps[key] = init

    def note_free(self, handle):
        try:
            m = self.nc.lookup_mloc(handle)
        except Exception:
            return
        key = m.type + ":" + m.name
        rng = self.phys.pop(key, None)
        if rng is None:
            return
        st = self.state.pop(key, None)
        init = self.initdeps.pop(key, None)
        deps = {}
        if init:
            deps.update(init)
        if st:
            for (kind, pos, lo, hi) in st["w"]:
                if deps.get(kind, -1) < pos:
                    deps[kind] = pos
            for kind, ent in st["r"].items():
                p = ent[0][-1] if isinstance(ent[0], list) else ent[0]
                if deps.get(kind, -1) < p:
                    deps[kind] = p
        self.freed.append((rng[0], rng[1], deps))

    @staticmethod
    def _key(ap):
        sp = str(ap.space)
        if "SBUF" in sp:
            return "SB:" + ap.name
        if "PSUM" in sp:
            return "PSUM:" + ap.name
        return "D:" + ap.name

    def _deps_for(self, reads, writes, mykind):
        need = {}

        def add(kind, pos):
            if kind == "te" and mykind == "te":
                return
            v = pos + 1
            if need.get(kind, 0) < v:
                need[kind] = v

        def add_read_entry(kind, ent):
            if isinstance(ent[0], list):
                for p in ent[0]:
                    add(kind, p)
            else:
                add(kind, ent[0])

        for (key, lo, hi) in reads:
            ini = self.initdeps.get(key)
            if ini:
                for k2, p2 in ini.items():
                    add(k2, p2)
            st = self.state.get(key)
            if st:
                for (k2, p2, l2, h2) in st["w"]:
                    if l2 < hi and lo < h2:
                        add(k2, p2)
        for (key, lo, hi) in writes:
            ini = self.initdeps.get(key)
            if ini:
                for k2, p2 in ini.items():
                    add(k2, p2)
            st = self.state.get(key)
            if st:
                for (k2, p2, l2, h2) in st["w"]:
                    if l2 < hi and lo < h2:
                        add(k2, p2)
                for k2, ent in st["r"].items():
                    if ent[1] < hi and lo < ent[2]:
                        add_read_entry(k2, ent)
        return need

    def _update_state(self, reads, writes, mykind, mypos):
        for (key, lo, hi) in reads:
            st = self.state.setdefault(key, {"w": [], "r": {}})
            ent = st["r"].get(mykind)
            if mykind in DMA_KINDS:
                if ent is None:
                    st["r"][mykind] = [[mypos], lo, hi]
                else:
                    ent[0].append(mypos)
                    ent[0] = [p for p in ent[0] if p > mypos - ZQ]
                    ent[1] = min(ent[1], lo)
                    ent[2] = max(ent[2], hi)
            else:
                if ent is None:
                    st["r"][mykind] = [mypos, lo, hi]
                else:
                    ent[0] = max(ent[0], mypos)
                    ent[1] = min(ent[1], lo)
                    ent[2] = max(ent[2], hi)
        for (key, lo, hi) in writes:
            st = self.state.setdefault(key, {"w": [], "r": {}})
            st["w"] = [w for w in st["w"] if not (lo <= w[2] and w[3] <= hi)]
            st["r"] = {k: v for k, v in st["r"].items()
                       if not (lo <= v[1] and v[2] <= hi)}
            st["w"].append((mykind, mypos, lo, hi))

    def _record(self, seq, fn, kind, r, w):
        deps = self._deps_for(r, w, kind)
        pos = self.kcount[kind]
        self.kcount[kind] = pos + 1
        self._update_state(r, w, kind, pos)
        self.seq_ops[seq].append((fn, kind, pos, deps))

    def _norm(self, aps):
        out = []
        for ap in aps:
            lo, hi = _ap_interval(ap)
            out.append((self._key(ap), lo, hi))
        return out

    def op(self, engine, fn, r=None, w=None):
        kind = ENG_KIND[engine]
        if r is None and w is None:
            rec = _Rec(self, engine)
            fn(rec)
            r, w = rec.reads, rec.writes
        else:
            r = r or []
            w = w or []
        self._record(engine, fn, kind, self._norm(r), self._norm(w))

    def dma(self, engine, out, in_, **kw):
        kind = DMA_KIND[engine]
        self._record(engine, lambda e: e.dma_start(out=out, in_=in_, **kw),
                     kind, self._norm([in_]), self._norm([out]))

    def dma_fn(self, engine, fn):
        kind = DMA_KIND[engine]
        rec = _Rec(self, engine)
        fn(rec)
        self._record(engine, fn, kind, self._norm(rec.reads), self._norm(rec.writes))

    def dma_transpose(self, out, in_):
        self._record("sync", lambda e: e.dma_start_transpose(out, in_),
                     "sd", self._norm([in_]), self._norm([out]))

    def group(self):
        class _G:
            def __enter__(self):
                return self

            def __exit__(self, *a):
                return False

        return _G()

    def make_sems(self, st):
        self.esem = {}
        for kind in ("te", "ve", "se", "ge", "sy"):
            self.esem[kind] = st.enter_context(self.nc.semaphore("q_" + kind))
        self.dsem = {}
        for kind in ("gd", "sd", "cd", "vd"):
            self.dsem[kind] = [st.enter_context(self.nc.semaphore(f"q_{kind}{i}"))
                               for i in range(ZQ)]

    def _wait_target(self, kind, val):
        if kind in DMA_KINDS:
            pos = val - 1
            return (self.dsem[kind][pos % ZQ], 16 * (pos // ZQ + 1))
        return (self.esem[kind], val)

    def emit(self, block):
        for seq in SEQS:
            ops = self.seq_ops[seq]
            if not ops:
                continue

            def make_body(ops):
                def body(eng):
                    clock = {}
                    for fn, kind, pos, deps in ops:
                        for k2 in sorted(deps):
                            sem, val = self._wait_target(k2, deps[k2])
                            cid = id(sem)
                            if clock.get(cid, 0) < val:
                                eng.wait_ge(sem, val)
                                clock[cid] = val
                        if kind in DMA_KINDS:
                            sem = self.dsem[kind][pos % ZQ]
                            if pos >= ZQ:
                                gate = 16 * (pos // ZQ)
                                cid = id(sem)
                                if clock.get(cid, 0) < gate:
                                    eng.wait_ge(sem, gate)
                                    clock[cid] = gate
                            inst = fn(eng)
                            if inst is None:
                                inst = eng.nop()
                            inst.then_inc(sem, 16)
                        else:
                            inst = fn(eng)
                            if inst is None:
                                inst = eng.nop()
                            inst.then_inc(self.esem[kind], 1)
                return body

            getattr(block, seq)(make_body(ops))

        kc = dict(self.kcount)
        dtot = {k: defaultdict(int) for k in DMA_KINDS}
        for k in DMA_KINDS:
            for p in range(kc.get(k, 0)):
                dtot[k][p % ZQ] += 16

        def fin(eng):
            for kind in ("te", "ve", "se", "ge"):
                if kc.get(kind, 0) > 0:
                    eng.wait_ge(self.esem[kind], kc[kind])
            for k in DMA_KINDS:
                for r, v in dtot[k].items():
                    eng.wait_ge(self.dsem[k][r], v)

        block.sync(fin)


def alloc_sb(nc, st, name, shape, dt=F32):
    h = st.enter_context(nc.sbuf_tensor(name, shape, dt))
    ch = _CURRENT[0]
    if ch is not None:
        ch.note_alloc(h)
        st.callback(lambda: ch.note_free(h))
    return h


def alloc_ps(nc, st, name, shape, dt=F32):
    h = st.enter_context(nc.psum_tensor(name, shape, dt))
    ch = _CURRENT[0]
    if ch is not None:
        ch.note_alloc(h)
        st.callback(lambda: ch.note_free(h))
    return h


def emit_ln_tok(nc, ch, st, pre, x_src_dram, ln_dst_dram, res_add_dram=None,
                x_out_dram=None, n_tiles=None):
    NB = 3
    xt = [alloc_sb(nc, st, pre + f"xt{b}", [128, D]) for b in range(NB)]
    bnb = [alloc_sb(nc, st, pre + f"bnb{b}", [128, 12]) for b in range(NB)]
    mv = [alloc_sb(nc, st, pre + f"mv{b}", [128, 2]) for b in range(NB)]
    t1 = [alloc_sb(nc, st, pre + f"t1{b}", [128, 1]) for b in range(NB)]
    rr = [alloc_sb(nc, st, pre + f"rr{b}", [128, 1]) for b in range(NB)]
    mb = [alloc_sb(nc, st, pre + f"mb{b}", [128, 1]) for b in range(NB)]
    lnb = [alloc_sb(nc, st, pre + f"lnb{b}", [128, D], BF16) for b in range(NB)]
    epsb = alloc_sb(nc, st, pre + "epsb", [128, 1])
    if n_tiles is None:
        n_tiles = T // 128
    ch.op("vector", lambda v: v.memset(epsb[:], 1e-5))
    if res_add_dram is not None:
        resb = [alloc_sb(nc, st, pre + f"resb{b}", [128, D]) for b in range(NB)]
        outb = [alloc_sb(nc, st, pre + f"outb{b}", [128, D]) for b in range(NB)]
    def _tile(i):
        r0 = i * 128
        b = i % NB
        xtb, bnbb, mvb, t1b, rrb, mbb = xt[b], bnb[b], mv[b], t1[b], rr[b], mb[b]
        ch.dma("scalar", xtb[:], x_src_dram[r0:r0 + 128, :])
        ch.op("vector", lambda v: v.bn_stats(out=bnbb[:, 0:6], in_=xtb[:, 0:512]))
        ch.op("vector", lambda v: v.bn_stats(out=bnbb[:, 6:12], in_=xtb[:, 512:1024]))
        ch.op("vector", lambda v: v.bn_aggr(out=mvb[:], in_=bnbb[:]))
        ch.op("scalar", lambda s: s.activation(out=t1b[:], in_=mvb[:, 1:2], func=AF.Sqrt, bias=epsb[:, 0:1]))
        ch.op("vector", lambda v: v.reciprocal(out=rrb[:], in_=t1b[:]))
        ch.op("vector", lambda v: v.tensor_scalar(
            out=mbb[:], in0=mvb[:, 0:1], scalar1=rrb[:, 0:1], scalar2=-1.0,
            op0=ALU.mult, op1=ALU.mult))
        if ln_dst_dram is not None:
            ch.op("vector", lambda v: v.tensor_scalar(
                out=lnb[b][:], in0=xtb[:], scalar1=rrb[:, 0:1], scalar2=mbb[:, 0:1],
                op0=ALU.mult, op1=ALU.add))
            ch.dma("gpsimd", ln_dst_dram[r0:r0 + 128, :], lnb[b][:])
        if res_add_dram is not None:
            ch.dma("scalar", resb[b][:], res_add_dram[r0:r0 + 128, :])
            ch.op("vector", lambda v: v.tensor_scalar(
                out=outb[b][:], in0=xtb[:], scalar1=rrb[:, 0:1], scalar2=mbb[:, 0:1],
                op0=ALU.mult, op1=ALU.add))
            ch.op("vector", lambda v: v.tensor_tensor(
                out=outb[b][:], in0=outb[b][:], in1=resb[b][:], op=ALU.add))
            ch.dma("gpsimd", x_out_dram[r0:r0 + 128, :], outb[b][:])

    for i in range(n_tiles):
        _tile(i)


def emit_transpose_to_feat(nc, ch, tok_dram_bf16, xT_sb, rows=None):
    if rows is None:
        rows = T
    for c in range(len(xT_sb)):
        ch.dma_transpose(xT_sb[c][:, 0:rows], tok_dram_bf16[0:rows, c * 128:(c + 1) * 128])


def emit_l0_attn(nc, ch, st, x_dram, w_attn_dram, w_proj_dram, x1_dram, scratch):
    ln_dram = scratch["ln"]
    qkT_dram = scratch["qkT"]
    v_dram = scratch["v"]
    oT_dram = scratch["oT"]
    y_dram = scratch["y"]
    nb_dram = scratch["nb"]

    with ExitStack() as st1:
        emit_ln_tok(nc, ch, st1, "l0a_ln1", x_dram, ln_dram)

    with ExitStack() as st2:
        xT = [alloc_sb(nc, st2, f"l0a_xT{c}", [128, T], BF16) for c in range(8)]
        emit_transpose_to_feat(nc, ch, ln_dram, xT)

        wat = [alloc_sb(nc, st2, f"l0a_w{c}", [128, 3072], BF16) for c in range(8)]
        for c in range(8):
            ch.dma("gpsimd", wat[c][:], w_attn_dram[c * 128:(c + 1) * 128, :])

        ps = [alloc_ps(nc, st2, f"l0a_ps{b}", [128, 512]) for b in range(2)]
        evb = [alloc_sb(nc, st2, f"l0a_evb{b}", [128, 512], BF16) for b in range(2)]
        sqb = [alloc_sb(nc, st2, f"l0a_sqb{b}", [128, 512], BF16) for b in range(2)]
        onesAB = alloc_sb(nc, st2, "l0a_ones", [128, 2], BF16)
        psn = [alloc_ps(nc, st2, f"l0a_psn{b}", [2, 512]) for b in range(2)]
        nrow = [alloc_sb(nc, st2, f"l0a_nrow{b}", [2, 1]) for b in range(2)]
        mxq = alloc_sb(nc, st2, "l0a_mxq", [2, 8])
        mxk = alloc_sb(nc, st2, "l0a_mxk", [2, 8])
        ch.op("vector", lambda v: v.memset(onesAB[:, :], 0.0))
        ch.op("vector", lambda v: v.memset(onesAB[0:64, 0:1], 1.0))
        ch.op("vector", lambda v: v.memset(onesAB[64:128, 1:2], 1.0))
        ch.op("vector", lambda v: v.memset(mxq[:], 0.0))
        ch.op("vector", lambda v: v.memset(mxk[:], 0.0))

        def _qk_tile(it, qk, col0, mx, j, t):
            scale = 0.125 if qk == "q" else 1.0
            psb_, evbb, sqbb, psnb, nrowb = ps[it % 2], evb[it % 2], sqb[it % 2], \
                psn[it % 2], nrow[it % 2]
            for c in range(8):
                ch.op("tensor", lambda tt, c=c: tt.matmul(
                    out=psb_[:], lhsT=wat[c][:, col0 + j * 128:col0 + (j + 1) * 128],
                    rhs=xT[c][:, t * 512:(t + 1) * 512],
                    start=(c == 0), stop=(c == 7)))
            ch.op("scalar", lambda s: s.activation(
                out=evbb[:], in_=psb_[:], func=AF.Copy, scale=scale))
            ch.dma("gpsimd", qkT_dram[col0 + j * 128:col0 + (j + 1) * 128,
                                      t * 512:(t + 1) * 512], evbb[:])
            ch.op("scalar", lambda s: s.activation(
                out=sqbb[:], in_=psb_[:], func=AF.Square, scale=scale))
            ch.op("tensor", lambda tt: tt.matmul(out=psnb[:], lhsT=onesAB[:], rhs=sqbb[:],
                                                 start=True, stop=True))
            ch.op("vector", lambda v: v.tensor_reduce(out=nrowb[:], in_=psnb[:],
                                                      axis=AX.X, op=ALU.max))
            ch.op("vector", lambda v: v.tensor_tensor(
                out=mx[:, j:j + 1], in0=mx[:, j:j + 1], in1=nrowb[:], op=ALU.max))

        it = 0
        for (qk, col0, mx) in (("k", D, mxk), ("q", 0, mxq)):
            for j in range(8):
                for t in range(T // 512):
                    _qk_tile(it, qk, col0, mx, j, t)
                    it += 1

        def _v_tile(it, g, ti):
            psb_, evbb = ps[it % 2], evb[it % 2]
            for c in range(8):
                ch.op("tensor", lambda tt, c=c: tt.matmul(
                    out=psb_[:], lhsT=xT[c][:, ti * 128:(ti + 1) * 128],
                    rhs=wat[c][:, 2 * D + g * 512: 2 * D + (g + 1) * 512],
                    start=(c == 0), stop=(c == 7)))
            ch.op("scalar", lambda s: s.copy(out=evbb[:], in_=psb_[:]))
            ch.dma("gpsimd", v_dram[ti, 8 * g:8 * (g + 1), :, :].rearrange("h p c -> p h c"),
                   evbb[:])

        for g in range(2):
            for ti in range(T // 128):
                _v_tile(it, g, ti)
                it += 1

        ch.dma("gpsimd", nb_dram[0], mxq[:])
        ch.dma("gpsimd", nb_dram[1], mxk[:])

    with ExitStack() as st3:
        nb16 = alloc_sb(nc, st3, "l0a_nb16", [1, 32])
        negc1 = alloc_sb(nc, st3, "l0a_negc1", [1, 16])
        ones_r = alloc_sb(nc, st3, "l0a_ones_r", [1, 128])
        biascol = [alloc_sb(nc, st3, f"l0a_biascol{b}", [128, 1]) for b in range(2)]
        psb = [alloc_ps(nc, st3, f"l0a_psb{b}", [128, 1]) for b in range(2)]
        qt = [alloc_sb(nc, st3, f"l0a_qt{b}", [64, T], BF16) for b in range(2)]
        kt = [alloc_sb(nc, st3, f"l0a_kt{b}", [64, T], BF16) for b in range(2)]
        vt = [alloc_sb(nc, st3, f"l0a_vt{b}", [128, (T // 128) * 65], BF16) for b in range(2)]
        rd = [alloc_sb(nc, st3, f"l0a_rd{b}", [1, 512]) for b in range(2)]
        ones64 = alloc_sb(nc, st3, "l0a_ones64", [1, 64])
        psbc = [alloc_ps(nc, st3, f"l0a_psbc{b}", [64, 512]) for b in range(2)]
        bcb = [alloc_sb(nc, st3, f"l0a_bcb{b}", [64, 512]) for b in range(2)]
        obn = [alloc_sb(nc, st3, f"l0a_obn{b}", [64, 512], BF16) for b in range(2)]
        et = [alloc_sb(nc, st3, f"l0a_et{i}", [128, 512], BF16) for i in range(T // 128)]
        masks = alloc_sb(nc, st3, "l0a_mask", [128, 4 * 512], BF16)
        colid = alloc_sb(nc, st3, "l0a_colid", [128, 512])
        rowid = alloc_sb(nc, st3, "l0a_rowid", [128, 1])
        pss = [alloc_ps(nc, st3, f"l0a_pss{b}", [128, 512]) for b in range(2)]
        pso = [alloc_ps(nc, st3, f"l0a_pso{b}", [65, 512]) for b in range(2)]
        ob = [alloc_sb(nc, st3, f"l0a_ob{b}", [65, 512]) for b in range(2)]

        ch.dma("gpsimd", nb16[:], nb_dram.rearrange("a b c -> (a b c)").rearrange("(x y) -> x y", x=1))
        ch.op("scalar", lambda s: s.activation(out=nb16[:], in_=nb16[:], func=AF.Sqrt))
        ch.op("vector", lambda v: v.tensor_tensor(
            out=negc1[:], in0=nb16[0:1, 0:16], in1=nb16[0:1, 16:32], op=ALU.mult))
        ch.op("vector", lambda v: v.tensor_scalar(
            out=negc1[:], in0=negc1[:], scalar1=-1.0, scalar2=-1.0,
            op0=ALU.mult, op1=ALU.add))
        ch.op("vector", lambda v: v.memset(ones_r[:], 1.0))
        ch.op("vector", lambda v: v.memset(ones64[:], 1.0))

        ch.op("gpsimd", lambda g: g.iota(colid[:], pattern=[[1, 512]], base=0,
                                         channel_multiplier=0,
                                         allow_small_or_imprecise_dtypes=True))
        ch.op("gpsimd", lambda g: g.iota(rowid[:], pattern=[[0, 1]], base=0,
                                         channel_multiplier=1,
                                         allow_small_or_imprecise_dtypes=True))
        for i in range(4):
            ch.op("vector", lambda v, i=i: v.tensor_scalar(
                out=masks[:, i * 512:(i + 1) * 512], in0=colid[:],
                scalar1=float(128 * i), scalar2=rowid[:, 0:1],
                op0=ALU.subtract, op1=ALU.is_ge))

        def _head(h):
            hb_ = h % 2
            hflat = (h % 2) * 8 + h // 2
            qtb, ktb, vtb, bcol = qt[hb_], kt[hb_], vt[hb_], biascol[hb_]
            ch.op("tensor", lambda tt: tt.matmul(
                out=psb[hb_][:], lhsT=ones_r[:], rhs=negc1[0:1, hflat:hflat + 1],
                start=True, stop=True))
            ch.op("vector", lambda v: v.tensor_copy(out=bcol[:], in_=psb[hb_][:]))

            ch.dma("scalar", qtb[:], qkT_dram[h * 64:(h + 1) * 64, :])
            ch.dma("scalar", ktb[:], qkT_dram[D + h * 64:D + (h + 1) * 64, :])
            ch.op("vector", lambda v: v.memset(vtb[:], 1.0))
            ch.dma("scalar", vtb[:].rearrange("p (k c) -> p k c", c=65)[:, :, 0:64],
                   v_dram[:, h, :, :].rearrange("k p c -> p k c"))
            def _jchunk(j):
                jb = j % 2
                nkt = 4 * (j + 1)
                for i in range(nkt):
                    psi = pss[i % 2]
                    ch.op("tensor", lambda tt, i=i, psi=psi: tt.matmul(
                        out=psi[:], lhsT=ktb[:, i * 128:(i + 1) * 128],
                        rhs=qtb[:, j * 512:(j + 1) * 512], start=True, stop=True))
                    ch.op("scalar", lambda s, i=i, psi=psi: s.activation(
                        out=et[i][:], in_=psi[:], func=AF.Exp, bias=bcol[:, 0:1]))
                    if i >= 4 * j:
                        mi = i - 4 * j
                        ch.op("vector", lambda v, i=i, mi=mi: v.tensor_tensor(
                            out=et[i][:], in0=et[i][:],
                            in1=masks[:, mi * 512:(mi + 1) * 512], op=ALU.mult))
                for i in range(nkt):
                    ch.op("tensor", lambda tt, i=i, nkt=nkt: tt.matmul(
                        out=pso[jb][:], lhsT=vtb[:, i * 65:(i + 1) * 65], rhs=et[i][:],
                        start=(i == 0), stop=(i == nkt - 1)))
                ch.op("vector", lambda v: v.tensor_copy(out=ob[jb][:], in_=pso[jb][:]))
                ch.op("vector", lambda v: v.tensor_copy(out=rd[jb][:], in_=ob[jb][64:65, :]))
                ch.op("tensor", lambda tt: tt.matmul(out=psbc[jb][:], lhsT=ones64[:],
                                                     rhs=rd[jb][:], start=True, stop=True))
                ch.op("vector", lambda v: v.reciprocal(out=bcb[jb][:], in_=psbc[jb][:]))
                ch.op("vector", lambda v: v.tensor_tensor(out=obn[jb][:], in0=ob[jb][0:64, :],
                                                          in1=bcb[jb][:], op=ALU.mult))
                ch.dma("gpsimd", oT_dram[h * 64:(h + 1) * 64, j * 512:(j + 1) * 512],
                       obn[jb][:])

            for j in range(T // 512):
                _jchunk(j)

        for h in range(NH):
            _head(h)

    with ExitStack() as st4:
        oT = [alloc_sb(nc, st4, f"l0a_oT{c}", [128, T], BF16) for c in range(8)]
        for c in range(8):
            ch.dma("scalar", oT[c][:], oT_dram[c * 128:(c + 1) * 128, :])
        wp = [alloc_sb(nc, st4, f"l0a_wp{c}", [128, D], BF16) for c in range(8)]
        for c in range(8):
            ch.dma("gpsimd", wp[c][:], w_proj_dram[c * 128:(c + 1) * 128, :])
        ps2 = [alloc_ps(nc, st4, f"l0a_ps2{b}", [128, 512]) for b in range(2)]
        yb = [alloc_sb(nc, st4, f"l0a_yb{b}", [128, 512]) for b in range(2)]

        def _op_tile(it, ti, jc):
            p, y = ps2[it % 2], yb[it % 2]
            for c in range(8):
                ch.op("tensor", lambda tt, c=c: tt.matmul(
                    out=p[:], lhsT=oT[c][:, ti * 128:(ti + 1) * 128],
                    rhs=wp[c][:, jc * 512:(jc + 1) * 512],
                    start=(c == 0), stop=(c == 7)))
            ch.op("vector", lambda v: v.tensor_copy(out=y[:], in_=p[:]))
            ch.dma("gpsimd", y_dram[ti * 128:(ti + 1) * 128,
                                    jc * 512:(jc + 1) * 512], y[:])

        it = 0
        for ti in range(T // 128):
            for jc in range(2):
                _op_tile(it, ti, jc)
                it += 1

    with ExitStack() as st5:
        emit_ln_tok(nc, ch, st5, "l0a_ln2", y_dram, None, res_add_dram=x_dram,
                    x_out_dram=x1_dram)


def build_launch_l0attn():
    nc = bass.Bass("TRN2", target_bir_lowering=False)
    x_in = nc.dram_tensor("x", [T, D], F32, kind="ExternalInput").ap()
    w_attn = nc.dram_tensor("w_attn", [D, 3 * D], F32, kind="ExternalInput").ap()
    w_proj = nc.dram_tensor("w_proj", [D, D], F32, kind="ExternalInput").ap()
    x1 = nc.dram_tensor("x1", [T, D], F32, kind="ExternalOutput").ap()
    scratch = {
        "ln": nc.dram_tensor("s_ln", [T, D], BF16).ap(),
        "qkT": nc.dram_tensor("s_qkT", [2 * D, T], BF16).ap(),
        "v": nc.dram_tensor("s_v", [T // 128, 16, 128, 64], BF16).ap(),
        "oT": nc.dram_tensor("s_oT", [D, T], BF16).ap(),
        "y": nc.dram_tensor("s_y", [T, D], F32).ap(),
        "nb": nc.dram_tensor("s_nb", [2, 2, 8], F32).ap(),
    }
    ch = Chain(nc)
    with ExitStack() as st:
        ch.make_sems(st)
        block = st.enter_context(nc.Block())
        emit_l0_attn(nc, ch, st, x_in, w_attn, w_proj, x1, scratch)
        ch.emit(block)
    return nc


E0 = 64
K0 = 6
CAP = 384
NSLOT = E0 * CAP
ROWW = 1040


def emit_l0ffn_routing(nc, ch, st, x1_dram, rw_dram, eb_dram, ln1_dram,
                       idx6_dram, w64_dram, totals_dram, scratch):
    with ExitStack() as st1:
        emit_ln_tok(nc, ch, st1, "l0f_ln", x1_dram, ln1_dram)

    with ExitStack() as st2:
        lnT = [alloc_sb(nc, st2, f"l0f_lnT{c}", [128, T], BF16) for c in range(8)]
        emit_transpose_to_feat(nc, ch, ln1_dram, lnT)
        rwb_dram = scratch["rwb"]
        ch.dma("gpsimd", rwb_dram, rw_dram)
        rwT = [alloc_sb(nc, st2, f"l0f_rwT{c}", [128, E0], BF16) for c in range(8)]
        for c in range(8):
            ch.dma_transpose(rwT[c][:], rwb_dram[:, c * 128:(c + 1) * 128])
        ebrow = alloc_sb(nc, st2, "l0f_ebrow", [1, E0])
        ones_r = alloc_sb(nc, st2, "l0f_ones_r", [1, 128])
        psb = alloc_ps(nc, st2, "l0f_psb", [128, E0])
        ebB = alloc_sb(nc, st2, "l0f_ebB", [128, E0])
        ch.dma("gpsimd", ebrow[:], eb_dram.rearrange("(x e) -> x e", x=1))
        ch.op("vector", lambda v: v.memset(ones_r[:], 1.0))
        ch.op("tensor", lambda t: t.matmul(out=psb[:], lhsT=ones_r[:], rhs=ebrow[:],
                                           start=True, stop=True))
        ch.op("vector", lambda v: v.tensor_copy(out=ebB[:], in_=psb[:]))

        iotaE = alloc_sb(nc, st2, "l0f_iotaE", [128, E0])
        ch.op("gpsimd", lambda g: g.iota(iotaE[:], pattern=[[1, E0]], base=0,
                                         channel_multiplier=0,
                                         allow_small_or_imprecise_dtypes=True))
        psl = [alloc_ps(nc, st2, f"l0f_psl{b}", [128, E0]) for b in range(2)]
        pst = alloc_ps(nc, st2, "l0f_pst", [E0, 1])
        scb = [alloc_sb(nc, st2, f"l0f_scb{b}", [128, E0]) for b in range(2)]
        corr = [alloc_sb(nc, st2, f"l0f_corr{b}", [128, E0]) for b in range(2)]
        mx8 = [alloc_sb(nc, st2, f"l0f_mx8{b}", [128, 8]) for b in range(2)]
        mi8 = [alloc_sb(nc, st2, f"l0f_mi8{b}", [128, 8], U32) for b in range(2)]
        mif = [alloc_sb(nc, st2, f"l0f_mif{b}", [128, 8]) for b in range(2)]
        ohr = [alloc_sb(nc, st2, f"l0f_ohr{b}", [128, E0]) for b in range(2)]
        amat = [alloc_sb(nc, st2, f"l0f_amat{b}", [128, E0]) for b in range(2)]
        amatb = [alloc_sb(nc, st2, f"l0f_amatb{b}", [128, E0], BF16) for b in range(2)]
        w64 = [alloc_sb(nc, st2, f"l0f_w64{b}", [128, E0]) for b in range(2)]
        tsum = [alloc_sb(nc, st2, f"l0f_tsum{b}", [128, 1]) for b in range(2)]
        trec = [alloc_sb(nc, st2, f"l0f_trec{b}", [128, 1]) for b in range(2)]
        onescol = alloc_sb(nc, st2, "l0f_onescol", [128, 1], BF16)
        totsb = alloc_sb(nc, st2, "l0f_totsb", [E0, 1])
        ch.op("vector", lambda v: v.memset(onescol[:], 1.0))

        def _rt_tile(ti):
            b = ti % 2
            pslb, scbb, corrb, mx8b, mi8b, mifb = psl[b], scb[b], corr[b], mx8[b], mi8[b], mif[b]
            ohrb, amb, ambb, w64b, tsumb, trecb = ohr[b], amat[b], amatb[b], w64[b], tsum[b], trec[b]
            for c in range(8):
                ch.op("tensor", lambda t, c=c: t.matmul(
                    out=pslb[:], lhsT=lnT[c][:, ti * 128:(ti + 1) * 128], rhs=rwT[c][:],
                    start=(c == 0), stop=(c == 7)))
            ch.op("scalar", lambda s: s.activation(out=scbb[:], in_=pslb[:], func=AF.Sigmoid))
            ch.op("vector", lambda v: v.tensor_tensor(out=corrb[:], in0=scbb[:], in1=ebB[:],
                                                      op=ALU.add))
            ch.op("vector", lambda v: v.max(out=mx8b[:], in_=corrb[:]))
            ch.op("vector", lambda v: v.max_index(out=mi8b[:], in_max=mx8b[:], in_values=corrb[:]))
            ch.op("vector", lambda v: v.tensor_copy(out=mifb[:], in_=mi8b[:]))
            for r in range(K0):
                if r == 0:
                    ch.op("vector", lambda v, r=r: v.tensor_scalar(
                        out=amb[:], in0=iotaE[:], scalar1=mifb[:, r:r + 1], scalar2=None,
                        op0=ALU.is_equal))
                else:
                    ch.op("vector", lambda v, r=r: v.tensor_scalar(
                        out=ohrb[:], in0=iotaE[:], scalar1=mifb[:, r:r + 1], scalar2=None,
                        op0=ALU.is_equal))
                    ch.op("vector", lambda v: v.tensor_tensor(out=amb[:], in0=amb[:],
                                                              in1=ohrb[:], op=ALU.add))
            ch.op("vector", lambda v: v.tensor_tensor(out=w64b[:], in0=amb[:], in1=scbb[:],
                                                      op=ALU.mult))
            ch.op("vector", lambda v: v.tensor_reduce(out=tsumb[:], in_=w64b[:],
                                                      axis=AX.X, op=ALU.add))
            ch.op("vector", lambda v: v.tensor_scalar(out=tsumb[:], in0=tsumb[:],
                                                      scalar1=1e-20, scalar2=None, op0=ALU.add))
            ch.op("vector", lambda v: v.reciprocal(out=trecb[:], in_=tsumb[:]))
            ch.op("vector", lambda v: v.tensor_scalar(out=w64b[:], in0=w64b[:],
                                                      scalar1=trecb[:, 0:1], scalar2=None,
                                                      op0=ALU.mult))
            ch.op("vector", lambda v: v.tensor_copy(out=ambb[:], in_=amb[:]))
            ch.op("tensor", lambda t: t.matmul(out=pst[:], lhsT=ambb[:], rhs=onescol[:],
                                               start=(ti == 0), stop=(ti == T // 128 - 1)))
            ch.dma("gpsimd", idx6_dram[ti * 128:(ti + 1) * 128, :], mi8b[:])
            ch.dma("gpsimd", w64_dram[ti * 128:(ti + 1) * 128, :], w64b[:])

        for ti in range(T // 128):
            _rt_tile(ti)
        ch.op("vector", lambda v: v.tensor_copy(out=totsb[:], in_=pst[:]))
        ch.dma("gpsimd", totals_dram, totsb[:])


def emit_l0ffn_scatter(nc, ch, st, ln1_dram, idx6_dram, w64_dram, totals_all_dram,
                       prevmask_dram, slotbuf_dram, gidx_dram, cap=None):
    if cap is None:
        cap = CAP
    with ExitStack() as st2:
        pmsk = alloc_sb(nc, st2, "l0s_pmsk", [8, 1])
        tall = alloc_sb(nc, st2, "l0s_tall", [8, E0])
        psbase = alloc_ps(nc, st2, "l0s_psbase", [1, E0])
        ch.dma("gpsimd", pmsk[:], prevmask_dram)
        ch.dma("gpsimd", tall[:], totals_all_dram)
        ch.op("tensor", lambda t: t.matmul(out=psbase[:], lhsT=pmsk[:], rhs=tall[:],
                                           start=True, stop=True))
        baser = alloc_sb(nc, st2, "l0s_baser", [1, E0])
        ch.op("vector", lambda v: v.tensor_copy(out=baser[:], in_=psbase[:]))

        iotaE = alloc_sb(nc, st2, "l0s_iotaE", [128, E0])
        colid = alloc_sb(nc, st2, "l0s_colid", [128, 128])
        rowid = alloc_sb(nc, st2, "l0s_rowid", [128, 1])
        stri = alloc_sb(nc, st2, "l0s_stri", [128, 128])
        onesv = alloc_sb(nc, st2, "l0s_onesv", [1, 128])
        ch.op("gpsimd", lambda g: g.iota(iotaE[:], pattern=[[1, E0]], base=0,
                                         channel_multiplier=0,
                                         allow_small_or_imprecise_dtypes=True))
        ch.op("gpsimd", lambda g: g.iota(colid[:], pattern=[[1, 128]], base=0,
                                         channel_multiplier=0,
                                         allow_small_or_imprecise_dtypes=True))
        ch.op("gpsimd", lambda g: g.iota(rowid[:], pattern=[[0, 1]], base=0,
                                         channel_multiplier=1,
                                         allow_small_or_imprecise_dtypes=True))
        ch.op("vector", lambda v: v.tensor_scalar(out=stri[:], in0=colid[:],
                                                  scalar1=rowid[:, 0:1], scalar2=None,
                                                  op0=ALU.is_gt))
        ch.op("vector", lambda v: v.memset(onesv[:], 1.0))

        mi8 = alloc_sb(nc, st2, "l0s_mi8", [128, 8], U32)
        mif = alloc_sb(nc, st2, "l0s_mif", [128, 8])
        w64 = alloc_sb(nc, st2, "l0s_w64", [128, E0])
        ohr = alloc_sb(nc, st2, "l0s_ohr", [128, E0])
        amat = alloc_sb(nc, st2, "l0s_amat", [128, E0])
        tmpe = alloc_sb(nc, st2, "l0s_tmpe", [128, E0])
        pspos = alloc_ps(nc, st2, "l0s_pspos", [128, E0])
        posf = alloc_sb(nc, st2, "l0s_posf", [128, E0])
        valid = alloc_sb(nc, st2, "l0s_valid", [128, E0])
        wmask = alloc_sb(nc, st2, "l0s_wmask", [128, E0])
        lnb = alloc_sb(nc, st2, "l0s_lnb", [128, D], BF16)
        comb = alloc_sb(nc, st2, "l0s_comb", [128, ROWW], BF16)
        ch.op("vector", lambda v: v.memset(comb[:], 0.0))
        posr = alloc_sb(nc, st2, "l0s_posr", [128, 1])
        wr = alloc_sb(nc, st2, "l0s_wr", [128, 1])
        vr = alloc_sb(nc, st2, "l0s_vr", [128, 1], U32)
        slotf = alloc_sb(nc, st2, "l0s_slotf", [128, 1])
        ssel = alloc_sb(nc, st2, "l0s_ssel", [128, 1])
        invf = alloc_sb(nc, st2, "l0s_invf", [128, 1])
        slotu = alloc_sb(nc, st2, "l0s_slotu", [128, 1], U32)
        gslotf = alloc_sb(nc, st2, "l0s_gslotf", [128, 1])
        gslotu = alloc_sb(nc, st2, "l0s_gsu", [128, 6], U32)
        zrow = alloc_sb(nc, st2, "l0s_zrow", [128, 1])
        ch.op("vector", lambda v: v.memset(zrow[:], float(NSLOT)))
        rbase = alloc_sb(nc, st2, "l0s_rbase", [1, E0])
        psts = alloc_ps(nc, st2, "l0s_psts", [1, E0])
        tsrow = alloc_sb(nc, st2, "l0s_tsrow", [1, E0])
        onesc128 = alloc_sb(nc, st2, "l0s_onesc128", [128, 1])
        ch.op("vector", lambda v: v.memset(onesc128[:], 1.0))
        ch.op("vector", lambda v: v.tensor_copy(out=rbase[:], in_=baser[:]))

        for ti in range(T // 128):
            r0 = ti * 128
            ch.dma("gpsimd", mi8[:], idx6_dram[r0:r0 + 128, :])
            ch.dma("gpsimd", w64[:], w64_dram[r0:r0 + 128, :])
            ch.dma("gpsimd", lnb[:], ln1_dram[r0:r0 + 128, :])
            ch.op("vector", lambda v: v.tensor_copy(out=mif[:], in_=mi8[:]))
            for r in range(K0):
                if r == 0:
                    ch.op("vector", lambda v, r=r: v.tensor_scalar(
                        out=amat[:], in0=iotaE[:], scalar1=mif[:, r:r + 1], scalar2=None,
                        op0=ALU.is_equal))
                else:
                    ch.op("vector", lambda v, r=r: v.tensor_scalar(
                        out=ohr[:], in0=iotaE[:], scalar1=mif[:, r:r + 1], scalar2=None,
                        op0=ALU.is_equal))
                    ch.op("vector", lambda v: v.tensor_tensor(out=amat[:], in0=amat[:],
                                                              in1=ohr[:], op=ALU.add))
            ch.op("tensor", lambda t: t.matmul(out=pspos[:], lhsT=stri[:], rhs=amat[:],
                                               start=True, stop=False))
            ch.op("tensor", lambda t: t.matmul(out=pspos[:], lhsT=onesv[:], rhs=rbase[:],
                                               start=False, stop=True))
            ch.op("vector", lambda v: v.tensor_copy(out=posf[:], in_=pspos[:]))
            ch.op("vector", lambda v, cap=cap: v.tensor_scalar(
                out=valid[:], in0=posf[:], scalar1=float(cap), scalar2=None, op0=ALU.is_lt))
            ch.op("vector", lambda v: v.tensor_tensor(out=wmask[:], in0=w64[:], in1=valid[:],
                                                      op=ALU.mult))
            ch.op("tensor", lambda t: t.matmul(out=psts[:], lhsT=onesc128[:], rhs=amat[:],
                                               start=True, stop=True))
            ch.op("vector", lambda v: v.tensor_copy(out=tsrow[:], in_=psts[:]))
            ch.op("vector", lambda v: v.tensor_tensor(out=rbase[:], in0=rbase[:], in1=tsrow[:],
                                                      op=ALU.add))
            for r in range(K0):
                ch.op("vector", lambda v, r=r: v.tensor_scalar(
                    out=ohr[:], in0=iotaE[:], scalar1=mif[:, r:r + 1], scalar2=None,
                    op0=ALU.is_equal))
                ch.op("vector", lambda v: v.tensor_tensor(out=tmpe[:], in0=ohr[:], in1=posf[:],
                                                          op=ALU.mult))
                ch.op("vector", lambda v: v.tensor_reduce(out=posr[:], in_=tmpe[:],
                                                          axis=AX.X, op=ALU.add))
                ch.op("vector", lambda v: v.tensor_tensor(out=tmpe[:], in0=ohr[:], in1=wmask[:],
                                                          op=ALU.mult))
                ch.op("vector", lambda v: v.tensor_reduce(out=wr[:], in_=tmpe[:],
                                                          axis=AX.X, op=ALU.add))
                ch.op("vector", lambda v, cap=cap: v.tensor_scalar(
                    out=vr[:], in0=posr[:], scalar1=float(cap), scalar2=None, op0=ALU.is_lt))
                ch.op("vector", lambda v, r=r, cap=cap: v.tensor_scalar(
                    out=slotf[:], in0=mif[:, r:r + 1], scalar1=float(cap),
                    scalar2=posr[:, 0:1], op0=ALU.mult, op1=ALU.add))
                ch.op("vector", lambda v: v.tensor_scalar(
                    out=invf[:], in0=rowid[:], scalar1=float(NSLOT), scalar2=None,
                    op0=ALU.add))
                ch.op("vector", lambda v: v.select(out=ssel[:], mask=vr[:], on_true=slotf[:],
                                                   on_false=invf[:], add_drain=True))
                ch.op("vector", lambda v: v.tensor_copy(out=slotu[:], in_=ssel[:]))
                ch.op("vector", lambda v: v.select(out=gslotf[:], mask=vr[:], on_true=slotf[:],
                                                   on_false=zrow[:], add_drain=True))
                ch.op("vector", lambda v, r=r: v.tensor_copy(out=gslotu[:, r:r + 1], in_=gslotf[:]))
                ch.op("vector", lambda v: v.tensor_copy(out=comb[:, 0:D], in_=lnb[:]))
                ch.op("vector", lambda v: v.tensor_copy(out=comb[:, D:D + 1], in_=wr[:]))
                ch.dma_fn("gpsimd", lambda g: g.indirect_dma_start(
                    out=slotbuf_dram,
                    out_offset=bass.IndirectOffsetOnAxis(ap=slotu[:], axis=0),
                    in_=comb[:], in_offset=None))
            ch.dma("gpsimd", gidx_dram[r0:r0 + 128, :], gslotu[:])


GELU_C = 0.7978845608028654
GELU_A = 0.044715


def emit_gelu_tanh(nc, ch, gb, pin, out_bf, pre):
    g2, g3, inner, th = gb
    ch.op("scalar", lambda s: s.activation(out=g2[:], in_=pin[:], func=AF.Square))
    ch.op("vector", lambda v: v.tensor_tensor(out=g3[:], in0=g2[:], in1=pin[:], op=ALU.mult))
    ch.op("vector", lambda v: v.tensor_scalar(out=inner[:], in0=g3[:], scalar1=GELU_A,
                                              scalar2=None, op0=ALU.mult))
    ch.op("vector", lambda v: v.tensor_tensor(out=inner[:], in0=inner[:], in1=pin[:], op=ALU.add))
    ch.op("scalar", lambda s: s.activation(out=th[:], in_=inner[:], func=AF.Tanh, scale=GELU_C))
    ch.op("vector", lambda v: v.tensor_scalar(out=th[:], in0=th[:], scalar1=1.0, scalar2=0.5,
                                              op0=ALU.add, op1=ALU.mult))
    ch.op("vector", lambda v: v.tensor_tensor(out=out_bf[:], in0=th[:], in1=pin[:], op=ALU.mult))


def emit_l0ffn_experts(nc, ch, st, parts_dram, w1_dram, w2_dram, ys_dram, scratch):
    NR = 8 * CAP
    NTile = NR // 128
    xs_dram = scratch["xs"]
    with ExitStack() as st2:
        acc = alloc_sb(nc, st2, "l0e_acc", [128, ROWW])
        pb = alloc_sb(nc, st2, "l0e_pb", [128, ROWW], BF16)
        accb = alloc_sb(nc, st2, "l0e_accb", [128, D], BF16)
        wcol = alloc_sb(nc, st2, "l0e_wcol", [128, NTile])
        for s in range(NTile):
            r0 = s * 128
            for p in range(8):
                ch.dma("gpsimd", pb[:], parts_dram[p, r0:r0 + 128, :])
                if p == 0:
                    ch.op("vector", lambda v: v.tensor_copy(out=acc[:], in_=pb[:]))
                else:
                    ch.op("vector", lambda v: v.tensor_tensor(out=acc[:], in0=acc[:],
                                                              in1=pb[:], op=ALU.add))
            ch.op("vector", lambda v: v.tensor_copy(out=accb[:], in_=acc[:, 0:D]))
            ch.op("vector", lambda v, s=s: v.tensor_copy(out=wcol[:, s:s + 1],
                                                         in_=acc[:, D:D + 1]))
            ch.dma("gpsimd", xs_dram[r0:r0 + 128, :], accb[:])

        xsT = [alloc_sb(nc, st2, f"l0e_xsT{c}", [128, NR], BF16) for c in range(8)]
        for c in range(8):
            ch.dma_transpose(xsT[c][:], xs_dram[:, c * 128:(c + 1) * 128])

        w1b = [alloc_sb(nc, st2, f"l0e_w1b{c}", [128, 512], BF16) for c in range(8)]
        w2b = [alloc_sb(nc, st2, f"l0e_w2b{c}", [128, D], BF16) for c in range(4)]
        psh = alloc_ps(nc, st2, "l0e_psh", [128, CAP])
        hb = [alloc_sb(nc, st2, f"l0e_hb{i}", [128, CAP], BF16) for i in range(4)]
        g2 = alloc_sb(nc, st2, "l0e_g2", [128, CAP])
        g3 = alloc_sb(nc, st2, "l0e_g3", [128, CAP])
        inner = alloc_sb(nc, st2, "l0e_inner", [128, CAP])
        th = alloc_sb(nc, st2, "l0e_th", [128, CAP])
        psd = alloc_ps(nc, st2, "l0e_psd", [128, 512])
        yb = alloc_sb(nc, st2, "l0e_yb", [128, 512], BF16)

        for e in range(8):
            for c in range(8):
                ch.dma("gpsimd", w1b[c][:], w1_dram[e, c * 128:(c + 1) * 128, :])
            for c in range(4):
                ch.dma("gpsimd", w2b[c][:], w2_dram[e, c * 128:(c + 1) * 128, :])
            s0 = e * CAP
            for ic in range(4):
                for c in range(8):
                    ch.op("tensor", lambda t, c=c, ic=ic, s0=s0: t.matmul(
                        out=psh[:], lhsT=w1b[c][:, ic * 128:(ic + 1) * 128],
                        rhs=xsT[c][:, s0:s0 + CAP],
                        start=(c == 0), stop=(c == 7)))
                emit_gelu_tanh(nc, ch, (g2, g3, inner, th), psh, hb[ic], f"l0e_g{e}_{ic}")
            for stile in range(CAP // 128):
                for jc in range(2):
                    for ic in range(4):
                        ch.op("tensor", lambda t, ic=ic, stile=stile, jc=jc: t.matmul(
                            out=psd[:], lhsT=hb[ic][:, stile * 128:(stile + 1) * 128],
                            rhs=w2b[ic][:, jc * 512:(jc + 1) * 512],
                            start=(ic == 0), stop=(ic == 3)))
                    gt = e * (CAP // 128) + stile
                    ch.op("vector", lambda v, gt=gt: v.tensor_scalar(
                        out=yb[:], in0=psd[:], scalar1=wcol[:, gt:gt + 1], scalar2=None,
                        op0=ALU.mult))
                    ch.dma("gpsimd", ys_dram[s0 + stile * 128:s0 + (stile + 1) * 128,
                                             jc * 512:(jc + 1) * 512], yb[:])


def emit_l0ffn_combine(nc, ch, st, x1_dram, ln1_dram, gidx_dram, ysall_dram,
                       sw1_dram, sw2_dram, x2_dram, scratch):
    ysbuf = scratch["ysbuf"]
    y_dram = scratch["yffn"]
    sw1b_dram = scratch["sw1b"]
    sw2b_dram = scratch["sw2b"]
    hsT_dram = scratch["hsT"]

    with ExitStack() as st2:
        zb = alloc_sb(nc, st2, "l0c_zb", [128, D], BF16)
        ch.op("vector", lambda v: v.memset(zb[:], 0.0))
        ch.dma("scalar", ysbuf[0:NSLOT, :], ysall_dram)
        ch.dma("gpsimd", ysbuf[NSLOT:NSLOT + 1, :], zb[0:1, :])

        ch.dma("gpsimd", sw1b_dram, sw1_dram)
        ch.dma("gpsimd", sw2b_dram, sw2_dram)
        sw1T = [alloc_sb(nc, st2, f"l0c_sw1T{c}", [128, 2048], BF16) for c in range(8)]
        for c in range(8):
            ch.dma_transpose(sw1T[c][:], sw1b_dram[:, c * 128:(c + 1) * 128])

        lnT = [alloc_sb(nc, st2, f"l0c_lnT{c}", [128, T], BF16) for c in range(8)]
        emit_transpose_to_feat(nc, ch, ln1_dram, lnT)

        psh = alloc_ps(nc, st2, "l0c_psh", [128, 512])
        hbf = alloc_sb(nc, st2, "l0c_hbf", [128, 512], BF16)
        g2 = alloc_sb(nc, st2, "l0c_g2", [128, 512])
        g3 = alloc_sb(nc, st2, "l0c_g3", [128, 512])
        inner = alloc_sb(nc, st2, "l0c_inner", [128, 512])
        th = alloc_sb(nc, st2, "l0c_th", [128, 512])
        for fc in range(16):
            for t in range(T // 512):
                for c in range(8):
                    ch.op("tensor", lambda tt, c=c, fc=fc, t=t: tt.matmul(
                        out=psh[:], lhsT=sw1T[c][:, fc * 128:(fc + 1) * 128],
                        rhs=lnT[c][:, t * 512:(t + 1) * 512],
                        start=(c == 0), stop=(c == 7)))
                emit_gelu_tanh(nc, ch, (g2, g3, inner, th), psh, hbf, f"l0c_g{fc}_{t}")
                ch.dma("gpsimd", hsT_dram[fc * 128:(fc + 1) * 128, t * 512:(t + 1) * 512],
                       hbf[:])

    with ExitStack() as st2b:
        sw2T = [alloc_sb(nc, st2b, f"l0c_sw2T{c}", [128, D], BF16) for c in range(16)]
        for c in range(16):
            ch.dma_transpose(sw2T[c][:], sw2b_dram[:, c * 128:(c + 1) * 128])
        hstile = alloc_sb(nc, st2b, "l0c_hstile", [128, 16, 128], BF16)
        gix = alloc_sb(nc, st2b, "l0c_gix", [128, 6], U32)
        racc = alloc_sb(nc, st2b, "l0c_racc", [128, D])
        psd = alloc_ps(nc, st2b, "l0c_psd", [128, 512])
        ycmb = alloc_sb(nc, st2b, "l0c_ycmb", [128, D])
        for ti in range(T // 128):
            r0 = ti * 128
            ch.dma("gpsimd", hstile[:],
                   hsT_dram[:, r0:r0 + 128].rearrange("(c p) t -> p c t", p=128))
            ch.dma("gpsimd", gix[:], gidx_dram[r0:r0 + 128, :])
            ch.op("vector", lambda v: v.memset(racc[:], 0.0))
            for r in range(K0):
                ch.dma_fn("gpsimd", lambda g, r=r: g.indirect_dma_start(
                    out=racc[:], out_offset=None, in_=ysbuf,
                    in_offset=bass.IndirectOffsetOnAxis(ap=gix[:, r:r + 1], axis=0),
                    compute_op=ALU.add))
            for jc in range(2):
                for c in range(16):
                    ch.op("tensor", lambda tt, c=c, jc=jc: tt.matmul(
                        out=psd[:], lhsT=hstile[:, c, :],
                        rhs=sw2T[c][:, jc * 512:(jc + 1) * 512],
                        start=(c == 0), stop=(c == 15)))
                ch.op("vector", lambda v, jc=jc: v.tensor_tensor(
                    out=ycmb[:, jc * 512:(jc + 1) * 512], in0=psd[:],
                    in1=racc[:, jc * 512:(jc + 1) * 512], op=ALU.add))
            ch.dma("gpsimd", y_dram[r0:r0 + 128, :], ycmb[:])

    with ExitStack() as st3:
        emit_ln_tok(nc, ch, st3, "l0c_ln2", y_dram, None, res_add_dram=x1_dram,
                    x_out_dram=x2_dram)



def _mk(nc, name, shape, dt, kind=None):
    if kind:
        return nc.dram_tensor(name, shape, dt, kind=kind).ap()
    return nc.dram_tensor(name, shape, dt).ap()


def build_launch_l0ffn_route():
    nc = bass.Bass("TRN2", target_bir_lowering=False)
    x1 = _mk(nc, "x1", [T, D], F32, "ExternalInput")
    rw = _mk(nc, "rw", [E0, D], F32, "ExternalInput")
    eb = _mk(nc, "eb", [E0], F32, "ExternalInput")
    ln1 = _mk(nc, "ln1", [T, D], BF16, "ExternalOutput")
    idx6 = _mk(nc, "idx6", [T, 8], U32, "ExternalOutput")
    w64 = _mk(nc, "w64", [T, E0], F32, "ExternalOutput")
    totals = _mk(nc, "totals", [E0, 1], F32, "ExternalOutput")
    scratch = {"rwb": _mk(nc, "s_rwb", [E0, D], BF16)}
    ch = Chain(nc)
    with ExitStack() as st:
        ch.make_sems(st)
        block = st.enter_context(nc.Block())
        emit_l0ffn_routing(nc, ch, st, x1, rw, eb, ln1, idx6, w64, totals, scratch)
        ch.emit(block)
    return nc


def build_launch_l0ffn_scatter():
    nc = bass.Bass("TRN2", target_bir_lowering=False)
    ln1 = _mk(nc, "ln1", [T, D], BF16, "ExternalInput")
    idx6 = _mk(nc, "idx6", [T, 8], U32, "ExternalInput")
    w64 = _mk(nc, "w64", [T, E0], F32, "ExternalInput")
    totals_all = _mk(nc, "totals_all", [NCORES, E0], F32, "ExternalInput")
    prevmask = _mk(nc, "prevmask", [NCORES, 1], F32, "ExternalInput")
    slotbuf = _mk(nc, "slotbuf", [NSLOT + 128, ROWW], BF16, "ExternalOutput")
    gidx = _mk(nc, "gidx", [T, 6], U32, "ExternalOutput")
    ch = Chain(nc)
    with ExitStack() as st:
        ch.make_sems(st)
        block = st.enter_context(nc.Block())
        zt = alloc_sb(nc, st, "l0s_zt", [128, ROWW], BF16)
        ch.op("vector", lambda v: v.memset(zt[:], 0.0))
        nrep = NSLOT // 128 + 1
        half = (nrep + 1) // 2
        for hh in range(2):
            lo, hi = hh * half, min((hh + 1) * half, nrep)
            ch.dma("gpsimd",
                   slotbuf[lo * 128:hi * 128, :]
                   .rearrange("(n p) w -> p n w", p=128),
                   zt[:].unsqueeze(1).broadcast_to((128, hi - lo, ROWW)))
        emit_l0ffn_scatter(nc, ch, st, ln1, idx6, w64, totals_all, prevmask, slotbuf, gidx)
        ch.emit(block)
    return nc


def build_launch_l0ffn_experts():
    nc = bass.Bass("TRN2", target_bir_lowering=False)
    parts = _mk(nc, "parts", [NCORES, 8 * CAP, ROWW], BF16, "ExternalInput")
    w1 = _mk(nc, "w1", [8, D, 512], F32, "ExternalInput")
    w2 = _mk(nc, "w2", [8, 512, D], F32, "ExternalInput")
    ys = _mk(nc, "ys", [8 * CAP, D], BF16, "ExternalOutput")
    scratch = {"xs": _mk(nc, "s_xs", [8 * CAP, D], BF16)}
    ch = Chain(nc)
    with ExitStack() as st:
        ch.make_sems(st)
        block = st.enter_context(nc.Block())
        emit_l0ffn_experts(nc, ch, st, parts, w1, w2, ys, scratch)
        ch.emit(block)
    return nc


def build_launch_l0ffn_combine():
    nc = bass.Bass("TRN2", target_bir_lowering=False)
    x1 = _mk(nc, "x1", [T, D], F32, "ExternalInput")
    ln1 = _mk(nc, "ln1", [T, D], BF16, "ExternalInput")
    gidx = _mk(nc, "gidx", [T, 6], U32, "ExternalInput")
    ysall = _mk(nc, "ysall", [NSLOT, D], BF16, "ExternalInput")
    sw1 = _mk(nc, "sw1", [2048, D], F32, "ExternalInput")
    sw2 = _mk(nc, "sw2", [D, 2048], F32, "ExternalInput")
    x2 = _mk(nc, "x2", [T, D], F32, "ExternalOutput")
    scratch = {
        "ysbuf": _mk(nc, "s_ysbuf", [NSLOT + 1, D], BF16),
        "yffn": _mk(nc, "s_yffn", [T, D], F32),
        "sw1b": _mk(nc, "s_sw1b", [2048, D], BF16),
        "sw2b": _mk(nc, "s_sw2b", [D, 2048], BF16),
        "hsT": _mk(nc, "s_hsT", [2048, T], BF16),
    }
    ch = Chain(nc)
    with ExitStack() as st:
        ch.make_sems(st)
        block = st.enter_context(nc.Block())
        emit_l0ffn_combine(nc, ch, st, x1, ln1, gidx, ysall, sw1, sw2, x2, scratch)
        ch.emit(block)
    return nc


LCH = 256


def emit_l1_attn(nc, ch, st, x2_dram, wqkv_dram, bqkv_dram, wout_dram, bout_dram,
                 x3_dram, scratch):
    ln_dram = scratch["l1ln"]
    pqT_dram = scratch["pqT"]
    pkT_dram = scratch["pkT"]
    ktok_dram = scratch["ktok"]
    v_dram = scratch["l1v"]
    oT_dram = scratch["l1oT"]
    y_dram = scratch["l1y"]

    with ExitStack() as st1:
        emit_ln_tok(nc, ch, st1, "l1a_ln1", x2_dram, ln_dram)

    with ExitStack() as st2:
        xT = [alloc_sb(nc, st2, f"l1a_xT{c}", [128, T], BF16) for c in range(8)]
        emit_transpose_to_feat(nc, ch, ln_dram, xT)
        wat = [alloc_sb(nc, st2, f"l1a_w{c}", [128, 3072], BF16) for c in range(8)]
        for c in range(8):
            ch.dma("gpsimd", wat[c][:], wqkv_dram[c * 128:(c + 1) * 128, :])
        bcol = [alloc_sb(nc, st2, f"l1a_bcol{b}", [128, 1]) for b in range(2)]
        bcol8 = [alloc_sb(nc, st2, f"l1a_bcol8{b}", [128, 1]) for b in range(2)]
        ps = [alloc_ps(nc, st2, f"l1a_ps{b}", [128, 512]) for b in range(2)]
        qf = [alloc_sb(nc, st2, f"l1a_qf{b}", [128, 512]) for b in range(2)]
        zneg = [alloc_sb(nc, st2, f"l1a_zneg{b}", [128, 512]) for b in range(2)]
        zexp = [alloc_sb(nc, st2, f"l1a_zexp{b}", [128, 512]) for b in range(2)]
        zpos = [alloc_sb(nc, st2, f"l1a_zpos{b}", [128, 512]) for b in range(2)]
        phib = [alloc_sb(nc, st2, f"l1a_phib{b}", [128, 512], BF16) for b in range(2)]
        evb = [alloc_sb(nc, st2, f"l1a_evb{b}", [128, 512], BF16) for b in range(2)]

        def _phi_col(jt, which, col0, dstT, j):
            scale = 0.125 if which == "q" else 1.0
            jb = jt % 2
            ch.dma("gpsimd", bcol[jb][:], bqkv_dram[col0 + j * 128:col0 + (j + 1) * 128]
                   .rearrange("(p o) -> p o", o=1))
            ch.op("vector", lambda v: v.tensor_scalar(
                out=bcol8[jb][:], in0=bcol[jb][:], scalar1=scale, scalar2=None, op0=ALU.mult))

            def _phi_tile(it, t):
                b = it % 2
                psb_, qfb, znegb, zexpb, zposb, phibb = \
                    ps[b], qf[b], zneg[b], zexp[b], zpos[b], phib[b]
                for c in range(8):
                    ch.op("tensor", lambda tt, c=c: tt.matmul(
                        out=psb_[:], lhsT=wat[c][:, col0 + j * 128:col0 + (j + 1) * 128],
                        rhs=xT[c][:, t * 512:(t + 1) * 512],
                        start=(c == 0), stop=(c == 7)))
                ch.op("vector", lambda v: v.tensor_scalar(
                    out=qfb[:], in0=psb_[:], scalar1=scale, scalar2=bcol8[jb][:, 0:1],
                    op0=ALU.mult, op1=ALU.add))
                ch.op("vector", lambda v: v.tensor_scalar(
                    out=znegb[:], in0=qfb[:], scalar1=0.0, scalar2=None, op0=ALU.min))
                ch.op("scalar", lambda s: s.activation(out=zexpb[:], in_=znegb[:], func=AF.Exp))
                ch.op("vector", lambda v: v.tensor_scalar(
                    out=zposb[:], in0=qfb[:], scalar1=0.0, scalar2=None, op0=ALU.max))
                ch.op("vector", lambda v: v.scalar_tensor_tensor(
                    out=phibb[:], in0=zposb[:], scalar=1e-4, in1=zexpb[:],
                    op0=ALU.add, op1=ALU.add))
                ch.dma("gpsimd", dstT[j * 128:(j + 1) * 128, t * 512:(t + 1) * 512],
                       phibb[:])

            for t in range(T // 512):
                _phi_tile(jt * (T // 512) + t, t)

        jt = 0
        for (which, col0, dstT) in (("q", 0, pqT_dram), ("k", D, pkT_dram)):
            for j in range(8):
                _phi_col(jt, which, col0, dstT, j)
                jt += 1

        vbrow = alloc_sb(nc, st2, "l1a_vbrow", [1, D])
        onesr = alloc_sb(nc, st2, "l1a_onesr", [1, 128])
        psvb = alloc_ps(nc, st2, "l1a_psvb", [128, 512])
        vbias = alloc_sb(nc, st2, "l1a_vbias", [128, D])
        ch.dma("gpsimd", vbrow[:], bqkv_dram[2 * D:3 * D].rearrange("(o d) -> o d", o=1))
        ch.op("vector", lambda v: v.memset(onesr[:], 1.0))
        for g in range(2):
            ch.op("tensor", lambda t, g=g: t.matmul(
                out=psvb[:], lhsT=onesr[:], rhs=vbrow[:, g * 512:(g + 1) * 512],
                start=True, stop=True))
            ch.op("vector", lambda v, g=g: v.tensor_copy(
                out=vbias[:, g * 512:(g + 1) * 512], in_=psvb[:]))
        def _v_tile(it, g, ti):
            b = it % 2
            for c in range(8):
                ch.op("tensor", lambda tt, c=c: tt.matmul(
                    out=ps[b][:], lhsT=xT[c][:, ti * 128:(ti + 1) * 128],
                    rhs=wat[c][:, 2 * D + g * 512: 2 * D + (g + 1) * 512],
                    start=(c == 0), stop=(c == 7)))
            ch.op("vector", lambda v: v.tensor_tensor(
                out=qf[b][:], in0=ps[b][:], in1=vbias[:, g * 512:(g + 1) * 512], op=ALU.add))
            ch.op("vector", lambda v: v.tensor_copy(out=evb[b][:], in_=qf[b][:]))
            ch.dma("gpsimd", v_dram[ti, 8 * g:8 * (g + 1), :, :].rearrange("h p c -> p h c"),
                   evb[b][:])

        it = 0
        for g in range(2):
            for ti in range(T // 128):
                _v_tile(it, g, ti)
                it += 1

        ktb = [alloc_sb(nc, st2, f"l1a_ktb{b}", [128, D], BF16) for b in range(2)]
        for ti in range(T // 128):
            ch.dma_transpose(ktb[ti % 2][:], pkT_dram[:, ti * 128:(ti + 1) * 128])
            ch.dma("gpsimd", ktok_dram[ti * 128:(ti + 1) * 128, :], ktb[ti % 2][:])

    with ExitStack() as st3:
        NCH = T // LCH
        SPC = LCH // 128
        pq = [alloc_sb(nc, st3, f"l1a_pq{b}", [64, T], BF16) for b in range(4)]
        pk = [alloc_sb(nc, st3, f"l1a_pk{b}", [64, T], BF16) for b in range(4)]
        vt = [alloc_sb(nc, st3, f"l1a_vt{b}", [128, (T // 128) * 65], BF16) for b in range(4)]
        kt = [alloc_sb(nc, st3, f"l1a_kt{b}", [128, (T // 128) * 64], BF16) for b in range(4)]
        kv = [alloc_sb(nc, st3, f"l1a_kv{j}", [64, 65]) for j in range(2)]
        kvb = [alloc_sb(nc, st3, f"l1a_kvb{j}", [64, 65], BF16) for j in range(2)]
        masks = alloc_sb(nc, st3, "l1a_masks", [128, SPC * LCH], BF16)
        colid = alloc_sb(nc, st3, "l1a_colid", [128, LCH])
        rowid = alloc_sb(nc, st3, "l1a_rowid", [128, 1])
        psa = [alloc_ps(nc, st3, f"l1a_psa{b}", [128, LCH]) for b in range(2)]
        ea = [[alloc_sb(nc, st3, f"l1a_ea{j}_{i}", [128, LCH], BF16) for i in range(SPC)]
              for j in range(2)]
        pso = [alloc_ps(nc, st3, f"l1a_pso{b}", [65, LCH]) for b in range(2)]
        pskv = [alloc_ps(nc, st3, f"l1a_pskv{b}", [64, 65]) for b in range(2)]
        rd = [alloc_sb(nc, st3, f"l1a_rd{b}", [1, LCH]) for b in range(2)]
        ones64 = alloc_sb(nc, st3, "l1a_ones64", [1, 64])
        psbc = [alloc_ps(nc, st3, f"l1a_psbc{b}", [64, LCH]) for b in range(2)]
        bcb = [alloc_sb(nc, st3, f"l1a_bcb{b}", [64, LCH]) for b in range(2)]
        obn = [alloc_sb(nc, st3, f"l1a_obn{b}", [64, LCH], BF16) for b in range(2)]

        ch.op("gpsimd", lambda g: g.iota(colid[:], pattern=[[1, LCH]], base=0,
                                         channel_multiplier=0,
                                         allow_small_or_imprecise_dtypes=True))
        ch.op("gpsimd", lambda g: g.iota(rowid[:], pattern=[[0, 1]], base=0,
                                         channel_multiplier=1,
                                         allow_small_or_imprecise_dtypes=True))
        for i in range(SPC):
            ch.op("vector", lambda v, i=i: v.tensor_scalar(
                out=masks[:, i * LCH:(i + 1) * LCH], in0=colid[:],
                scalar1=float(128 * i), scalar2=rowid[:, 0:1],
                op0=ALU.subtract, op1=ALU.is_ge))
        ch.op("vector", lambda v: v.memset(ones64[:], 1.0))

        def _load_head(h):
            hb_ = h % 4
            pqb, pkb, vtb, ktb2 = pq[hb_], pk[hb_], vt[hb_], kt[hb_]
            ch.dma("scalar", pqb[:], pqT_dram[h * 64:(h + 1) * 64, :])
            ch.dma("scalar", pkb[:], pkT_dram[h * 64:(h + 1) * 64, :])
            ch.op("vector", lambda v: v.memset(vtb[:], 1.0))
            ch.dma("scalar", vtb[:].rearrange("p (k c) -> p k c", c=65)[:, :, 0:64],
                   v_dram[:, h, :, :].rearrange("k p c -> p k c"))
            ch.dma("scalar", ktb2[:].rearrange("p (k c) -> p k c", c=64),
                   ktok_dram.rearrange("(k p) (hh c) -> p k hh c", p=128, c=64)[:, :, h, :])

        def _chunk(h, hh, cc):
            hb_ = h % 4
            pqb, pkb, vtb, ktb2 = pq[hb_], pk[hb_], vt[hb_], kt[hb_]
            kvh, kvbh, eah = kv[hh], kvb[hh], ea[hh]
            q0 = cc * LCH
            for i in range(SPC):
                s0 = (cc * SPC + i) * 128
                pa = psa[i % 2]
                ch.op("tensor", lambda tt, s0=s0, pa=pa, pkb=pkb, pqb=pqb, q0=q0: tt.matmul(
                    out=pa[:], lhsT=pkb[:, s0:s0 + 128], rhs=pqb[:, q0:q0 + LCH],
                    start=True, stop=True))
                ch.op("vector", lambda v, i=i, pa=pa, eah=eah: v.tensor_tensor(
                    out=eah[i][:], in0=pa[:], in1=masks[:, i * LCH:(i + 1) * LCH],
                    op=ALU.mult))
            for i in range(SPC):
                kti = cc * SPC + i
                ch.op("tensor", lambda tt, i=i, kti=kti, vtb=vtb, eah=eah: tt.matmul(
                    out=pso[hh][:], lhsT=vtb[:, kti * 65:(kti + 1) * 65], rhs=eah[i][:],
                    start=(i == 0), stop=False))
            ch.op("tensor", lambda tt, kvbh=kvbh, pqb=pqb, q0=q0: tt.matmul(
                out=pso[hh][:], lhsT=kvbh[:], rhs=pqb[:, q0:q0 + LCH],
                start=False, stop=True))
            ch.op("vector", lambda v: v.tensor_scalar(
                out=rd[hh][:], in0=pso[hh][64:65, :], scalar1=1e-6, scalar2=None,
                op0=ALU.add))
            ch.op("tensor", lambda tt: tt.matmul(out=psbc[hh][:], lhsT=ones64[:],
                                                 rhs=rd[hh][:], start=True, stop=True))
            ch.op("vector", lambda v: v.reciprocal(out=bcb[hh][:], in_=psbc[hh][:]))
            ch.op("vector", lambda v: v.tensor_tensor(out=obn[hh][:], in0=pso[hh][0:64, :],
                                                      in1=bcb[hh][:], op=ALU.mult))
            ch.dma("gpsimd", oT_dram[h * 64:(h + 1) * 64, q0:q0 + LCH], obn[hh][:])
            for i in range(SPC):
                kti = cc * SPC + i
                ch.op("tensor", lambda tt, i=i, kti=kti, ktb2=ktb2, vtb=vtb: tt.matmul(
                    out=pskv[hh][:], lhsT=ktb2[:, kti * 64:(kti + 1) * 64],
                    rhs=vtb[:, kti * 65:(kti + 1) * 65],
                    start=(i == 0), stop=(i == SPC - 1)))
            ch.op("vector", lambda v, kvh=kvh: v.tensor_tensor(out=kvh[:], in0=kvh[:],
                                                               in1=pskv[hh][:], op=ALU.add))
            ch.op("gpsimd", lambda g, kvh=kvh, kvbh=kvbh: g.tensor_copy(out=kvbh[:],
                                                                        in_=kvh[:]))

        for hp in range(NH // 2):
            h0 = hp * 2
            _load_head(h0)
            _load_head(h0 + 1)
            for j in range(2):
                ch.op("vector", lambda v, j=j: v.memset(kv[j][:], 0.0))
                ch.op("vector", lambda v, j=j: v.memset(kvb[j][:], 0.0))
            for cc in range(NCH):
                _chunk(h0, 0, cc)
                _chunk(h0 + 1, 1, cc)

    with ExitStack() as st4:
        oT = [alloc_sb(nc, st4, f"l1a_oT{c}", [128, T], BF16) for c in range(8)]
        for c in range(8):
            ch.dma("scalar", oT[c][:], oT_dram[c * 128:(c + 1) * 128, :])
        wp = [alloc_sb(nc, st4, f"l1a_wp{c}", [128, D], BF16) for c in range(8)]
        for c in range(8):
            ch.dma("gpsimd", wp[c][:], wout_dram[c * 128:(c + 1) * 128, :])
        borow = alloc_sb(nc, st4, "l1a_borow", [1, D])
        ones_r = alloc_sb(nc, st4, "l1a_onesr2", [1, 128])
        psbb = alloc_ps(nc, st4, "l1a_psbb", [128, 512])
        bob = alloc_sb(nc, st4, "l1a_bob", [128, D])
        ch.dma("gpsimd", borow[:], bout_dram.rearrange("(o d) -> o d", o=1))
        ch.op("vector", lambda v: v.memset(ones_r[:], 1.0))
        for jc in range(2):
            ch.op("tensor", lambda t, jc=jc: t.matmul(
                out=psbb[:], lhsT=ones_r[:], rhs=borow[:, jc * 512:(jc + 1) * 512],
                start=True, stop=True))
            ch.op("vector", lambda v, jc=jc: v.tensor_copy(
                out=bob[:, jc * 512:(jc + 1) * 512], in_=psbb[:]))
        ps2 = [alloc_ps(nc, st4, f"l1a_ps2{b}", [128, 512]) for b in range(2)]
        yb = [alloc_sb(nc, st4, f"l1a_yb{b}", [128, 512]) for b in range(2)]

        def _oproj(it, ti, jc):
            p, y = ps2[it % 2], yb[it % 2]
            for c in range(8):
                ch.op("tensor", lambda tt, c=c: tt.matmul(
                    out=p[:], lhsT=oT[c][:, ti * 128:(ti + 1) * 128],
                    rhs=wp[c][:, jc * 512:(jc + 1) * 512],
                    start=(c == 0), stop=(c == 7)))
            ch.op("vector", lambda v: v.tensor_tensor(
                out=y[:], in0=p[:], in1=bob[:, jc * 512:(jc + 1) * 512], op=ALU.add))
            ch.dma("gpsimd", y_dram[ti * 128:(ti + 1) * 128,
                                    jc * 512:(jc + 1) * 512], y[:])

        it = 0
        for ti in range(T // 128):
            for jc in range(2):
                _oproj(it, ti, jc)
                it += 1

    with ExitStack() as st5:
        emit_ln_tok(nc, ch, st5, "l1a_ln2", y_dram, None, res_add_dram=x2_dram,
                    x_out_dram=x3_dram)


def build_launch_l1attn():
    nc = bass.Bass("TRN2", target_bir_lowering=False)
    x2 = _mk(nc, "x2", [T, D], F32, "ExternalInput")
    wqkv = _mk(nc, "wqkv", [D, 3 * D], F32, "ExternalInput")
    bqkv = _mk(nc, "bqkv", [3 * D], F32, "ExternalInput")
    wout = _mk(nc, "wout", [D, D], F32, "ExternalInput")
    bout = _mk(nc, "bout", [D], F32, "ExternalInput")
    x3 = _mk(nc, "x3", [T, D], F32, "ExternalOutput")
    scratch = {
        "l1ln": _mk(nc, "s_l1ln", [T, D], BF16),
        "pqT": _mk(nc, "s_pqT", [D, T], BF16),
        "pkT": _mk(nc, "s_pkT", [D, T], BF16),
        "ktok": _mk(nc, "s_ktok", [T, D], BF16),
        "l1v": _mk(nc, "s_l1v", [T // 128, 16, 128, 64], BF16),
        "l1oT": _mk(nc, "s_l1oT", [D, T], BF16),
        "l1y": _mk(nc, "s_l1y", [T, D], F32),
    }
    ch = Chain(nc)
    with ExitStack() as st:
        ch.make_sems(st)
        block = st.enter_context(nc.Block())
        emit_l1_attn(nc, ch, st, x2, wqkv, bqkv, wout, bout, x3, scratch)
        ch.emit(block)
    return nc


E1 = 8
K1 = 4


def emit_l1_ffn(nc, ch, st, x3_dram, rw_dram, rb_dram, gw_dram, uw_dram, dw_dram,
                sgw_dram, suw_dram, sdw_dram, xout_dram, scratch):
    ln_dram = scratch["l2ln"]
    actT_dram = scratch["actT"]
    hsT_dram = scratch["hsT2"]
    y_dram = scratch["l2y"]

    w8all = alloc_sb(nc, st, "l2_w8all", [128, (T // 128) * E1])

    with ExitStack() as st1:
        emit_ln_tok(nc, ch, st1, "l2_ln1", x3_dram, ln_dram)

    with ExitStack() as st2:
        lnT = [alloc_sb(nc, st2, f"l2_lnT{c}", [128, T], BF16) for c in range(8)]
        emit_transpose_to_feat(nc, ch, ln_dram, lnT)

        rwb = [alloc_sb(nc, st2, f"l2_rwb{c}", [128, E1], BF16) for c in range(8)]
        for c in range(8):
            ch.dma("gpsimd", rwb[c][:], rw_dram[c * 128:(c + 1) * 128, :])
        psr = [alloc_ps(nc, st2, f"l2_psr{b}", [128, E1]) for b in range(2)]
        nmx = [alloc_sb(nc, st2, f"l2_nmx{b}", [128, 1]) for b in range(2)]
        exr = [alloc_sb(nc, st2, f"l2_exr{b}", [128, E1]) for b in range(2)]
        ssum = [alloc_sb(nc, st2, f"l2_ssum{b}", [128, 1]) for b in range(2)]
        srec = [alloc_sb(nc, st2, f"l2_srec{b}", [128, 1]) for b in range(2)]
        scr = [alloc_sb(nc, st2, f"l2_scr{b}", [128, E1]) for b in range(2)]
        mx8 = [alloc_sb(nc, st2, f"l2_mx8{b}", [128, 8]) for b in range(2)]
        mi8 = [alloc_sb(nc, st2, f"l2_mi8{b}", [128, 8], U32) for b in range(2)]
        mif = [alloc_sb(nc, st2, f"l2_mif{b}", [128, 8]) for b in range(2)]
        w4s = [alloc_sb(nc, st2, f"l2_w4s{b}", [128, 1]) for b in range(2)]
        w4r = [alloc_sb(nc, st2, f"l2_w4r{b}", [128, 1]) for b in range(2)]
        w4n = [alloc_sb(nc, st2, f"l2_w4n{b}", [128, K1]) for b in range(2)]
        iota8 = alloc_sb(nc, st2, "l2_iota8", [128, E1])
        rb1 = alloc_sb(nc, st2, "l2_rb1", [1, 1])
        onesc = alloc_sb(nc, st2, "l2_onesc", [1, 128])
        psrb = alloc_ps(nc, st2, "l2_psrb", [128, 1])
        rbB = alloc_sb(nc, st2, "l2_rbB", [128, 1])
        corr8 = [alloc_sb(nc, st2, f"l2_corr8{b}", [128, E1]) for b in range(2)]
        ch.dma("gpsimd", rb1[:], rb_dram)
        ch.op("vector", lambda v: v.memset(onesc[:], 1.0))
        ch.op("tensor", lambda t: t.matmul(out=psrb[:], lhsT=onesc[:], rhs=rb1[:],
                                           start=True, stop=True))
        ch.op("vector", lambda v: v.tensor_copy(out=rbB[:], in_=psrb[:]))
        ohw = [alloc_sb(nc, st2, f"l2_ohw{b}", [128, E1]) for b in range(2)]
        w8 = [alloc_sb(nc, st2, f"l2_w8{b}", [128, E1]) for b in range(2)]
        ch.op("gpsimd", lambda g: g.iota(iota8[:], pattern=[[1, E1]], base=0,
                                         channel_multiplier=0,
                                         allow_small_or_imprecise_dtypes=True))

        def _router_tile(ti, psr, nmx, exr, ssum, srec, scr, mx8, mi8, mif,
                         w4s, w4r, w4n, corr8, ohw, w8):
            for c in range(8):
                ch.op("tensor", lambda t, c=c: t.matmul(
                    out=psr[:], lhsT=lnT[c][:, ti * 128:(ti + 1) * 128], rhs=rwb[c][:],
                    start=(c == 0), stop=(c == 7)))
            ch.op("vector", lambda v: v.tensor_reduce(out=nmx[:], in_=psr[:],
                                                      axis=AX.X, op=ALU.max, negate=True))
            ch.op("scalar", lambda s: s.activation(out=exr[:], in_=psr[:], func=AF.Exp,
                                                   bias=nmx[:, 0:1]))
            ch.op("vector", lambda v: v.tensor_reduce(out=ssum[:], in_=exr[:],
                                                      axis=AX.X, op=ALU.add))
            ch.op("vector", lambda v: v.reciprocal(out=srec[:], in_=ssum[:]))
            ch.op("vector", lambda v: v.tensor_scalar(out=scr[:], in0=exr[:],
                                                      scalar1=srec[:, 0:1], scalar2=None,
                                                      op0=ALU.mult))
            ch.op("vector", lambda v: v.tensor_scalar(out=corr8[:], in0=scr[:],
                                                      scalar1=rbB[:, 0:1], scalar2=None,
                                                      op0=ALU.add))
            ch.op("vector", lambda v: v.max(out=mx8[:], in_=corr8[:]))
            ch.op("vector", lambda v: v.max_index(out=mi8[:], in_max=mx8[:], in_values=corr8[:]))
            ch.op("vector", lambda v: v.tensor_copy(out=mif[:], in_=mi8[:]))
            ch.op("vector", lambda v: v.tensor_scalar(out=mx8[:], in0=mx8[:],
                                                      scalar1=rbB[:, 0:1], scalar2=None,
                                                      op0=ALU.subtract))
            ch.op("vector", lambda v: v.tensor_reduce(out=w4s[:], in_=mx8[:, 0:K1],
                                                      axis=AX.X, op=ALU.add))
            ch.op("vector", lambda v: v.tensor_scalar(out=w4s[:], in0=w4s[:],
                                                      scalar1=1e-20, scalar2=None, op0=ALU.add))
            ch.op("vector", lambda v: v.reciprocal(out=w4r[:], in_=w4s[:]))
            ch.op("vector", lambda v: v.tensor_scalar(out=w4n[:], in0=mx8[:, 0:K1],
                                                      scalar1=w4r[:, 0:1], scalar2=None,
                                                      op0=ALU.mult))
            for r in range(K1):
                if r == 0:
                    ch.op("vector", lambda v, r=r: v.tensor_scalar(
                        out=w8[:], in0=iota8[:], scalar1=mif[:, r:r + 1],
                        scalar2=w4n[:, r:r + 1], op0=ALU.is_equal, op1=ALU.mult))
                else:
                    ch.op("vector", lambda v, r=r: v.tensor_scalar(
                        out=ohw[:], in0=iota8[:], scalar1=mif[:, r:r + 1],
                        scalar2=w4n[:, r:r + 1], op0=ALU.is_equal, op1=ALU.mult))
                    ch.op("vector", lambda v: v.tensor_tensor(out=w8[:], in0=w8[:],
                                                              in1=ohw[:], op=ALU.add))
            ch.op("vector", lambda v, ti=ti: v.tensor_copy(
                out=w8all[:, ti * E1:(ti + 1) * E1], in_=w8[:]))

        for ti in range(T // 128):
            b = ti % 2
            _router_tile(ti, psr[b], nmx[b], exr[b], ssum[b], srec[b], scr[b],
                         mx8[b], mi8[b], mif[b], w4s[b], w4r[b], w4n[b],
                         corr8[b], ohw[b], w8[b])

        gwb = [[alloc_sb(nc, st2, f"l2_gwb{b}_{c}", [128, 512], BF16) for c in range(8)]
               for b in range(2)]
        uwb = [[alloc_sb(nc, st2, f"l2_uwb{b}_{c}", [128, 512], BF16) for c in range(8)]
               for b in range(2)]
        psg = [alloc_ps(nc, st2, f"l2_psg{b}", [128, 512]) for b in range(2)]
        psu = [alloc_ps(nc, st2, f"l2_psu{b}", [128, 512]) for b in range(2)]
        sg = [alloc_sb(nc, st2, f"l2_sg{b}", [128, 512]) for b in range(2)]
        actb = [alloc_sb(nc, st2, f"l2_actb{b}", [128, 512], BF16) for b in range(2)]

        def _gu_tile(it, gw8, uw8, co, t, dst_dram, dst_row):
            b = it % 2
            for c in range(8):
                ch.op("tensor", lambda tt, c=c: tt.matmul(
                    out=psg[b][:], lhsT=gw8[c][:, co:co + 128],
                    rhs=lnT[c][:, t * 512:(t + 1) * 512],
                    start=(c == 0), stop=(c == 7)))
            for c in range(8):
                ch.op("tensor", lambda tt, c=c: tt.matmul(
                    out=psu[b][:], lhsT=uw8[c][:, co:co + 128],
                    rhs=lnT[c][:, t * 512:(t + 1) * 512],
                    start=(c == 0), stop=(c == 7)))
            ch.op("scalar", lambda s: s.activation(out=sg[b][:], in_=psg[b][:], func=AF.Silu))
            ch.op("vector", lambda v: v.tensor_tensor(out=actb[b][:], in0=sg[b][:],
                                                      in1=psu[b][:], op=ALU.mult))
            ch.dma("gpsimd", dst_dram[dst_row * 128:(dst_row + 1) * 128,
                                      t * 512:(t + 1) * 512], actb[b][:])

        it = 0
        for e in range(E1):
            eb_ = e % 2
            for c in range(8):
                ch.dma("gpsimd", gwb[eb_][c][:], gw_dram[e, c * 128:(c + 1) * 128, :])
                ch.dma("gpsimd", uwb[eb_][c][:], uw_dram[e, c * 128:(c + 1) * 128, :])
            for ic in range(4):
                for t in range(T // 512):
                    _gu_tile(it, gwb[eb_], uwb[eb_], ic * 128, t, actT_dram[e], ic)
                    it += 1

        sgw = [alloc_sb(nc, st2, f"l2_sgw{c}", [128, 2048], BF16) for c in range(8)]
        suw = [alloc_sb(nc, st2, f"l2_suw{c}", [128, 2048], BF16) for c in range(8)]
        for c in range(8):
            ch.dma("gpsimd", sgw[c][:], sgw_dram[c * 128:(c + 1) * 128, :])
            ch.dma("gpsimd", suw[c][:], suw_dram[c * 128:(c + 1) * 128, :])
        for mc in range(16):
            for t in range(T // 512):
                _gu_tile(it, sgw, suw, mc * 128, t, hsT_dram, mc)
                it += 1

    with ExitStack() as st3:
        dwb = [[alloc_sb(nc, st3, f"l2_dwb{e}_{c}", [128, D], BF16) for c in range(4)]
               for e in range(E1)]
        for e in range(E1):
            for c in range(4):
                ch.dma("gpsimd", dwb[e][c][:], dw_dram[e, c * 128:(c + 1) * 128, :])
        sdw = [alloc_sb(nc, st3, f"l2_sdw{c}", [128, D], BF16) for c in range(16)]
        for c in range(16):
            ch.dma("gpsimd", sdw[c][:], sdw_dram[c * 128:(c + 1) * 128, :])
        atile = [alloc_sb(nc, st3, f"l2_atile{b}", [128, E1, 4, 128], BF16) for b in range(2)]
        hstile = [alloc_sb(nc, st3, f"l2_hstile{b}", [128, 16, 128], BF16) for b in range(2)]
        psd = [alloc_ps(nc, st3, f"l2_psd{b}", [128, 512]) for b in range(2)]
        pss = [alloc_ps(nc, st3, f"l2_pss{b}", [128, 512]) for b in range(2)]
        rsum = [alloc_sb(nc, st3, f"l2_rsum{b}", [128, 512]) for b in range(2)]
        yb = [alloc_sb(nc, st3, f"l2_yb{b}", [128, 512]) for b in range(2)]

        def _down_tile(ti):
            b = ti % 2
            r0 = ti * 128
            ch.dma("scalar", atile[b][:],
                   actT_dram[:, :, r0:r0 + 128].rearrange("e (c p) t -> p e c t", p=128))
            ch.dma("scalar", hstile[b][:],
                   hsT_dram[:, r0:r0 + 128].rearrange("(c p) t -> p c t", p=128))
            for jc in range(2):
                jb = (ti * 2 + jc) % 2
                for e in range(E1):
                    pe_ = psd[e % 2]
                    for ic in range(4):
                        ch.op("tensor", lambda tt, e=e, ic=ic, jc=jc, pe_=pe_: tt.matmul(
                            out=pe_[:], lhsT=atile[b][:, e, ic, :],
                            rhs=dwb[e][ic][:, jc * 512:(jc + 1) * 512],
                            start=(ic == 0), stop=(ic == 3)))
                    if e == 0:
                        ch.op("vector", lambda v, e=e, jb=jb, pe_=pe_: v.tensor_scalar(
                            out=rsum[jb][:], in0=pe_[:],
                            scalar1=w8all[:, ti * E1 + e:ti * E1 + e + 1], scalar2=None,
                            op0=ALU.mult))
                    else:
                        ch.op("vector", lambda v, e=e, jb=jb, pe_=pe_: v.scalar_tensor_tensor(
                            out=rsum[jb][:], in0=pe_[:],
                            scalar=w8all[:, ti * E1 + e:ti * E1 + e + 1],
                            in1=rsum[jb][:], op0=ALU.mult, op1=ALU.add))
                for c in range(16):
                    ch.op("tensor", lambda tt, c=c, jc=jc, jb=jb: tt.matmul(
                        out=pss[jb][:], lhsT=hstile[b][:, c, :],
                        rhs=sdw[c][:, jc * 512:(jc + 1) * 512],
                        start=(c == 0), stop=(c == 15)))
                ch.op("vector", lambda v, jb=jb: v.tensor_tensor(out=yb[jb][:], in0=pss[jb][:],
                                                                 in1=rsum[jb][:], op=ALU.add))
                ch.dma("gpsimd", y_dram[r0:r0 + 128, jc * 512:(jc + 1) * 512], yb[jb][:])

        for ti in range(T // 128):
            _down_tile(ti)

    with ExitStack() as st4:
        emit_ln_tok(nc, ch, st4, "l2_ln2", y_dram, None, res_add_dram=x3_dram,
                    x_out_dram=xout_dram)


def build_launch_l1ffn():
    nc = bass.Bass("TRN2", target_bir_lowering=False)
    x3 = _mk(nc, "x3", [T, D], F32, "ExternalInput")
    rw = _mk(nc, "rw", [D, E1], F32, "ExternalInput")
    rb = _mk(nc, "rb", [1, 1], F32, "ExternalInput")
    gw = _mk(nc, "gw", [E1, D, 512], F32, "ExternalInput")
    uw = _mk(nc, "uw", [E1, D, 512], F32, "ExternalInput")
    dw = _mk(nc, "dw", [E1, 512, D], F32, "ExternalInput")
    sgw = _mk(nc, "sgw", [D, 2048], F32, "ExternalInput")
    suw = _mk(nc, "suw", [D, 2048], F32, "ExternalInput")
    sdw = _mk(nc, "sdw", [2048, D], F32, "ExternalInput")
    xout = _mk(nc, "xout", [T, D], F32, "ExternalOutput")
    scratch = {
        "l2ln": _mk(nc, "s_l2ln", [T, D], BF16),
        "actT": _mk(nc, "s_actT", [E1, 512, T], BF16),
        "hsT2": _mk(nc, "s_hsT2", [2048, T], BF16),
        "l2y": _mk(nc, "s_l2y", [T, D], F32),
    }
    ch = Chain(nc)
    with ExitStack() as st:
        ch.make_sems(st)
        block = st.enter_context(nc.Block())
        emit_l1_ffn(nc, ch, st, x3, rw, rb, gw, uw, dw, sgw, suw, sdw, xout, scratch)
        ch.emit(block)
    return nc


_BUILD_CACHE = {}


def _get(name, fn):
    if name not in _BUILD_CACHE:
        _BUILD_CACHE[name] = fn()
    return _BUILD_CACHE[name]


def kernel_7launch(**inputs):
    cores = list(range(NCORES))
    x = np.ascontiguousarray(inputs["L0_attn_x"])

    nc1 = _get("l0attn", build_launch_l0attn)
    r1 = run_bass_kernel_spmd(nc1, [
        {"x": x[i], "w_attn": inputs["L0_attn_w_attn"],
         "w_proj": inputs["L0_attn_w_proj"]} for i in cores], core_ids=cores).results
    x1 = [r["x1"] for r in r1]

    nc2 = _get("l0route", build_launch_l0ffn_route)
    r2 = run_bass_kernel_spmd(nc2, [
        {"x1": x1[i], "rw": inputs["L0_ffn_router_weight"],
         "eb": inputs["L0_ffn_expert_bias"]} for i in cores], core_ids=cores).results
    totals_all = np.stack([r["totals"].reshape(E0) for r in r2], 0)

    nc3 = _get("l0scat", build_launch_l0ffn_scatter)
    ins3 = []
    for i in cores:
        pm = np.zeros((NCORES, 1), np.float32)
        pm[:i] = 1.0
        ins3.append({"ln1": r2[i]["ln1"], "idx6": r2[i]["idx6"], "w64": r2[i]["w64"],
                     "totals_all": totals_all, "prevmask": pm})
    r3 = run_bass_kernel_spmd(nc3, ins3, core_ids=cores).results

    nc4 = _get("l0exp", build_launch_l0ffn_experts)
    ins4 = []
    for c in cores:
        parts = np.stack([r3[p]["slotbuf"][c * 8 * CAP:(c + 1) * 8 * CAP]
                          for p in cores], 0)
        ins4.append({"parts": parts, "w1": inputs["L0_ffn_w1"][c * 8:(c + 1) * 8],
                     "w2": inputs["L0_ffn_w2"][c * 8:(c + 1) * 8]})
    r4 = run_bass_kernel_spmd(nc4, ins4, core_ids=cores).results
    ysall = np.concatenate([r["ys"] for r in r4], 0)

    nc5 = _get("l0comb", build_launch_l0ffn_combine)
    ins5 = [{"x1": x1[i], "ln1": r2[i]["ln1"], "gidx": r3[i]["gidx"],
             "ysall": ysall, "sw1": inputs["L0_ffn_sw1"],
             "sw2": inputs["L0_ffn_sw2"]} for i in cores]
    r5 = run_bass_kernel_spmd(nc5, ins5, core_ids=cores).results
    x2 = [r["x2"] for r in r5]

    nc6 = _get("l1attn", build_launch_l1attn)
    r6 = run_bass_kernel_spmd(nc6, [
        {"x2": x2[i], "wqkv": inputs["L1_attn_W_qkv"], "bqkv": inputs["L1_attn_b_qkv"],
         "wout": inputs["L1_attn_W_out"], "bout": inputs["L1_attn_b_out"]}
        for i in cores], core_ids=cores).results
    x3 = [r["x3"] for r in r6]

    nc7 = _get("l1ffn", build_launch_l1ffn)
    r7 = run_bass_kernel_spmd(nc7, [
        {"x3": x3[i], "rw": inputs["L1_ffn_router_w"],
         "rb": np.asarray(inputs["L1_ffn_router_bias"]).reshape(1, 1), "gw": inputs["L1_ffn_gate_w"],
         "uw": inputs["L1_ffn_up_w"], "dw": inputs["L1_ffn_down_w"],
         "sgw": inputs["L1_ffn_s_gate_w"], "suw": inputs["L1_ffn_s_up_w"],
         "sdw": inputs["L1_ffn_s_down_w"]} for i in cores], core_ids=cores).results
    return np.stack([r["xout"] for r in r7], 0)



def emit_l0ffn_scatter_local(nc, ch, st, ln1_dram, idx6_dram, w64_dram,
                             slotbuf_dram, counts_dram, cx_i32=None):
    with ExitStack() as st2:
        iotaE = alloc_sb(nc, st2, "l0s_iotaE", [128, E0])
        colid = alloc_sb(nc, st2, "l0s_colid", [128, 128])
        rowid = alloc_sb(nc, st2, "l0s_rowid", [128, 1])
        stri = alloc_sb(nc, st2, "l0s_stri", [128, 128])
        onesv = alloc_sb(nc, st2, "l0s_onesv", [1, 128])
        ch.op("gpsimd", lambda g: g.iota(iotaE[:], pattern=[[1, E0]], base=0,
                                         channel_multiplier=0,
                                         allow_small_or_imprecise_dtypes=True))
        ch.op("gpsimd", lambda g: g.iota(colid[:], pattern=[[1, 128]], base=0,
                                         channel_multiplier=0,
                                         allow_small_or_imprecise_dtypes=True))
        ch.op("gpsimd", lambda g: g.iota(rowid[:], pattern=[[0, 1]], base=0,
                                         channel_multiplier=1,
                                         allow_small_or_imprecise_dtypes=True))
        ch.op("vector", lambda v: v.tensor_scalar(out=stri[:], in0=colid[:],
                                                  scalar1=rowid[:, 0:1], scalar2=None,
                                                  op0=ALU.is_gt))
        ch.op("vector", lambda v: v.memset(onesv[:], 1.0))

        mi8 = [alloc_sb(nc, st2, f"l0s_mi8{b}", [128, 8], U32) for b in range(2)]
        mif = [alloc_sb(nc, st2, f"l0s_mif{b}", [128, 8]) for b in range(2)]
        w64 = [alloc_sb(nc, st2, f"l0s_w64{b}", [128, E0]) for b in range(2)]
        ohr = [alloc_sb(nc, st2, f"l0s_ohr{b}", [128, E0]) for b in range(2)]
        amat = [alloc_sb(nc, st2, f"l0s_amat{b}", [128, E0]) for b in range(2)]
        tmpe = [alloc_sb(nc, st2, f"l0s_tmpe{b}", [128, E0]) for b in range(2)]
        pspos = [alloc_ps(nc, st2, f"l0s_pspos{b}", [128, E0]) for b in range(2)]
        posf = [alloc_sb(nc, st2, f"l0s_posf{b}", [128, E0]) for b in range(2)]
        lnb = [alloc_sb(nc, st2, f"l0s_lnb{b}", [128, D], BF16) for b in range(2)]
        comb = [alloc_sb(nc, st2, f"l0s_comb{b}", [128, ROWW], BF16) for b in range(2)]
        posr = [alloc_sb(nc, st2, f"l0s_posr{b}", [128, 1]) for b in range(2)]
        wr = [alloc_sb(nc, st2, f"l0s_wr{b}", [128, 1]) for b in range(2)]
        vr = [alloc_sb(nc, st2, f"l0s_vr{b}", [128, 1], U32) for b in range(2)]
        slotf = [alloc_sb(nc, st2, f"l0s_slotf{b}", [128, 1]) for b in range(2)]
        ssel = [alloc_sb(nc, st2, f"l0s_ssel{b}", [128, 1]) for b in range(2)]
        invf = alloc_sb(nc, st2, "l0s_invf", [128, 1])
        slotu = [alloc_sb(nc, st2, f"l0s_slotu{b}", [128, 1], U32) for b in range(2)]
        rbase = alloc_sb(nc, st2, "l0s_rbase", [1, E0])
        psts = alloc_ps(nc, st2, "l0s_psts", [1, E0])
        tsrow = alloc_sb(nc, st2, "l0s_tsrow", [1, E0])
        onesc128 = alloc_sb(nc, st2, "l0s_onesc128", [128, 1])
        ch.op("vector", lambda v: v.memset(onesc128[:], 1.0))
        ch.op("vector", lambda v: v.memset(rbase[:], 0.0))
        ch.op("vector", lambda v: v.memset(comb[0][:], 0.0))
        ch.op("vector", lambda v: v.memset(comb[1][:], 0.0))
        ch.op("vector", lambda v: v.tensor_scalar(
            out=invf[:], in0=rowid[:], scalar1=float(NSLOT), scalar2=None,
            op0=ALU.add))

        def _sc_tile(ti):
            b = ti % 2
            mi8b, mifb, w64b, lnbb = mi8[b], mif[b], w64[b], lnb[b]
            amb, pspb, posfb = amat[b], pspos[b], posf[b]
            r0 = ti * 128
            ch.dma("gpsimd", mi8b[:], idx6_dram[r0:r0 + 128, :])
            ch.dma("gpsimd", w64b[:], w64_dram[r0:r0 + 128, :])
            ch.dma("scalar", lnbb[:], ln1_dram[r0:r0 + 128, :])
            ch.op("vector", lambda v: v.tensor_copy(out=mifb[:], in_=mi8b[:]))
            for r in range(K0):
                if r == 0:
                    ch.op("vector", lambda v, r=r: v.tensor_scalar(
                        out=amb[:], in0=iotaE[:], scalar1=mifb[:, r:r + 1], scalar2=None,
                        op0=ALU.is_equal))
                else:
                    ch.op("vector", lambda v, r=r: v.tensor_scalar(
                        out=ohr[b][:], in0=iotaE[:], scalar1=mifb[:, r:r + 1], scalar2=None,
                        op0=ALU.is_equal))
                    ch.op("vector", lambda v: v.tensor_tensor(out=amb[:], in0=amb[:],
                                                              in1=ohr[b][:], op=ALU.add))
            ch.op("tensor", lambda t: t.matmul(out=pspb[:], lhsT=stri[:], rhs=amb[:],
                                               start=True, stop=False))
            ch.op("tensor", lambda t: t.matmul(out=pspb[:], lhsT=onesv[:], rhs=rbase[:],
                                               start=False, stop=True))
            ch.op("vector", lambda v: v.tensor_copy(out=posfb[:], in_=pspb[:]))
            ch.op("tensor", lambda t: t.matmul(out=psts[:], lhsT=onesc128[:], rhs=amb[:],
                                               start=True, stop=True))
            ch.op("vector", lambda v: v.tensor_copy(out=tsrow[:], in_=psts[:]))
            ch.op("vector", lambda v: v.tensor_tensor(out=rbase[:], in0=rbase[:], in1=tsrow[:],
                                                      op=ALU.add))
            for r in range(K0):
                rb_ = r % 2
                ohrb, tmpb, posrb, wrb = ohr[rb_], tmpe[rb_], posr[rb_], wr[rb_]
                vrb, slotfb, sselb, slotub, cmb = vr[rb_], slotf[rb_], ssel[rb_], slotu[rb_], comb[rb_]
                ch.op("vector", lambda v, r=r: v.tensor_scalar(
                    out=ohrb[:], in0=iotaE[:], scalar1=mifb[:, r:r + 1], scalar2=None,
                    op0=ALU.is_equal))
                ch.op("vector", lambda v: v.tensor_tensor(out=tmpb[:], in0=ohrb[:], in1=posfb[:],
                                                          op=ALU.mult))
                ch.op("vector", lambda v: v.tensor_reduce(out=posrb[:], in_=tmpb[:],
                                                          axis=AX.X, op=ALU.add))
                ch.op("vector", lambda v: v.tensor_tensor(out=tmpb[:], in0=ohrb[:], in1=w64b[:],
                                                          op=ALU.mult))
                ch.op("vector", lambda v: v.tensor_reduce(out=wrb[:], in_=tmpb[:],
                                                          axis=AX.X, op=ALU.add))
                ch.op("vector", lambda v: v.tensor_scalar(
                    out=vrb[:], in0=posrb[:], scalar1=float(CAP), scalar2=None, op0=ALU.is_lt))
                ch.op("vector", lambda v, r=r: v.tensor_scalar(
                    out=slotfb[:], in0=mifb[:, r:r + 1], scalar1=float(CAP),
                    scalar2=posrb[:, 0:1], op0=ALU.mult, op1=ALU.add))
                ch.op("vector", lambda v: v.select(out=sselb[:], mask=vrb[:], on_true=slotfb[:],
                                                   on_false=invf[:], add_drain=True))
                ch.op("vector", lambda v: v.tensor_copy(out=slotub[:], in_=sselb[:]))
                ch.op("vector", lambda v: v.tensor_copy(out=cmb[:, 0:D], in_=lnbb[:]))
                ch.op("vector", lambda v: v.tensor_copy(out=cmb[:, D:D + 1], in_=wrb[:]))
                ch.dma_fn("gpsimd", lambda g: g.indirect_dma_start(
                    out=slotbuf_dram,
                    out_offset=bass.IndirectOffsetOnAxis(ap=slotub[:], axis=0),
                    in_=cmb[:], in_offset=None))

        for ti in range(T // 128):
            _sc_tile(ti)
        cnt64 = alloc_sb(nc, st2, "l0s_cnt64", [1, E0])
        ch.op("vector", lambda v: v.tensor_copy(out=cnt64[:], in_=rbase[:]))
        ch.dma("gpsimd", counts_dram, cnt64[:])
        if cx_i32 is not None:
            ch.op("vector", lambda v: v.tensor_copy(out=cx_i32[:], in_=rbase[:]))


def emit_l0ffn_experts_local(nc, ch, st, slotbuf_dram, w1_dram, w2_dram, ys_dram,
                             scratch):
    NR = NSLOT
    NTile = NR // 128
    xs_dram = scratch["xs"]
    with ExitStack() as st2:
        pb = [alloc_sb(nc, st2, f"l0e_pb{b}", [128, ROWW], BF16) for b in range(2)]
        accb = [alloc_sb(nc, st2, f"l0e_accb{b}", [128, D], BF16) for b in range(2)]
        wcol = alloc_sb(nc, st2, "l0e_wcol", [128, NTile])

        def _stage(s):
            b = s % 2
            r0 = s * 128
            ch.dma("scalar", pb[b][:], slotbuf_dram[r0:r0 + 128, :])
            ch.op("vector", lambda v: v.tensor_copy(out=accb[b][:], in_=pb[b][:, 0:D]))
            ch.op("vector", lambda v, s=s: v.tensor_copy(out=wcol[:, s:s + 1],
                                                         in_=pb[b][:, D:D + 1]))
            ch.dma("gpsimd", xs_dram[r0:r0 + 128, :], accb[b][:])

        for s in range(NTile):
            _stage(s)

        GRP = 8 * CAP
        xsT = [alloc_sb(nc, st2, f"l0e_xsT{c}", [128, GRP], BF16) for c in range(8)]

        w1b = [[alloc_sb(nc, st2, f"l0e_w1b{b}_{c}", [128, 512], BF16) for c in range(8)]
               for b in range(2)]
        w2b = [[alloc_sb(nc, st2, f"l0e_w2b{b}_{c}", [128, D], BF16) for c in range(4)]
               for b in range(2)]
        psh = [alloc_ps(nc, st2, f"l0e_psh{b}", [128, CAP]) for b in range(2)]
        hb = [alloc_sb(nc, st2, f"l0e_hb{i}", [128, CAP], BF16) for i in range(4)]
        gsc = [tuple(alloc_sb(nc, st2, f"l0e_g{n}{b}", [128, CAP])
                     for n in ("a", "b", "c", "d")) for b in range(2)]
        psd = [alloc_ps(nc, st2, f"l0e_psd{b}", [128, 512]) for b in range(2)]
        yb = [alloc_sb(nc, st2, f"l0e_yb{b}", [128, 512], BF16) for b in range(2)]

        def _expert(grp, e):
            eb_ = e % 2
            w1e, w2e = w1b[eb_], w2b[eb_]
            for c in range(8):
                ch.dma("gpsimd", w1e[c][:], w1_dram[e, c * 128:(c + 1) * 128, :])
            for c in range(4):
                ch.dma("gpsimd", w2e[c][:], w2_dram[e, c * 128:(c + 1) * 128, :])
            s0 = e * CAP
            sg0 = e * CAP - grp * GRP
            for ic in range(4):
                ib_ = ic % 2
                ch_psh = psh[ib_]
                for c in range(8):
                    ch.op("tensor", lambda t, c=c, ic=ic, ch_psh=ch_psh: t.matmul(
                        out=ch_psh[:], lhsT=w1e[c][:, ic * 128:(ic + 1) * 128],
                        rhs=xsT[c][:, sg0:sg0 + CAP],
                        start=(c == 0), stop=(c == 7)))
                emit_gelu_tanh(nc, ch, gsc[ib_], ch_psh, hb[ic], "")
            for stile in range(CAP // 128):
                for jc in range(2):
                    db_ = (stile * 2 + jc) % 2
                    for ic in range(4):
                        ch.op("tensor", lambda t, ic=ic, stile=stile, jc=jc, db_=db_: t.matmul(
                            out=psd[db_][:], lhsT=hb[ic][:, stile * 128:(stile + 1) * 128],
                            rhs=w2e[ic][:, jc * 512:(jc + 1) * 512],
                            start=(ic == 0), stop=(ic == 3)))
                    gt = e * (CAP // 128) + stile
                    ch.op("vector", lambda v, gt=gt, db_=db_: v.tensor_scalar(
                        out=yb[db_][:], in0=psd[db_][:], scalar1=wcol[:, gt:gt + 1],
                        scalar2=None, op0=ALU.mult))
                    ch.dma("gpsimd", ys_dram[s0 + stile * 128:s0 + (stile + 1) * 128,
                                             jc * 512:(jc + 1) * 512], yb[db_][:])

        for grp in range(E0 * CAP // GRP):
            for c in range(8):
                ch.dma_transpose(xsT[c][:], xs_dram[grp * GRP:(grp + 1) * GRP,
                                                    c * 128:(c + 1) * 128])
            for e in range(grp * (GRP // CAP), (grp + 1) * (GRP // CAP)):
                _expert(grp, e)


def emit_l0ffn_combine_local(nc, ch, st, x1_dram, idx6_dram, counts_all_dram,
                             prevmask_dram, ysall_dram, yshared_dram, x2_dram,
                             scratch):
    ysbuf = scratch["ysbuf"]
    y_dram = scratch["yffn"]

    with ExitStack() as st2:
        zb = alloc_sb(nc, st2, "l0c_zb", [128, D], BF16)
        ch.op("vector", lambda v: v.memset(zb[:], 0.0))
        ch.dma("scalar", ysbuf[0:NSLOT, :], ysall_dram)
        ch.dma("gpsimd", ysbuf[NSLOT:NSLOT + 1, :], zb[0:1, :])

        pmsk = alloc_sb(nc, st2, "l0c_pmsk", [8, 1])
        tall = alloc_sb(nc, st2, "l0c_tall", [8, E0])
        psbase = alloc_ps(nc, st2, "l0c_psbase", [1, E0])
        baser = alloc_sb(nc, st2, "l0c_baser", [1, E0])
        onesr = alloc_sb(nc, st2, "l0c_onesr", [1, 128])
        psbb = alloc_ps(nc, st2, "l0c_psbb", [128, E0])
        baseB = alloc_sb(nc, st2, "l0c_baseB", [128, E0])
        ch.dma("gpsimd", pmsk[:], prevmask_dram)
        if counts_all_dram.dtype == F32:
            ch.dma("gpsimd", tall[:], counts_all_dram)
        else:
            talli = alloc_sb(nc, st2, "l0c_talli", [8, E0], I32)
            ch.dma("gpsimd", talli[:], counts_all_dram)
            ch.op("vector", lambda v: v.tensor_copy(out=tall[:], in_=talli[:]))
        ch.op("tensor", lambda t: t.matmul(out=psbase[:], lhsT=pmsk[:], rhs=tall[:],
                                           start=True, stop=True))
        ch.op("vector", lambda v: v.tensor_copy(out=baser[:], in_=psbase[:]))
        ch.op("vector", lambda v: v.memset(onesr[:], 1.0))
        ch.op("tensor", lambda t: t.matmul(out=psbb[:], lhsT=onesr[:], rhs=baser[:],
                                           start=True, stop=True))
        ch.op("vector", lambda v: v.tensor_copy(out=baseB[:], in_=psbb[:]))

        iotaE = alloc_sb(nc, st2, "l0c_iotaE", [128, E0])
        colid = alloc_sb(nc, st2, "l0c_colid", [128, 128])
        rowid = alloc_sb(nc, st2, "l0c_rowid", [128, 1])
        stri = alloc_sb(nc, st2, "l0c_stri", [128, 128])
        onesv = alloc_sb(nc, st2, "l0c_onesv", [1, 128])
        ch.op("gpsimd", lambda g: g.iota(iotaE[:], pattern=[[1, E0]], base=0,
                                         channel_multiplier=0,
                                         allow_small_or_imprecise_dtypes=True))
        ch.op("gpsimd", lambda g: g.iota(colid[:], pattern=[[1, 128]], base=0,
                                         channel_multiplier=0,
                                         allow_small_or_imprecise_dtypes=True))
        ch.op("gpsimd", lambda g: g.iota(rowid[:], pattern=[[0, 1]], base=0,
                                         channel_multiplier=1,
                                         allow_small_or_imprecise_dtypes=True))
        ch.op("vector", lambda v: v.tensor_scalar(out=stri[:], in0=colid[:],
                                                  scalar1=rowid[:, 0:1], scalar2=None,
                                                  op0=ALU.is_gt))
        ch.op("vector", lambda v: v.memset(onesv[:], 1.0))

        mi8 = [alloc_sb(nc, st2, f"l0c_mi8{b}", [128, 8], U32) for b in range(2)]
        mif = [alloc_sb(nc, st2, f"l0c_mif{b}", [128, 8]) for b in range(2)]
        ohr = [alloc_sb(nc, st2, f"l0c_ohr{b}", [128, E0]) for b in range(2)]
        amat = [alloc_sb(nc, st2, f"l0c_amat{b}", [128, E0]) for b in range(2)]
        tmpe = [alloc_sb(nc, st2, f"l0c_tmpe{b}", [128, E0]) for b in range(2)]
        pspos = [alloc_ps(nc, st2, f"l0c_pspos{b}", [128, E0]) for b in range(2)]
        posf = [alloc_sb(nc, st2, f"l0c_posf{b}", [128, E0]) for b in range(2)]
        vmat = [alloc_sb(nc, st2, f"l0c_vmat{b}", [128, E0]) for b in range(2)]
        posr = alloc_sb(nc, st2, "l0c_posr", [128, 1])
        vr = alloc_sb(nc, st2, "l0c_vr", [128, 1], U32)
        slotf = alloc_sb(nc, st2, "l0c_slotf", [128, 1])
        gsel = alloc_sb(nc, st2, "l0c_gsel", [128, 1])
        zrow = alloc_sb(nc, st2, "l0c_zrow", [128, 1])
        gslotu = [alloc_sb(nc, st2, f"l0c_gsu{b}", [128, 6], U32) for b in range(2)]
        rbase = alloc_sb(nc, st2, "l0c_rbase", [1, E0])
        psts = alloc_ps(nc, st2, "l0c_psts", [1, E0])
        tsrow = alloc_sb(nc, st2, "l0c_tsrow", [1, E0])
        onesc128 = alloc_sb(nc, st2, "l0c_onesc128", [128, 1])
        racc = [alloc_sb(nc, st2, f"l0c_racc{b}", [128, D]) for b in range(2)]
        ysh = [alloc_sb(nc, st2, f"l0c_ysh{b}", [128, D], BF16) for b in range(2)]
        ycmb = [alloc_sb(nc, st2, f"l0c_ycmb{b}", [128, D]) for b in range(2)]
        ch.op("vector", lambda v: v.memset(onesc128[:], 1.0))
        ch.op("vector", lambda v: v.memset(rbase[:], 0.0))
        ch.op("vector", lambda v: v.memset(zrow[:], float(NSLOT)))

        def _cb_tile(ti, mi8, mif, ohr, amat, tmpe, pspos, posf, vmat, gslotu,
                     racc, ysh, ycmb):
            r0 = ti * 128
            ch.dma("gpsimd", mi8[:], idx6_dram[r0:r0 + 128, :])
            ch.op("vector", lambda v: v.tensor_copy(out=mif[:], in_=mi8[:]))
            for r in range(K0):
                if r == 0:
                    ch.op("vector", lambda v, r=r: v.tensor_scalar(
                        out=amat[:], in0=iotaE[:], scalar1=mif[:, r:r + 1], scalar2=None,
                        op0=ALU.is_equal))
                else:
                    ch.op("vector", lambda v, r=r: v.tensor_scalar(
                        out=ohr[:], in0=iotaE[:], scalar1=mif[:, r:r + 1], scalar2=None,
                        op0=ALU.is_equal))
                    ch.op("vector", lambda v: v.tensor_tensor(out=amat[:], in0=amat[:],
                                                              in1=ohr[:], op=ALU.add))
            ch.op("tensor", lambda t: t.matmul(out=pspos[:], lhsT=stri[:], rhs=amat[:],
                                               start=True, stop=False))
            ch.op("tensor", lambda t: t.matmul(out=pspos[:], lhsT=onesv[:], rhs=rbase[:],
                                               start=False, stop=True))
            ch.op("vector", lambda v: v.tensor_copy(out=posf[:], in_=pspos[:]))
            ch.op("tensor", lambda t: t.matmul(out=psts[:], lhsT=onesc128[:], rhs=amat[:],
                                               start=True, stop=True))
            ch.op("vector", lambda v: v.tensor_copy(out=tsrow[:], in_=psts[:]))
            ch.op("vector", lambda v: v.tensor_tensor(out=rbase[:], in0=rbase[:], in1=tsrow[:],
                                                      op=ALU.add))
            ch.op("vector", lambda v: v.tensor_tensor(out=vmat[:], in0=posf[:], in1=baseB[:],
                                                      op=ALU.add))
            ch.op("vector", lambda v: v.tensor_scalar(out=vmat[:], in0=vmat[:],
                                                      scalar1=float(CAP), scalar2=None,
                                                      op0=ALU.is_lt))
            ch.op("vector", lambda v: v.memset(racc[:], 0.0))
            for r in range(K0):
                ch.op("vector", lambda v, r=r: v.tensor_scalar(
                    out=ohr[:], in0=iotaE[:], scalar1=mif[:, r:r + 1], scalar2=None,
                    op0=ALU.is_equal))
                ch.op("vector", lambda v: v.tensor_tensor(out=tmpe[:], in0=ohr[:], in1=posf[:],
                                                          op=ALU.mult))
                ch.op("vector", lambda v: v.tensor_reduce(out=posr[:], in_=tmpe[:],
                                                          axis=AX.X, op=ALU.add))
                ch.op("vector", lambda v: v.tensor_tensor(out=tmpe[:], in0=ohr[:], in1=vmat[:],
                                                          op=ALU.mult))
                ch.op("vector", lambda v: v.tensor_reduce(out=slotf[:], in_=tmpe[:],
                                                          axis=AX.X, op=ALU.add))
                ch.op("vector", lambda v: v.tensor_copy(out=vr[:], in_=slotf[:]))
                ch.op("vector", lambda v, r=r: v.tensor_scalar(
                    out=slotf[:], in0=mif[:, r:r + 1], scalar1=float(CAP),
                    scalar2=posr[:, 0:1], op0=ALU.mult, op1=ALU.add))
                ch.op("vector", lambda v: v.select(out=gsel[:], mask=vr[:], on_true=slotf[:],
                                                   on_false=zrow[:], add_drain=True))
                ch.op("vector", lambda v, r=r: v.tensor_copy(out=gslotu[:, r:r + 1],
                                                             in_=gsel[:]))
            for r in range(K0):
                ch.dma_fn("gpsimd", lambda g, r=r: g.indirect_dma_start(
                    out=racc[:], out_offset=None, in_=ysbuf,
                    in_offset=bass.IndirectOffsetOnAxis(ap=gslotu[:, r:r + 1], axis=0),
                    compute_op=ALU.add))
            ch.dma("scalar", ysh[:], yshared_dram[r0:r0 + 128, :])
            ch.op("vector", lambda v: v.tensor_tensor(out=ycmb[:], in0=racc[:], in1=ysh[:],
                                                      op=ALU.add))
            ch.dma("gpsimd", y_dram[r0:r0 + 128, :], ycmb[:])

        for ti in range(T // 128):
            b = ti % 2
            _cb_tile(ti, mi8[b], mif[b], ohr[b], amat[b], tmpe[b], pspos[b],
                     posf[b], vmat[b], gslotu[b], racc[b], ysh[b], ycmb[b])

    with ExitStack() as st3:
        emit_ln_tok(nc, ch, st3, "l0c_ln2", y_dram, None, res_add_dram=x1_dram,
                    x_out_dram=x2_dram)


def emit_l0ffn_shared(nc, ch, st, ln1_dram, sw1_dram, sw2_dram, yshared_dram,
                      scratch):
    sw1b_dram = scratch["sw1b"]
    sw2b_dram = scratch["sw2b"]
    hsT_dram = scratch["hsT"]
    with ExitStack() as st2:
        ch.dma("gpsimd", sw1b_dram, sw1_dram)
        ch.dma("gpsimd", sw2b_dram, sw2_dram)
        sw1T = [alloc_sb(nc, st2, f"l0h_sw1T{c}", [128, 2048], BF16) for c in range(8)]
        for c in range(8):
            ch.dma_transpose(sw1T[c][:], sw1b_dram[:, c * 128:(c + 1) * 128])
        lnT = [alloc_sb(nc, st2, f"l0h_lnT{c}", [128, T], BF16) for c in range(8)]
        emit_transpose_to_feat(nc, ch, ln1_dram, lnT)
        psh = [alloc_ps(nc, st2, f"l0h_psh{b}", [128, 512]) for b in range(2)]
        hbf = [alloc_sb(nc, st2, f"l0h_hbf{b}", [128, 512], BF16) for b in range(2)]
        gsc = [tuple(alloc_sb(nc, st2, f"l0h_g{n}{b}", [128, 512])
                     for n in ("a", "b", "c", "d")) for b in range(2)]

        def _mid(it, fc, t):
            b = it % 2
            for c in range(8):
                ch.op("tensor", lambda tt, c=c: tt.matmul(
                    out=psh[b][:], lhsT=sw1T[c][:, fc * 128:(fc + 1) * 128],
                    rhs=lnT[c][:, t * 512:(t + 1) * 512],
                    start=(c == 0), stop=(c == 7)))
            emit_gelu_tanh(nc, ch, gsc[b], psh[b], hbf[b], "")
            ch.dma("gpsimd", hsT_dram[fc * 128:(fc + 1) * 128, t * 512:(t + 1) * 512],
                   hbf[b][:])

        it = 0
        for fc in range(16):
            for t in range(T // 512):
                _mid(it, fc, t)
                it += 1
    with ExitStack() as st2b:
        sw2T = [alloc_sb(nc, st2b, f"l0h_sw2T{c}", [128, D], BF16) for c in range(16)]
        for c in range(16):
            ch.dma_transpose(sw2T[c][:], sw2b_dram[:, c * 128:(c + 1) * 128])
        hstile = [alloc_sb(nc, st2b, f"l0h_hstile{b}", [128, 16, 128], BF16) for b in range(2)]
        psd = [alloc_ps(nc, st2b, f"l0h_psd{b}", [128, 512]) for b in range(2)]
        yw = [alloc_sb(nc, st2b, f"l0h_yw{b}", [128, 512], BF16) for b in range(2)]

        def _down(ti):
            b = ti % 2
            r0 = ti * 128
            ch.dma("scalar", hstile[b][:],
                   hsT_dram[:, r0:r0 + 128].rearrange("(c p) t -> p c t", p=128))
            for jc in range(2):
                jb = (ti * 2 + jc) % 2
                for c in range(16):
                    ch.op("tensor", lambda tt, c=c, jc=jc, jb=jb: tt.matmul(
                        out=psd[jb][:], lhsT=hstile[b][:, c, :],
                        rhs=sw2T[c][:, jc * 512:(jc + 1) * 512],
                        start=(c == 0), stop=(c == 15)))
                ch.op("vector", lambda v, jb=jb: v.tensor_copy(out=yw[jb][:], in_=psd[jb][:]))
                ch.dma("gpsimd", yshared_dram[r0:r0 + 128, jc * 512:(jc + 1) * 512],
                       yw[jb][:])

        for ti in range(T // 128):
            _down(ti)


def emit_l0ffn_scatter_v2(nc, ch, st, ln1_dram, idx6_dram, slotbuf_dram,
                          counts_dram, cx_i32=None, wlab=None):
    with ExitStack() as st2:
        iotaE = alloc_sb(nc, st2, "l2s_iotaE", [128, E0])
        colid = alloc_sb(nc, st2, "l2s_colid", [128, 128])
        rowid = alloc_sb(nc, st2, "l2s_rowid", [128, 1])
        stri = alloc_sb(nc, st2, "l2s_stri", [128, 128])
        onesv = alloc_sb(nc, st2, "l2s_onesv", [1, 128])
        ch.op("gpsimd", lambda g: g.iota(iotaE[:], pattern=[[1, E0]], base=0,
                                         channel_multiplier=0,
                                         allow_small_or_imprecise_dtypes=True))
        ch.op("gpsimd", lambda g: g.iota(colid[:], pattern=[[1, 128]], base=0,
                                         channel_multiplier=0,
                                         allow_small_or_imprecise_dtypes=True))
        ch.op("gpsimd", lambda g: g.iota(rowid[:], pattern=[[0, 1]], base=0,
                                         channel_multiplier=1,
                                         allow_small_or_imprecise_dtypes=True))
        ch.op("vector", lambda v: v.tensor_scalar(out=stri[:], in0=colid[:],
                                                  scalar1=rowid[:, 0:1], scalar2=None,
                                                  op0=ALU.is_gt))
        ch.op("vector", lambda v: v.memset(onesv[:], 1.0))

        mi8 = [alloc_sb(nc, st2, f"l2s_mi8{b}", [128, 8], U32) for b in range(2)]
        mif = [alloc_sb(nc, st2, f"l2s_mif{b}", [128, 8]) for b in range(2)]
        ohr = [alloc_sb(nc, st2, f"l2s_ohr{b}", [128, E0]) for b in range(2)]
        amat = [alloc_sb(nc, st2, f"l2s_amat{b}", [128, E0]) for b in range(2)]
        tmpe = [alloc_sb(nc, st2, f"l2s_tmpe{b}", [128, E0]) for b in range(2)]
        pspos = [alloc_ps(nc, st2, f"l2s_pspos{b}", [128, E0]) for b in range(2)]
        posf = [alloc_sb(nc, st2, f"l2s_posf{b}", [128, E0]) for b in range(2)]
        lnb = [alloc_sb(nc, st2, f"l2s_lnb{b}", [128, D], BF16) for b in range(2)]
        posr = [alloc_sb(nc, st2, f"l2s_posr{b}", [128, 1]) for b in range(2)]
        vr = [alloc_sb(nc, st2, f"l2s_vr{b}", [128, 1], U32) for b in range(2)]
        slotf = [alloc_sb(nc, st2, f"l2s_slotf{b}", [128, 1]) for b in range(2)]
        ssel = [alloc_sb(nc, st2, f"l2s_ssel{b}", [128, 1]) for b in range(2)]
        invf = alloc_sb(nc, st2, "l2s_invf", [128, 1])
        slotu6 = [alloc_sb(nc, st2, f"l2s_slotu6{b}", [128, K0], U32) for b in range(2)]
        rbase = alloc_sb(nc, st2, "l2s_rbase", [1, E0])
        psts = alloc_ps(nc, st2, "l2s_psts", [1, E0])
        tsrow = alloc_sb(nc, st2, "l2s_tsrow", [1, E0])
        onesc128 = alloc_sb(nc, st2, "l2s_onesc128", [128, 1])
        ch.op("vector", lambda v: v.memset(onesc128[:], 1.0))
        ch.op("vector", lambda v: v.memset(rbase[:], 0.0))
        ch.op("vector", lambda v: v.tensor_scalar(
            out=invf[:], in0=rowid[:], scalar1=float(NSLOT), scalar2=None,
            op0=ALU.add))

        def _sc_tile(ti):
            b = ti % 2
            mi8b, mifb, lnbb = mi8[b], mif[b], lnb[b]
            amb, pspb, posfb = amat[b], pspos[b], posf[b]
            r0 = ti * 128
            ch.dma("gpsimd", mi8b[:], idx6_dram[r0:r0 + 128, :])
            ch.dma("scalar", lnbb[:], ln1_dram[r0:r0 + 128, :])
            ch.op("vector", lambda v: v.tensor_copy(out=mifb[:], in_=mi8b[:]))
            for r in range(K0):
                if r == 0:
                    ch.op("vector", lambda v, r=r: v.tensor_scalar(
                        out=amb[:], in0=iotaE[:], scalar1=mifb[:, r:r + 1], scalar2=None,
                        op0=ALU.is_equal))
                else:
                    ch.op("vector", lambda v, r=r: v.tensor_scalar(
                        out=ohr[b][:], in0=iotaE[:], scalar1=mifb[:, r:r + 1], scalar2=None,
                        op0=ALU.is_equal))
                    ch.op("vector", lambda v: v.tensor_tensor(out=amb[:], in0=amb[:],
                                                              in1=ohr[b][:], op=ALU.add))
            ch.op("tensor", lambda t: t.matmul(out=pspb[:], lhsT=stri[:], rhs=amb[:],
                                               start=True, stop=False))
            ch.op("tensor", lambda t: t.matmul(out=pspb[:], lhsT=onesv[:], rhs=rbase[:],
                                               start=False, stop=True))
            ch.op("vector", lambda v: v.tensor_copy(out=posfb[:], in_=pspb[:]))
            ch.op("tensor", lambda t: t.matmul(out=psts[:], lhsT=onesc128[:], rhs=amb[:],
                                               start=True, stop=True))
            ch.op("vector", lambda v: v.tensor_copy(out=tsrow[:], in_=psts[:]))
            ch.op("vector", lambda v: v.tensor_tensor(out=rbase[:], in0=rbase[:], in1=tsrow[:],
                                                      op=ALU.add))
            for r in range(K0):
                rb_ = r % 2
                ohrb, tmpb, posrb = ohr[rb_], tmpe[rb_], posr[rb_]
                vrb, slotfb, sselb = vr[rb_], slotf[rb_], ssel[rb_]
                ch.op("vector", lambda v, r=r: v.tensor_scalar(
                    out=ohrb[:], in0=iotaE[:], scalar1=mifb[:, r:r + 1], scalar2=None,
                    op0=ALU.is_equal))
                ch.op("vector", lambda v: v.tensor_tensor(out=tmpb[:], in0=ohrb[:], in1=posfb[:],
                                                          op=ALU.mult))
                ch.op("vector", lambda v: v.tensor_reduce(out=posrb[:], in_=tmpb[:],
                                                          axis=AX.X, op=ALU.add))
                ch.op("vector", lambda v: v.tensor_scalar(
                    out=vrb[:], in0=posrb[:], scalar1=float(CAP), scalar2=None, op0=ALU.is_lt))
                ch.op("vector", lambda v, r=r: v.tensor_scalar(
                    out=slotfb[:], in0=mifb[:, r:r + 1], scalar1=float(CAP),
                    scalar2=posrb[:, 0:1], op0=ALU.mult, op1=ALU.add))
                ch.op("vector", lambda v: v.select(out=sselb[:], mask=vrb[:], on_true=slotfb[:],
                                                   on_false=invf[:], add_drain=True))
                ch.op("vector", lambda v, r=r: v.tensor_copy(out=slotu6[b][:, r:r + 1],
                                                             in_=sselb[:]))
                kl = wlab[0]
                wlab[0] += 1
                ch._record(
                    "gpsimd",
                    lambda g, sl=slotu6[b], r=r, src=lnbb: g.indirect_dma_start(
                        out=slotbuf_dram,
                        out_offset=bass.IndirectOffsetOnAxis(ap=sl[:, r:r + 1], axis=0),
                        in_=src[:], in_offset=None),
                    "gd",
                    ch._norm([slotu6[b][:, r:r + 1], lnbb[:]]),
                    ch._norm([slotbuf_dram[kl:kl + 1, :]]))

        for ti in range(T // 128):
            _sc_tile(ti)
        cnt64 = alloc_sb(nc, st2, "l2s_cnt64", [1, E0])
        ch.op("vector", lambda v: v.tensor_copy(out=cnt64[:], in_=rbase[:]))
        ch.dma("gpsimd", counts_dram, cnt64[:])
        if cx_i32 is not None:
            ch.op("vector", lambda v: v.tensor_copy(out=cx_i32[:], in_=rbase[:]))


def emit_l0ffn_experts_v2(nc, ch, st, slotbuf_dram, w1_dram, w2_dram, ysbuf_dram):
    with ExitStack() as st2:
        GRP = 8 * CAP
        xsT = [alloc_sb(nc, st2, f"l2e_xsT{c}", [128, GRP], BF16) for c in range(8)]
        w1b = [alloc_sb(nc, st2, f"l2e_w1b{b}", [128, 8 * 512], BF16) for b in range(2)]
        w2b = [alloc_sb(nc, st2, f"l2e_w2b{b}", [128, 4 * D], BF16) for b in range(2)]
        w1s = [alloc_sb(nc, st2, f"l2e_w1s{b}", [128, 8 * 512]) for b in range(2)]
        w2s = [alloc_sb(nc, st2, f"l2e_w2s{b}", [128, 4 * D]) for b in range(2)]
        psh = [alloc_ps(nc, st2, f"l2e_psh{b}", [128, CAP]) for b in range(4)]
        hb = [alloc_sb(nc, st2, f"l2e_hb{i}", [128, CAP], BF16) for i in range(4)]
        gsc = [tuple(alloc_sb(nc, st2, f"l2e_g{n}{b}", [128, CAP])
                     for n in ("a", "b", "c", "d")) for b in range(2)]
        psd = [alloc_ps(nc, st2, f"l2e_psd{b}", [128, 512]) for b in range(4)]
        yb = [alloc_sb(nc, st2, f"l2e_yb{b}", [128, 512], BF16) for b in range(4)]

        def _expert(grp, e):
            eb_ = e % 2
            w1e, w2e = w1b[eb_], w2b[eb_]
            w1f, w2f = w1s[eb_], w2s[eb_]
            ch.dma("sync", w1f[:].rearrange("p (c f) -> p c f", c=8),
                   w1_dram[e].rearrange("(c p) f -> p c f", p=128))
            ch.dma("scalar", w2f[:].rearrange("p (c f) -> p c f", c=4),
                   w2_dram[e].rearrange("(c p) f -> p c f", p=128))
            ch.op("vector", lambda v, w1e=w1e, w1f=w1f: v.tensor_copy(out=w1e[:], in_=w1f[:]))
            ch.op("scalar", lambda s, w2e=w2e, w2f=w2f: s.copy(out=w2e[:], in_=w2f[:]))
            sg0 = e * CAP - grp * GRP
            for ic in range(4):
                ib_ = ic
                ch_psh = psh[ib_]
                for c in range(8):
                    ch.op("tensor", lambda t, c=c, ic=ic, ch_psh=ch_psh: t.matmul(
                        out=ch_psh[:],
                        lhsT=w1e[:, c * 512 + ic * 128:c * 512 + (ic + 1) * 128],
                        rhs=xsT[c][:, sg0:sg0 + CAP],
                        start=(c == 0), stop=(c == 7)))
                emit_gelu_tanh(nc, ch, gsc[ic % 2], ch_psh, hb[ic], "")
            s0 = e * CAP
            for stile in range(CAP // 128):
                for jc in range(2):
                    db_ = (stile * 2 + jc) % 4
                    for ic in range(4):
                        ch.op("tensor", lambda t, ic=ic, stile=stile, jc=jc, db_=db_: t.matmul(
                            out=psd[db_][:], lhsT=hb[ic][:, stile * 128:(stile + 1) * 128],
                            rhs=w2e[:, ic * D + jc * 512:ic * D + (jc + 1) * 512],
                            start=(ic == 0), stop=(ic == 3)))
                    ch.op("scalar", lambda s, db_=db_: s.copy(out=yb[db_][:], in_=psd[db_][:]))
                    ch.dma("gpsimd", ysbuf_dram[s0 + stile * 128:s0 + (stile + 1) * 128,
                                                jc * 512:(jc + 1) * 512], yb[db_][:])

        for grp in range(E0 * CAP // GRP):
            for c in range(8):
                ch._record(
                    "sync",
                    lambda e, c=c, grp=grp: e.dma_start_transpose(
                        xsT[c][:],
                        slotbuf_dram[grp * GRP:(grp + 1) * GRP, c * 128:(c + 1) * 128]),
                    "sd",
                    ch._norm([slotbuf_dram]),
                    ch._norm([xsT[c][:]]))
            for e in range(grp * (GRP // CAP), (grp + 1) * (GRP // CAP)):
                _expert(grp, e)


def emit_l0ffn_combine_v2(nc, ch, st, x1_dram, idx6_dram, w64_dram, counts_all_dram,
                          prevmask_dram, ysbuf_dram, yshared_dram, x2_dram, scratch):
    y_dram = scratch["yffn"]

    with ExitStack() as st2:
        zb = alloc_sb(nc, st2, "l2c_zb", [128, D], BF16)
        ch.op("vector", lambda v: v.memset(zb[:], 0.0))
        ch.dma("gpsimd", ysbuf_dram[NSLOT:NSLOT + 1, :], zb[0:1, :])

        pmsk = alloc_sb(nc, st2, "l2c_pmsk", [8, 1])
        tall = alloc_sb(nc, st2, "l2c_tall", [8, E0])
        psbase = alloc_ps(nc, st2, "l2c_psbase", [1, E0])
        baser = alloc_sb(nc, st2, "l2c_baser", [1, E0])
        onesr = alloc_sb(nc, st2, "l2c_onesr", [1, 128])
        psbb = alloc_ps(nc, st2, "l2c_psbb", [128, E0])
        baseB = alloc_sb(nc, st2, "l2c_baseB", [128, E0])
        ch.dma("gpsimd", pmsk[:], prevmask_dram)
        talli = alloc_sb(nc, st2, "l2c_talli", [8, E0], I32)
        ch.dma("gpsimd", talli[:], counts_all_dram)
        ch.op("vector", lambda v: v.tensor_copy(out=tall[:], in_=talli[:]))
        ch.op("tensor", lambda t: t.matmul(out=psbase[:], lhsT=pmsk[:], rhs=tall[:],
                                           start=True, stop=True))
        ch.op("vector", lambda v: v.tensor_copy(out=baser[:], in_=psbase[:]))
        ch.op("vector", lambda v: v.memset(onesr[:], 1.0))
        ch.op("tensor", lambda t: t.matmul(out=psbb[:], lhsT=onesr[:], rhs=baser[:],
                                           start=True, stop=True))
        ch.op("vector", lambda v: v.tensor_copy(out=baseB[:], in_=psbb[:]))

        iotaE = alloc_sb(nc, st2, "l2c_iotaE", [128, E0])
        colid = alloc_sb(nc, st2, "l2c_colid", [128, 128])
        rowid = alloc_sb(nc, st2, "l2c_rowid", [128, 1])
        stri = alloc_sb(nc, st2, "l2c_stri", [128, 128])
        onesv = alloc_sb(nc, st2, "l2c_onesv", [1, 128])
        ch.op("gpsimd", lambda g: g.iota(iotaE[:], pattern=[[1, E0]], base=0,
                                         channel_multiplier=0,
                                         allow_small_or_imprecise_dtypes=True))
        ch.op("gpsimd", lambda g: g.iota(colid[:], pattern=[[1, 128]], base=0,
                                         channel_multiplier=0,
                                         allow_small_or_imprecise_dtypes=True))
        ch.op("gpsimd", lambda g: g.iota(rowid[:], pattern=[[0, 1]], base=0,
                                         channel_multiplier=1,
                                         allow_small_or_imprecise_dtypes=True))
        ch.op("vector", lambda v: v.tensor_scalar(out=stri[:], in0=colid[:],
                                                  scalar1=rowid[:, 0:1], scalar2=None,
                                                  op0=ALU.is_gt))
        ch.op("vector", lambda v: v.memset(onesv[:], 1.0))

        mi8 = [alloc_sb(nc, st2, f"l2c_mi8{b}", [128, 8], U32) for b in range(2)]
        mif = [alloc_sb(nc, st2, f"l2c_mif{b}", [128, 8]) for b in range(2)]
        ohr = [alloc_sb(nc, st2, f"l2c_ohr{b}", [128, E0]) for b in range(2)]
        amat = [alloc_sb(nc, st2, f"l2c_amat{b}", [128, E0]) for b in range(2)]
        tmpe = [alloc_sb(nc, st2, f"l2c_tmpe{b}", [128, E0]) for b in range(2)]
        pspos = [alloc_ps(nc, st2, f"l2c_pspos{b}", [128, E0]) for b in range(2)]
        posf = [alloc_sb(nc, st2, f"l2c_posf{b}", [128, E0]) for b in range(2)]
        vmat = [alloc_sb(nc, st2, f"l2c_vmat{b}", [128, E0]) for b in range(2)]
        w64b = [alloc_sb(nc, st2, f"l2c_w64b{b}", [128, E0]) for b in range(2)]
        wrb = [alloc_sb(nc, st2, f"l2c_wrb{b}", [128, K0]) for b in range(2)]
        posr = alloc_sb(nc, st2, "l2c_posr", [128, 1])
        vr = alloc_sb(nc, st2, "l2c_vr", [128, 1], U32)
        slotf = alloc_sb(nc, st2, "l2c_slotf", [128, 1])
        gsel = alloc_sb(nc, st2, "l2c_gsel", [128, 1])
        zrow = alloc_sb(nc, st2, "l2c_zrow", [128, 1])
        gslotu = [alloc_sb(nc, st2, f"l2c_gsu{b}", [128, K0], U32) for b in range(2)]
        gbuf = [[alloc_sb(nc, st2, f"l2c_gb{b}_{r}", [128, D], BF16) for r in range(K0)]
                for b in range(2)]
        ohrg = [alloc_sb(nc, st2, f"l2c_ohrg{b}", [128, E0]) for b in range(2)]
        tmpg = [alloc_sb(nc, st2, f"l2c_tmpg{b}", [128, E0]) for b in range(2)]
        racc2 = [alloc_sb(nc, st2, f"l2c_racc2{b}", [128, D]) for b in range(2)]
        rbase = alloc_sb(nc, st2, "l2c_rbase", [1, E0])
        psts = alloc_ps(nc, st2, "l2c_psts", [1, E0])
        tsrow = alloc_sb(nc, st2, "l2c_tsrow", [1, E0])
        onesc128 = alloc_sb(nc, st2, "l2c_onesc128", [128, 1])
        racc = [alloc_sb(nc, st2, f"l2c_racc{b}", [128, D]) for b in range(2)]
        ysh = [alloc_sb(nc, st2, f"l2c_ysh{b}", [128, D], BF16) for b in range(2)]
        ycmb = [alloc_sb(nc, st2, f"l2c_ycmb{b}", [128, D]) for b in range(2)]
        ch.op("vector", lambda v: v.memset(onesc128[:], 1.0))
        ch.op("vector", lambda v: v.memset(rbase[:], 0.0))
        ch.op("vector", lambda v: v.memset(zrow[:], float(NSLOT)))

        GQ = ("gpsimd",) * K0

        def _cb_tile(ti, mi8, mif, ohr, amat, tmpe, pspos, posf, vmat, w64t,
                     wrt, gslotu, gbufs, racc, ysh, ycmb):
            bb = ti % 2
            r0 = ti * 128
            ch.dma("gpsimd", mi8[:], idx6_dram[r0:r0 + 128, :])
            ch.dma("gpsimd", w64t[:], w64_dram[r0:r0 + 128, :])
            ch.op("vector", lambda v: v.tensor_copy(out=mif[:], in_=mi8[:]))
            for r in range(K0):
                if r == 0:
                    ch.op("vector", lambda v, r=r: v.tensor_scalar(
                        out=amat[:], in0=iotaE[:], scalar1=mif[:, r:r + 1], scalar2=None,
                        op0=ALU.is_equal))
                else:
                    ch.op("vector", lambda v, r=r: v.tensor_scalar(
                        out=ohr[:], in0=iotaE[:], scalar1=mif[:, r:r + 1], scalar2=None,
                        op0=ALU.is_equal))
                    ch.op("vector", lambda v: v.tensor_tensor(out=amat[:], in0=amat[:],
                                                              in1=ohr[:], op=ALU.add))
            ch.op("tensor", lambda t: t.matmul(out=pspos[:], lhsT=stri[:], rhs=amat[:],
                                               start=True, stop=False))
            ch.op("tensor", lambda t: t.matmul(out=pspos[:], lhsT=onesv[:], rhs=rbase[:],
                                               start=False, stop=True))
            ch.op("vector", lambda v: v.tensor_copy(out=posf[:], in_=pspos[:]))
            ch.op("tensor", lambda t: t.matmul(out=psts[:], lhsT=onesc128[:], rhs=amat[:],
                                               start=True, stop=True))
            ch.op("vector", lambda v: v.tensor_copy(out=tsrow[:], in_=psts[:]))
            ch.op("vector", lambda v: v.tensor_tensor(out=rbase[:], in0=rbase[:], in1=tsrow[:],
                                                      op=ALU.add))
            ch.op("vector", lambda v: v.tensor_tensor(out=vmat[:], in0=posf[:], in1=baseB[:],
                                                      op=ALU.add))
            ch.op("vector", lambda v: v.tensor_scalar(out=vmat[:], in0=vmat[:],
                                                      scalar1=float(CAP), scalar2=None,
                                                      op0=ALU.is_lt))
            for r in range(K0):
                ch.op("vector", lambda v, r=r: v.tensor_scalar(
                    out=ohr[:], in0=iotaE[:], scalar1=mif[:, r:r + 1], scalar2=None,
                    op0=ALU.is_equal))
                ch.op("vector", lambda v: v.tensor_tensor(out=tmpe[:], in0=ohr[:], in1=posf[:],
                                                          op=ALU.mult))
                ch.op("vector", lambda v: v.tensor_reduce(out=posr[:], in_=tmpe[:],
                                                          axis=AX.X, op=ALU.add))
                ch.op("vector", lambda v: v.tensor_tensor(out=tmpe[:], in0=ohr[:], in1=vmat[:],
                                                          op=ALU.mult))
                ch.op("vector", lambda v: v.tensor_reduce(out=slotf[:], in_=tmpe[:],
                                                          axis=AX.X, op=ALU.add))
                ch.op("vector", lambda v: v.tensor_copy(out=vr[:], in_=slotf[:]))
                ch.op("vector", lambda v, r=r: v.tensor_scalar(
                    out=slotf[:], in0=mif[:, r:r + 1], scalar1=float(CAP),
                    scalar2=posr[:, 0:1], op0=ALU.mult, op1=ALU.add))
                ch.op("vector", lambda v: v.select(out=gsel[:], mask=vr[:], on_true=slotf[:],
                                                   on_false=zrow[:], add_drain=True))
                ch.op("vector", lambda v, r=r: v.tensor_copy(out=gslotu[:, r:r + 1],
                                                             in_=gsel[:]))
                ch.op("vector", lambda v: v.tensor_tensor(out=tmpe[:], in0=ohr[:], in1=w64t[:],
                                                          op=ALU.mult))
                ch.op("vector", lambda v, r=r: v.tensor_reduce(out=wrt[:, r:r + 1], in_=tmpe[:],
                                                               axis=AX.X, op=ALU.add))
            for r in range(K0):
                ch.dma_fn(GQ[r], lambda g, r=r: g.indirect_dma_start(
                    out=gbufs[r][:], out_offset=None, in_=ysbuf_dram,
                    in_offset=bass.IndirectOffsetOnAxis(ap=gslotu[:, r:r + 1], axis=0)))
            ch.dma("scalar", ysh[:], yshared_dram[r0:r0 + 128, :])
            ch.op("vector", lambda v: v.tensor_scalar(
                out=racc[:], in0=gbufs[0][:], scalar1=wrt[:, 0:1], scalar2=None,
                op0=ALU.mult))
            for r in range(1, K0):
                ch.op("vector", lambda v, r=r: v.scalar_tensor_tensor(
                    out=racc[:], in0=gbufs[r][:], scalar=wrt[:, r:r + 1],
                    in1=racc[:], op0=ALU.mult, op1=ALU.add))
            ch.op("vector", lambda v: v.tensor_tensor(out=ycmb[:], in0=racc[:], in1=ysh[:],
                                                      op=ALU.add))
            ch.dma("sync", y_dram[r0:r0 + 128, :], ycmb[:])

        for ti in range(T // 128):
            b = ti % 2
            _cb_tile(ti, mi8[b], mif[b], ohr[b], amat[b], tmpe[b], pspos[b],
                     posf[b], vmat[b], w64b[b], wrb[b], gslotu[b], gbuf[b],
                     racc[b], ysh[b], ycmb[b])

    with ExitStack() as st3:
        emit_ln_tok(nc, ch, st3, "l2c_ln2", y_dram, None, res_add_dram=x1_dram,
                    x_out_dram=x2_dram)


def build_launch_A():
    nc = bass.Bass("TRN2", target_bir_lowering=False)
    x_in = _mk(nc, "x", [T, D], F32, "ExternalInput")
    w_attn = _mk(nc, "w_attn", [D, 3 * D], F32, "ExternalInput")
    w_proj = _mk(nc, "w_proj", [D, D], F32, "ExternalInput")
    rw = _mk(nc, "rw", [E0, D], F32, "ExternalInput")
    eb = _mk(nc, "eb", [E0], F32, "ExternalInput")
    w1 = _mk(nc, "w1", [E0, D, 512], F32, "ExternalInput")
    w2 = _mk(nc, "w2", [E0, 512, D], F32, "ExternalInput")
    sw1 = _mk(nc, "sw1", [2048, D], F32, "ExternalInput")
    sw2 = _mk(nc, "sw2", [D, 2048], F32, "ExternalInput")
    x1 = _mk(nc, "x1", [T, D], F32, "ExternalOutput")
    idx6 = _mk(nc, "idx6", [T, 8], U32, "ExternalOutput")
    counts = _mk(nc, "counts", [1, E0], F32, "ExternalOutput")
    ys = _mk(nc, "ys", [NSLOT, D], BF16, "ExternalOutput")
    yshared = _mk(nc, "yshared", [T, D], BF16, "ExternalOutput")

    scratch = {
        "ln": _mk(nc, "s_ln", [T, D], BF16),
        "qkT": _mk(nc, "s_qkT", [2 * D, T], BF16),
        "v": _mk(nc, "s_v", [T // 128, 16, 128, 64], BF16),
        "oT": _mk(nc, "s_oT", [D, T], BF16),
        "y": _mk(nc, "s_y", [T, D], F32),
        "nb": _mk(nc, "s_nb", [2, 2, 8], F32),
        "rwb": _mk(nc, "s_rwb", [E0, D], BF16),
        "xs": _mk(nc, "s_xs", [NSLOT, D], BF16),
        "sw1b": _mk(nc, "s_sw1b", [2048, D], BF16),
        "sw2b": _mk(nc, "s_sw2b", [D, 2048], BF16),
        "hsT": _mk(nc, "s_hsT", [2048, T], BF16),
    }
    ln1 = _mk(nc, "s_ln1", [T, D], BF16)
    w64 = _mk(nc, "s_w64", [T, E0], F32)
    totals_junk = _mk(nc, "s_totjunk", [E0, 1], F32)
    slotbuf = _mk(nc, "s_slotbuf", [NSLOT + 128, ROWW], BF16)

    ch = Chain(nc)
    with ExitStack() as st:
        ch.make_sems(st)
        block = st.enter_context(nc.Block())
        emit_l0_attn(nc, ch, st, x_in, w_attn, w_proj, x1, scratch)
        emit_l0ffn_routing(nc, ch, st, x1, rw, eb, ln1, idx6, w64, totals_junk, scratch)
        zt = alloc_sb(nc, st, "lA_zt", [128, ROWW], BF16)
        ch.op("vector", lambda v: v.memset(zt[:], 0.0))
        nrep = NSLOT // 128 + 1
        half = (nrep + 1) // 2
        for hh in range(2):
            lo, hi = hh * half, min((hh + 1) * half, nrep)
            ch.dma("gpsimd", slotbuf[lo * 128:hi * 128, :].rearrange("(n p) w -> p n w", p=128),
                   zt[:].unsqueeze(1).broadcast_to((128, hi - lo, ROWW)))
        emit_l0ffn_scatter_local(nc, ch, st, ln1, idx6, w64, slotbuf, counts)
        emit_l0ffn_experts_local(nc, ch, st, slotbuf, w1, w2, ys, scratch)
        emit_l0ffn_shared(nc, ch, st, ln1, sw1, sw2, yshared, scratch)
        ch.emit(block)
    return nc


def build_launch_B():
    nc = bass.Bass("TRN2", target_bir_lowering=False)
    x1 = _mk(nc, "x1", [T, D], F32, "ExternalInput")
    idx6 = _mk(nc, "idx6", [T, 8], U32, "ExternalInput")
    counts_all = _mk(nc, "counts_all", [NCORES, E0], F32, "ExternalInput")
    prevmask = _mk(nc, "prevmask", [NCORES, 1], F32, "ExternalInput")
    ysall = _mk(nc, "ysall", [NSLOT, D], BF16, "ExternalInput")
    yshared = _mk(nc, "yshared", [T, D], BF16, "ExternalInput")
    wqkv = _mk(nc, "wqkv", [D, 3 * D], F32, "ExternalInput")
    bqkv = _mk(nc, "bqkv", [3 * D], F32, "ExternalInput")
    wout = _mk(nc, "wout", [D, D], F32, "ExternalInput")
    bout = _mk(nc, "bout", [D], F32, "ExternalInput")
    rw1 = _mk(nc, "rw1", [D, E1], F32, "ExternalInput")
    rb1 = _mk(nc, "rb1", [1, 1], F32, "ExternalInput")
    gw = _mk(nc, "gw", [E1, D, 512], F32, "ExternalInput")
    uw = _mk(nc, "uw", [E1, D, 512], F32, "ExternalInput")
    dw = _mk(nc, "dw", [E1, 512, D], F32, "ExternalInput")
    sgw = _mk(nc, "sgw", [D, 2048], F32, "ExternalInput")
    suw = _mk(nc, "suw", [D, 2048], F32, "ExternalInput")
    sdw = _mk(nc, "sdw", [2048, D], F32, "ExternalInput")
    xout = _mk(nc, "xout", [T, D], F32, "ExternalOutput")

    x2 = _mk(nc, "s_x2", [T, D], F32)
    x3 = _mk(nc, "s_x3", [T, D], F32)
    scratch = {
        "ysbuf": _mk(nc, "s_ysbuf", [NSLOT + 1, D], BF16),
        "yffn": _mk(nc, "s_yffn", [T, D], F32),
        "l1ln": _mk(nc, "s_l1ln", [T, D], BF16),
        "pqT": _mk(nc, "s_pqT", [D, T], BF16),
        "pkT": _mk(nc, "s_pkT", [D, T], BF16),
        "ktok": _mk(nc, "s_ktok", [T, D], BF16),
        "l1v": _mk(nc, "s_l1v", [T // 128, 16, 128, 64], BF16),
        "l1oT": _mk(nc, "s_l1oT", [D, T], BF16),
        "l1y": _mk(nc, "s_l1y", [T, D], F32),
        "l2ln": _mk(nc, "s_l2ln", [T, D], BF16),
        "actT": _mk(nc, "s_actT", [E1, 512, T], BF16),
        "hsT2": _mk(nc, "s_hsT2", [2048, T], BF16),
        "l2y": _mk(nc, "s_l2y", [T, D], F32),
    }
    ch = Chain(nc)
    with ExitStack() as st:
        ch.make_sems(st)
        block = st.enter_context(nc.Block())
        emit_l0ffn_combine_local(nc, ch, st, x1, idx6, counts_all, prevmask,
                                 ysall, yshared, x2, scratch)
        emit_l1_attn(nc, ch, st, x2, wqkv, bqkv, wout, bout, x3, scratch)
        emit_l1_ffn(nc, ch, st, x3, rw1, rb1, gw, uw, dw, sgw, suw, sdw, xout, scratch)
        ch.emit(block)
    return nc


def kernel_2launch(**inputs):
    cores = list(range(NCORES))
    x = np.ascontiguousarray(inputs["L0_attn_x"])
    ncA = _get("launchA", build_launch_A)
    rA = run_bass_kernel_spmd(ncA, [
        {"x": x[i], "w_attn": inputs["L0_attn_w_attn"], "w_proj": inputs["L0_attn_w_proj"],
         "rw": inputs["L0_ffn_router_weight"], "eb": inputs["L0_ffn_expert_bias"],
         "w1": inputs["L0_ffn_w1"], "w2": inputs["L0_ffn_w2"],
         "sw1": inputs["L0_ffn_sw1"], "sw2": inputs["L0_ffn_sw2"]} for i in cores],
        core_ids=cores).results
    counts_all = np.concatenate([r["counts"] for r in rA], 0)

    ncB = _get("launchB", build_launch_B)
    insB = []
    for i in cores:
        pm = np.zeros((NCORES, 1), np.float32)
        pm[:i] = 1.0
        insB.append({
            "x1": rA[i]["x1"], "idx6": rA[i]["idx6"], "counts_all": counts_all,
            "prevmask": pm, "ysall": rA[i]["ys"], "yshared": rA[i]["yshared"],
            "wqkv": inputs["L1_attn_W_qkv"], "bqkv": inputs["L1_attn_b_qkv"],
            "wout": inputs["L1_attn_W_out"], "bout": inputs["L1_attn_b_out"],
            "rw1": inputs["L1_ffn_router_w"],
            "rb1": np.asarray(inputs["L1_ffn_router_bias"]).reshape(1, 1),
            "gw": inputs["L1_ffn_gate_w"], "uw": inputs["L1_ffn_up_w"],
            "dw": inputs["L1_ffn_down_w"], "sgw": inputs["L1_ffn_s_gate_w"],
            "suw": inputs["L1_ffn_s_up_w"], "sdw": inputs["L1_ffn_s_down_w"]})
    rB = run_bass_kernel_spmd(ncB, insB, core_ids=cores).results
    return np.stack([r["xout"] for r in rB], 0)


TPB_BASES = [0x00002000000000, 0x00003000000000, 0x00006000000000, 0x00007000000000,
             0x00802000000000, 0x00803000000000, 0x00806000000000, 0x00807000000000]
I32 = mybir.dt.int32


def emit_exchange_bcast(nc, g, cx, crecv, flags, seedt, met):
    crecv_off = nc.lookup_mloc(crecv).addr
    flags_off = nc.lookup_mloc(flags).addr
    me_v = g.value_load(met[0:1, 0:1])
    seed_r = g.alloc_register(name="xseed")
    g.reg_load(seed_r, seedt[0:1, 0:1])
    val_r = g.alloc_register(name="xval")
    addr = g.alloc_register64(name="xaddr")
    for slot in g.Switch(me_v, 8):
        for e in range(E0):
            g.reg_load(val_r, cx[0:1, e:e + 1])
            for dest in range(8):
                g.reg_mov64(addr, TPB_BASES[dest] + crecv_off + (slot * E0 + e) * 4)
                g.store(addr, val_r)
        for dest in range(8):
            g.reg_mov64(addr, TPB_BASES[dest] + flags_off + slot * 4)
            g.store(addr, seed_r)


def emit_exchange_poll(nc, g, flags, seedt, met):
    seed_r = g.alloc_register(name="pseed")
    g.reg_load(seed_r, seedt[0:1, 0:1])
    me_r = g.alloc_register(name="pme")
    g.reg_load(me_r, met[0:1, 0:1])
    gates = []
    for j in range(8):
        r = g.alloc_register(name=f"pg{j}")
        g.reg_alu(r, me_r, j, ALU.is_gt)
        gates.append(r)
    facc = g.alloc_register(name="xfacc")
    ftmp = g.alloc_register(name="xftmp")
    cond = g.alloc_register(name="xcond")
    g.reg_mov(cond, 1)
    wh = g.While(cond)
    with wh:
        g.reg_mov(facc, 0)
        for p in range(8):
            g.reg_load(ftmp, flags[0:1, p:p + 1])
            g.reg_alu(ftmp, ftmp, seed_r, ALU.is_equal)
            g.reg_alu(ftmp, ftmp, gates[p], ALU.mult)
            g.reg_alu(facc, facc, ftmp, ALU.add)
        g.reg_alu(cond, facc, me_r, ALU.is_lt)


def build_launch_FUSED():
    nc = bass.Bass("TRN2", target_bir_lowering=False)
    x_in = _mk(nc, "x", [T, D], F32, "ExternalInput")
    w_attn = _mk(nc, "w_attn", [D, 3 * D], F32, "ExternalInput")
    w_proj = _mk(nc, "w_proj", [D, D], F32, "ExternalInput")
    rw = _mk(nc, "rw", [E0, D], F32, "ExternalInput")
    eb = _mk(nc, "eb", [E0], F32, "ExternalInput")
    w1 = _mk(nc, "w1", [E0, D, 512], F32, "ExternalInput")
    w2 = _mk(nc, "w2", [E0, 512, D], F32, "ExternalInput")
    sw1 = _mk(nc, "sw1", [2048, D], F32, "ExternalInput")
    sw2 = _mk(nc, "sw2", [D, 2048], F32, "ExternalInput")
    wqkv = _mk(nc, "wqkv", [D, 3 * D], F32, "ExternalInput")
    bqkv = _mk(nc, "bqkv", [3 * D], F32, "ExternalInput")
    wout = _mk(nc, "wout", [D, D], F32, "ExternalInput")
    bout = _mk(nc, "bout", [D], F32, "ExternalInput")
    rw1 = _mk(nc, "rw1", [D, E1], F32, "ExternalInput")
    rb1 = _mk(nc, "rb1", [1, 1], F32, "ExternalInput")
    gw = _mk(nc, "gw", [E1, D, 512], F32, "ExternalInput")
    uw = _mk(nc, "uw", [E1, D, 512], F32, "ExternalInput")
    dw = _mk(nc, "dw", [E1, 512, D], F32, "ExternalInput")
    sgw = _mk(nc, "sgw", [D, 2048], F32, "ExternalInput")
    suw = _mk(nc, "suw", [D, 2048], F32, "ExternalInput")
    sdw = _mk(nc, "sdw", [2048, D], F32, "ExternalInput")
    me_in = _mk(nc, "me", [1, 1], I32, "ExternalInput")
    seed_in = _mk(nc, "seed", [1, 1], I32, "ExternalInput")
    prevmask = _mk(nc, "prevmask", [NCORES, 1], F32, "ExternalInput")
    xout = _mk(nc, "xout", [T, D], F32, "ExternalOutput")

    x1 = _mk(nc, "s_x1", [T, D], F32)
    x2 = _mk(nc, "s_x2", [T, D], F32)
    x3 = _mk(nc, "s_x3", [T, D], F32)
    ln1 = _mk(nc, "s_ln1", [T, D], BF16)
    idx6 = _mk(nc, "s_idx6", [T, 8], U32)
    w64 = _mk(nc, "s_w64", [T, E0], F32)
    totals_junk = _mk(nc, "s_totjunk", [E0, 1], F32)
    slotbuf = _mk(nc, "s_slotbuf", [NSLOT + 128, D], BF16)
    yshared = _mk(nc, "s_yshared", [T, D], BF16)
    counts_junk = _mk(nc, "s_cntjunk", [1, E0], F32)
    counts_dram = _mk(nc, "s_cnt8", [NCORES, E0], I32)
    scratch = {
        "ln": _mk(nc, "s_ln", [T, D], BF16),
        "qkT": _mk(nc, "s_qkT", [2 * D, T], BF16),
        "v": _mk(nc, "s_v", [T // 128, 16, 128, 64], BF16),
        "oT": _mk(nc, "s_oT", [D, T], BF16),
        "y": _mk(nc, "s_y", [T, D], F32),
        "nb": _mk(nc, "s_nb", [2, 2, 8], F32),
        "rwb": _mk(nc, "s_rwb", [E0, D], BF16),
        "xs": _mk(nc, "s_xs", [NSLOT, D], BF16),
        "sw1b": _mk(nc, "s_sw1b", [2048, D], BF16),
        "sw2b": _mk(nc, "s_sw2b", [D, 2048], BF16),
        "hsT": _mk(nc, "s_hsT", [2048, T], BF16),
        "ysbuf": _mk(nc, "s_ysbuf", [NSLOT + 1, D], BF16),
        "yffn": _mk(nc, "s_yffn", [T, D], F32),
        "l1ln": _mk(nc, "s_l1ln", [T, D], BF16),
        "pqT": _mk(nc, "s_pqT", [D, T], BF16),
        "pkT": _mk(nc, "s_pkT", [D, T], BF16),
        "ktok": _mk(nc, "s_ktok", [T, D], BF16),
        "l1v": _mk(nc, "s_l1v", [T // 128, 16, 128, 64], BF16),
        "l1oT": _mk(nc, "s_l1oT", [D, T], BF16),
        "l1y": _mk(nc, "s_l1y", [T, D], F32),
        "l2ln": _mk(nc, "s_l2ln", [T, D], BF16),
        "actT": _mk(nc, "s_actT", [E1, 512, T], BF16),
        "hsT2": _mk(nc, "s_hsT2", [2048, T], BF16),
        "l2y": _mk(nc, "s_l2y", [T, D], F32),
    }

    ch = Chain(nc)
    with ExitStack() as st:
        crecv = alloc_sb(nc, st, "xc_crecv", [1, 8 * E0], I32)
        flags = alloc_sb(nc, st, "xc_flags", [1, 8], I32)
        met = alloc_sb(nc, st, "xc_met", [1, 1], I32)
        seedt = alloc_sb(nc, st, "xc_seedt", [1, 1], I32)
        cx = alloc_sb(nc, st, "xc_cx", [1, E0], I32)
        ch.make_sems(st)
        block = st.enter_context(nc.Block())

        with ch.group():
            ch.dma("gpsimd", met[:], me_in)
            ch.dma("gpsimd", seedt[:], seed_in)

        emit_l0_attn(nc, ch, st, x_in, w_attn, w_proj, x1, scratch)
        emit_l0ffn_routing(nc, ch, st, x1, rw, eb, ln1, idx6, w64, totals_junk, scratch)
        wlab = [0]
        emit_l0ffn_scatter_v2(nc, ch, st, ln1, idx6, slotbuf, counts_junk,
                              cx_i32=cx, wlab=wlab)
        ch.op("gpsimd", lambda g: emit_exchange_bcast(nc, g, cx, crecv, flags, seedt, met),
              r=[cx[:], met[:], seedt[:]],
              w=[flags[:]])

        emit_l0ffn_experts_v2(nc, ch, st, slotbuf, w1, w2, scratch["ysbuf"])
        emit_l0ffn_shared(nc, ch, st, ln1, sw1, sw2, yshared, scratch)

        ch.op("gpsimd", lambda g: emit_exchange_poll(nc, g, flags, seedt, met),
              r=[flags[:], seedt[:], met[:]],
              w=[crecv[:]])
        ch.dma("gpsimd", counts_dram.rearrange("a b -> (a b)").rearrange("(o x) -> o x", o=1),
               crecv[:])
        emit_l0ffn_combine_v2(nc, ch, st, x1, idx6, w64, counts_dram, prevmask,
                              scratch["ysbuf"], yshared, x2, scratch)
        emit_l1_attn(nc, ch, st, x2, wqkv, bqkv, wout, bout, x3, scratch)
        emit_l1_ffn(nc, ch, st, x3, rw1, rb1, gw, uw, dw, sgw, suw, sdw, xout, scratch)
        ch.emit(block)
    return nc


_SEED_COUNTER = [0]


def kernel(**inputs):
    cores = list(range(NCORES))
    x = np.ascontiguousarray(inputs["L0_attn_x"])
    nc = _get("fused", build_launch_FUSED)
    import time
    _SEED_COUNTER[0] += 1
    seed = np.array([[(int(time.time_ns() // 1000) + _SEED_COUNTER[0] * 7919) % 2000000000 + 1]],
                    np.int32)
    ins = []
    for i in cores:
        pm = np.zeros((NCORES, 1), np.float32)
        pm[:i] = 1.0
        ins.append({
            "x": x[i], "w_attn": inputs["L0_attn_w_attn"], "w_proj": inputs["L0_attn_w_proj"],
            "rw": inputs["L0_ffn_router_weight"], "eb": inputs["L0_ffn_expert_bias"],
            "w1": inputs["L0_ffn_w1"], "w2": inputs["L0_ffn_w2"],
            "sw1": inputs["L0_ffn_sw1"], "sw2": inputs["L0_ffn_sw2"],
            "wqkv": inputs["L1_attn_W_qkv"], "bqkv": inputs["L1_attn_b_qkv"],
            "wout": inputs["L1_attn_W_out"], "bout": inputs["L1_attn_b_out"],
            "rw1": inputs["L1_ffn_router_w"],
            "rb1": np.asarray(inputs["L1_ffn_router_bias"]).reshape(1, 1),
            "gw": inputs["L1_ffn_gate_w"], "uw": inputs["L1_ffn_up_w"],
            "dw": inputs["L1_ffn_down_w"], "sgw": inputs["L1_ffn_s_gate_w"],
            "suw": inputs["L1_ffn_s_up_w"], "sdw": inputs["L1_ffn_s_down_w"],
            "me": np.array([[i]], np.int32), "seed": seed, "prevmask": pm})
    res = run_bass_kernel_spmd(nc, ins, core_ids=cores)
    return np.stack([r["xout"] for r in res.results], 0)
